# version 03
# baseline (speedup 1.0000x reference)
import contextlib

import numpy as np
import ml_dtypes
import concourse.bass as bass
import concourse.mybir as mybir
from concourse.bass_utils import run_bass_kernel_spmd
from bass_rust import IndirectOffsetOnAxis as IOA

F32 = mybir.dt.float32
BF16 = mybir.dt.bfloat16
U32 = mybir.dt.uint32
AF = mybir.ActivationFunctionType
ALU = mybir.AluOpType
AX = mybir.AxisListType

NCORE = 8
B, T, D = 2, 8192, 1024
TC = 2048
ST = 512
NTILE = TC // ST
G = 8
E, CAP, EH, SHD = 8, 5120, 2048, 4096
NTOK = B * T
EPS = 1e-5
RG = [list(range(NCORE))]
SENT_ROW = float(E * CAP)

ENGINES = ("sync", "gpsimd", "vector", "scalar", "tensor")


class Sem:
    __slots__ = ("h", "n")

    def __init__(self, h):
        self.h = h
        self.n = 0


class EngineProxy:
    def __init__(self, ctx, name):
        self._ctx = ctx
        self._name = name

    def __getattr__(self, op):
        ctx, name = self._ctx, self._name

        def f(*a, **kw):
            if ctx.cur == name:
                return getattr(ctx.eng[name], op)(*a, **kw)
            return None

        return f


class Ctx:
    def __init__(self, nc):
        self.nc = nc
        self.eng = {
            "sync": nc.sync, "gpsimd": nc.gpsimd, "vector": nc.vector,
            "scalar": nc.scalar, "tensor": nc.tensor,
        }
        self.cur = None
        self.s = EngineProxy(self, "sync")
        self.g = EngineProxy(self, "gpsimd")
        self.v = EngineProxy(self, "vector")
        self.a = EngineProxy(self, "scalar")
        self.t = EngineProxy(self, "tensor")
        self.sems = []

    def sem(self, handle):
        s = Sem(handle)
        self.sems.append(s)
        return s

    def reset(self):
        for s in self.sems:
            s.n = 0

    @staticmethod
    def mark(instr, sem, k=1):
        sem.n += k
        if instr is not None:
            instr.then_inc(sem.h, k)
        return sem.n

    def wait(self, eng_name, sem, val=None):
        if self.cur == eng_name:
            self.eng[eng_name].wait_ge(sem.h, sem.n if val is None else val)


def _chan2d(v):
    return np.ascontiguousarray(np.asarray(v).reshape(-1, 128).T)


def _prep_inputs(inp):
    x = np.asarray(inp["L0_attn_input_bxtxd"])
    maps = []
    shared = {
        "w_x": np.ascontiguousarray(inp["L0_attn_w_x"]),
        "gi_w": np.ascontiguousarray(inp["L0_attn_gi_w"]),
        "ga_w": np.ascontiguousarray(inp["L0_attn_ga_w"]),
        "w_y": np.ascontiguousarray(inp["L0_attn_w_y"]),
        "bx2d": _chan2d(inp["L0_attn_b_x"]),
        "by2d": _chan2d(inp["L0_attn_b_y"]),
        "cb2d": _chan2d(inp["L0_attn_conv_b"]),
        "gib2d": _chan2d(inp["L0_attn_gi_b"]),
        "gab2d": _chan2d(inp["L0_attn_ga_b"]),
        "ap2d": _chan2d(inp["L0_attn_a_param"]),
        "cw2d": np.ascontiguousarray(
            np.asarray(inp["L0_attn_conv_w"]).reshape(4, G, 128)
            .transpose(2, 1, 0).reshape(128, G * 4)),
        "projT": np.ascontiguousarray(np.asarray(inp["L0_ffn_proj_w"]).T),
        "projb_rep": np.ascontiguousarray(
            np.tile(np.asarray(inp["L0_ffn_proj_b"])[None, :], (128, 1))),
        "ebias_rep": np.ascontiguousarray(
            np.tile(np.asarray(inp["L0_ffn_expert_bias"])[None, :], (128, 1))),
        "sh1T": np.ascontiguousarray(np.asarray(inp["L0_ffn_sh1_w"]).T),
        "sh1b2d": _chan2d(inp["L0_ffn_sh1_b"]),
        "sh2T": np.ascontiguousarray(np.asarray(inp["L0_ffn_sh2_w"]).T),
        "sh2b_rep": np.ascontiguousarray(
            np.tile(np.asarray(inp["L0_ffn_sh2_b"])[None, :], (128, 1))),
        "idn": np.eye(128, dtype=np.float32),
        "idnb": np.eye(128, dtype=ml_dtypes.bfloat16),
        "iota8": np.tile(np.arange(8, dtype=np.float32)[None, :], (128, 1)),
        "iota8p": np.tile(np.arange(8, dtype=np.float32)[None, :] + 1000.0, (128, 1)),
        "iota_ref": (np.arange(128)[:, None] * 256
                     + np.concatenate([2 * np.arange(128), 2 * np.arange(128) + 1])[None, :]
                     ).astype(np.float32),
        "iota_j": np.tile(np.arange(256, dtype=np.float32)[None, :], (128, 1)),
        "iota_slot": (np.arange(40)[None, :] * 128
                      + np.arange(128)[:, None]).astype(np.float32),
    }
    fc1 = np.asarray(inp["L0_ffn_fc1_w"])
    fc1b = np.asarray(inp["L0_ffn_fc1_b"])
    fc2 = np.asarray(inp["L0_ffn_fc2_w"])
    fc2b = np.asarray(inp["L0_ffn_fc2_b"])

    for c in range(NCORE):
        b, k = c // 4, c % 4
        t0 = k * TC
        xs = np.zeros((TC + 3, D), np.float32)
        lo = max(t0 - 3, 0)
        xs[3 - (t0 - lo):] = x[b, lo:t0 + TC]
        hv = 0.0 if k == 0 else 1.0
        msk = np.zeros((128, 3), np.float32)
        msk[:, :k] = 1.0
        consts = np.concatenate([
            np.full((128, 1), hv, np.float32),
            np.full((128, 1), float(b), np.float32),
            msk,
            1.0 - msk,
            np.full((128, 1), float(c), np.float32),
            np.full((128, 1), float(c * CAP), np.float32),
            np.arange(128, dtype=np.float32).reshape(128, 1),
            np.full((128, 1), 1.0 - hv, np.float32),
        ], axis=1).astype(np.float32)
        m = dict(shared)
        m.update({
            "xT": np.ascontiguousarray(xs.T),
            "x_tok": np.ascontiguousarray(x[b, t0:t0 + TC]),
            "fc1T": np.ascontiguousarray(fc1[c].T),
            "fc1b2d": _chan2d(fc1b[c]),
            "fc2T": np.ascontiguousarray(fc2[c].T),
            "fc2b_rep": np.ascontiguousarray(np.tile(fc2b[c][None, :], (128, 1))),
            "consts": consts,
        })
        maps.append(m)
    return maps


def _build(debug_stage=None):
    nc = bass.Bass("TRN2", target_bir_lowering=False, debug=False)
    C = Ctx(nc)

    dd = {}

    def din(name, shape, dt=F32):
        dd[name] = nc.dram_tensor(name, shape, dt, kind="ExternalInput").ap()

    din("xT", [D, TC + 3]); din("x_tok", [TC, D])
    for n in ("w_x", "gi_w", "ga_w", "w_y"):
        din(n, [D, D])
    for n in ("bx2d", "by2d", "cb2d", "gib2d", "gab2d", "ap2d"):
        din(n, [128, G])
    din("cw2d", [128, G * 4])
    din("projT", [D, E]); din("projb_rep", [128, E]); din("ebias_rep", [128, E])
    din("sh1T", [D, SHD]); din("sh1b2d", [128, SHD // 128])
    din("sh2T", [SHD, D]); din("sh2b_rep", [128, D])
    din("fc1T", [D, EH]); din("fc1b2d", [128, EH // 128])
    din("fc2T", [EH, D]); din("fc2b_rep", [128, D])
    din("idn", [128, 128])
    din("idnb", [128, 128], BF16)
    din("iota8", [128, E]); din("iota8p", [128, E])
    din("iota_ref", [128, 256]); din("iota_j", [128, 256]); din("iota_slot", [128, 40])
    din("consts", [128, 12])

    out = nc.dram_tensor("out", [TC, D], F32, kind="ExternalOutput").ap()
    if debug_stage == "fulldbg":
        dbg_xe = nc.dram_tensor("dbg_xe", [128, 6144], BF16, kind="ExternalOutput").ap()
        dbg_he = nc.dram_tensor("dbg_he", [128, 8192], BF16, kind="ExternalOutput").ap()
        dbg_xall = nc.dram_tensor("dbg_xall", [NTOK, D], BF16, kind="ExternalOutput").ap()
        dbg_tok = nc.dram_tensor("dbg_tok", [128, 40], F32, kind="ExternalOutput").ap()
        dbg_pair = nc.dram_tensor("dbg_pair", [4 * NCORE, TC], F32, kind="ExternalOutput").ap()
        dbg_sel2 = nc.dram_tensor("dbg_sel2", [128, 2200], F32, kind="ExternalOutput").ap()
        dbg_slot = nc.dram_tensor("dbg_slot", [NCORE, 128 * 256], F32, kind="ExternalOutput").ap()
        dbg_oe = nc.dram_tensor("dbg_oe", [CAP, D], F32, kind="ExternalOutput").ap()
        dbg_xm = nc.dram_tensor("dbg_xm", [TC, D], BF16, kind="ExternalOutput").ap()
        dbg_hsh = nc.dram_tensor("dbg_hsh", [SHD, 128], BF16, kind="ExternalOutput").ap()

    h_sp = nc.dram_tensor("h_sp", [G, 128, TC], F32).ap()
    p_sp = nc.dram_tensor("p_sp", [G, 128, TC], F32).ap()
    cin_carry = nc.dram_tensor("cin_carry", [2, D], F32).ap()
    cout_carry = nc.dram_tensor("cout_carry", [2 * NCORE, D], F32).ap()
    cin_xm = nc.dram_tensor("cin_xm", [TC, D], BF16).ap()
    cout_xm = nc.dram_tensor("cout_xm", [NTOK, D], BF16).ap()
    cin_pair = nc.dram_tensor("cin_pair", [4, TC], F32).ap()
    cout_pair = nc.dram_tensor("cout_pair", [4 * NCORE, TC], F32).ap()
    cnt_dram = nc.dram_tensor("cnt_dram", [128], F32).ap()
    base_dram = nc.dram_tensor("base_dram", [130], F32).ap()
    padded_dram = nc.dram_tensor("padded_dram", [128 * 256], F32).ap()
    cin_slot = nc.dram_tensor("cin_slot", [1, 128 * 256], F32).ap()
    cout_slot = nc.dram_tensor("cout_slot", [NCORE, 128 * 256], F32).ap()
    cin_oe = nc.dram_tensor("cin_oe", [CAP, D], F32).ap()
    cout_oe = nc.dram_tensor("cout_oe", [NCORE * CAP, D], F32).ap()
    hsh_sp = nc.dram_tensor("hsh_sp", [SHD, TC], BF16).ap()

    ctx = contextlib.ExitStack()

    def sbuf(name, shape, dt=F32):
        return ctx.enter_context(nc.sbuf_tensor(name, shape, dt))

    def psum(name, shape, dt=F32):
        return ctx.enter_context(nc.psum_tensor(name, shape, dt))

    AR_W = sbuf("AR_W", [128, 32768], BF16)
    AR_A = sbuf("AR_A", [128, 16384], F32)
    AR_X = sbuf("AR_X", [128, 16384], BF16)
    AR_B = sbuf("AR_B", [128, 4120], F32)

    def vw(ar, lo, n, g=None):
        a = ar[:, lo:lo + n]
        if g is not None:
            a = a.rearrange("p (g n) -> p g n", g=g)
        return a

    wx_sb = vw(AR_W, 0, 8192, G)
    wgi_sb = vw(AR_W, 8192, 8192, G)
    wga_sb = vw(AR_W, 16384, 8192, G)
    gx = vw(AR_A, 0, 4096, G)
    aamu = vw(AR_A, 4096, 4096, G)
    av = vw(AR_A, 8192, 4096, G)
    x2f = vw(AR_A, 12288, 4096, G)
    xm_tok = AR_A[:, :].rearrange("p (c n) -> p c n", c=TC // 128)
    x_fmb = vw(AR_X, 0, 16384, G)
    x1e = AR_B[:, 0:4120].rearrange("p (g n) -> p g n", g=G)
    y_fm = vw(AR_B, 0, 4096, G)

    halob = sbuf("halob", [128, G, 3], BF16)
    x1h3 = sbuf("x1h3", [128, G, 3], F32)
    x2b = sbuf("x2b", [128, G, ST], BF16)
    h_t = sbuf("h_t", [128, 2, ST], F32)
    p_t = sbuf("p_t", [128, 2, ST], F32)
    hc = sbuf("hc", [128, G], F32)
    pc = sbuf("pc", [128, G], F32)
    zeros512 = sbuf("zeros512", [128, ST], F32)
    car = sbuf("car", [128, 2 * NCORE, G], F32)
    h_in = sbuf("h_in", [128, G], F32)
    Hb = [sbuf(f"Hb_{i}", [128, G], F32) for i in range(2)]
    tmp8a = sbuf("tmp8a", [128, G], F32)
    tmp8b = sbuf("tmp8b", [128, G], F32)
    cexp = sbuf("cexp", [128, G], F32)
    idn_sb = sbuf("idn_sb", [128, 128], F32)
    bx_sb = sbuf("bx_sb", [128, G], F32)
    by_sb = sbuf("by_sb", [128, G], F32)
    cb_sb = sbuf("cb_sb", [128, G], F32)
    gib_sb = sbuf("gib_sb", [128, G], F32)
    gab_sb = sbuf("gab_sb", [128, G], F32)
    ap_sb = sbuf("ap_sb", [128, G], F32)
    cw_sb = sbuf("cw_sb", [128, G * 4], F32)
    cst = sbuf("cst", [128, 12], F32)
    xf2 = sbuf("xf2", [128, 2, ST], F32)
    xm_bf = sbuf("xm_bf", [128, D], BF16)
    stat = sbuf("stat", [128, 8], F32)
    x2b_f32 = x2b.rearrange("p g n -> p (g n)").bitcast(F32)
    lnsq = x2b_f32[:, 0:D]
    r1_tok = x2b_f32[:, D:2 * D]

    projT_sb = sbuf("projT_sb", [128, G, 8], BF16)
    pjb_sb = sbuf("pjb_sb", [128, 8], F32)
    eb_sb = sbuf("eb_sb", [128, 8], F32)
    io8_sb = sbuf("io8_sb", [128, 8], F32)
    io8p_sb = sbuf("io8p_sb", [128, 8], F32)
    idnb_sb = sbuf("idnb_sb", [128, 128], BF16)
    e1a = sbuf("e1a", [128, 16], F32)
    e2a = sbuf("e2a", [128, 16], F32)
    w1a = sbuf("w1a", [128, 16], F32)
    w2a = sbuf("w2a", [128, 16], F32)
    zch = sbuf("zch", [128, 8], F32)
    sg8 = sbuf("sg8", [128, 8], F32)
    eq8 = sbuf("eq8", [128, 8], F32)
    is8 = sbuf("is8", [128, 8], F32)
    t8 = sbuf("t8", [128, 8], F32)
    t8b = sbuf("t8b", [128, 8], F32)
    m1c = sbuf("m1c", [128, 1], F32)
    m2c = sbuf("m2c", [128, 1], F32)
    i1c = sbuf("i1c", [128, 1], F32)
    i2c = sbuf("i2c", [128, 1], F32)
    wsc = sbuf("wsc", [128, 1], F32)
    wic = sbuf("wic", [128, 1], F32)
    eA = eB = wAt = wBt = None
    fc1b_sb = sbuf("fc1b_sb", [128, 16], F32)
    sh1b_sb = sbuf("sh1b_sb", [128, 32], F32)
    hshb = sbuf("hshb", [128, 2, 512], BF16)

    arB = AR_B
    def bview(lo, n, g=None):
        a = arB[:, lo:lo + n]
        if g is not None:
            a = a.rearrange("p (g n) -> p g n", g=g)
        return a
    wm = bview(0, 256)
    eA = bview(768, 128)
    eB = bview(896, 128)
    wAt = bview(1024, 128)
    wBt = bview(1152, 128)
    io_ref = bview(256, 256)
    io_j = bview(512, 256)
    pos = bview(768, 256)
    rexc = bview(1024, 256)
    arr_ = bview(1280, 256)
    delt = bview(1536, 256)
    dshs = bview(1792, 256)
    selb = bview(2048, 256)
    bitu = sbuf("bitu_t", [128, 256], U32)
    mskio = bview(2304, 256)
    tokv = bview(2560, 256)
    eqm = bview(2816, 256)
    keptb = bview(3072, 256)
    rowm = bview(3328, 256)
    slotm = bview(3584, 256)
    fc2b_rp = bview(0, 1024)
    sh2b_rp = bview(1024, 1024)
    oeb = [bview(2048, 1024), bview(3072, 1024)]
    sht = [bview(0, 1024), bview(2048, 1024)]
    otok = bview(3072, 1024)

    xf2v = xf2.rearrange("p a n -> p (a n)")
    cntb = xf2v[:, 0:128]
    pr16 = xf2v[:, 0:128]
    base_bc = xf2v[:, 128:256]
    srcx = xf2v[:, 256:296]
    tokf = xf2v[:, 296:336]
    srcu = sbuf("srcu", [128, 40], U32)
    toku = sbuf("toku", [128, 40], U32)
    io_slot = xf2v[:, 416:456]
    _sc = {}
    for _i, _n in enumerate(("lo", "hi", "mid", "ge", "neq", "asg", "ov", "theta",
                             "cntgt", "m_p", "t1", "t2", "t3", "cnt1", "ctot", "bcol")):
        _sc[_n] = xf2v[:, 456 + _i:457 + _i]
    def sv(n):
        return _sc[n]
    mrow = xf2v[0:1, 472:600]
    brow = xf2v[0:1, 600:728]
    _p4 = {}
    for _i, _n in enumerate(("fo", "ixA", "ixB", "svA", "svB", "kA", "kB",
                             "wAe", "wBe")):
        _p4[_n] = xf2v[:, 728 + _i:729 + _i]
    def p4(n):
        return _p4[n]
    idxu4 = sbuf("idxu4", [128, 4], U32)
    ixAu = idxu4[:, 0:1]
    ixBu = idxu4[:, 1:2]
    rAu = idxu4[:, 2:3]
    rBu = idxu4[:, 3:4]
    sh1T_sb = AR_W[:, 0:32768].rearrange("p (g n) -> p g n", g=G)
    fc1T_sb = AR_W[:, 0:16384].rearrange("p (g n) -> p g n", g=G)
    fc2T_sb = AR_W[:, 16384:32768].rearrange("p (g n) -> p g n", g=16)
    sh2T_sb = AR_W[:, 0:32768].rearrange("p (g n) -> p g n", g=32)
    xeg = AR_X[:, 0:2048].rearrange("p (a n) -> p a n", a=2)
    xe_fm = AR_X[:, 2048:6144].rearrange("p (g n) -> p g n", g=G)
    he = AR_X[:, 6144:14336].rearrange("p (g n) -> p g n", g=16)
    hsh2 = [AR_X[:, 0:4096].rearrange("p (g n) -> p g n", g=32),
            AR_X[:, 4096:8192].rearrange("p (g n) -> p g n", g=32)]
    cout_slot_flat = cout_slot.rearrange("r n -> (r n)")

    ps = [psum(f"ps{i}", [128, ST], F32) for i in range(3)]
    pw = [psum(f"pw{i}", [128, D], F32) for i in range(2)]
    pb16 = psum("pb16", [128, D], BF16)

    sems = {}
    for name in ("sS", "sG", "sV", "sA", "sPE", "cC1", "cC2a", "cC2b",
                 "cSlot", "cOe"):
        sems[name] = C.sem(ctx.enter_context(nc.semaphore(name)))
    sS, sG, sV, sA, sPE = (sems[k] for k in ("sS", "sG", "sV", "sA", "sPE"))
    cC1 = sems["cC1"]
    cC2a = sems["cC2a"]
    cC2b = sems["cC2b"]

    CID = {"hv": 0, "mb": 1, "msk": 2, "msk1m": 5, "mye": 8, "ecap": 9,
           "iota128": 10, "hv1m": 11}

    def cstc(name, off=0):
        i = CID[name] + off
        return cst[:, i:i + 1]

    xT_r = dd["xT"].rearrange("(g p) t -> p g t", p=128)

    def program():
        mark, wait = C.mark, C.wait
        evac = {}

        def swv(i):
            n = mark(i, sV, 1)
            wait("vector", sV, n)
            return n

        def phase_barrier():
            for e in ENGINES:
                for sem in (sS, sG, sV, sA, sPE):
                    wait(e, sem, sem.n)

        for t_, d_ in [(idn_sb, dd["idn"]), (bx_sb, dd["bx2d"]), (by_sb, dd["by2d"]),
                       (cb_sb, dd["cb2d"]), (gib_sb, dd["gib2d"]), (gab_sb, dd["gab2d"]),
                       (ap_sb, dd["ap2d"]), (cw_sb, dd["cw2d"]), (cst, dd["consts"])]:
            mark(C.s.dma_start(out=t_[:], in_=d_), sS, 16)
        consts_ld = sS.n

        for wsb, wd in [(wx_sb, dd["w_x"]), (wgi_sb, dd["gi_w"]), (wga_sb, dd["ga_w"])]:
            for k in range(G):
                mark(C.g.dma_start(out=wsb[:, k, :], in_=wd[k * 128:(k + 1) * 128, :]), sG, 16)
        attn_w_ld = sG.n

        mark(C.g.dma_start(out=x_fmb[:, :, 0:ST], in_=xT_r[:, :, 3:3 + ST]), sG, 16)
        x_ld = {0: sG.n}
        mark(C.g.dma_start(out=halob[:], in_=xT_r[:, :, 0:3]), sG, 16)
        halo_ld = sG.n

        wait("scalar", sS, consts_ld)
        wait("vector", sS, consts_ld)
        a_n = mark(C.a.activation(out=tmp8a[:], in_=ap_sb[:], func=AF.Abs), sA, 1)
        wait("scalar", sA, a_n)
        a_n = mark(C.a.activation(out=tmp8b[:], in_=tmp8a[:], func=AF.Exp, scale=-1.0), sA, 1)
        wait("scalar", sA, a_n)
        mark(C.a.activation(out=tmp8a[:], in_=tmp8b[:], func=AF.Ln, bias=1.0), sA, 1)
        ln_done = sA.n
        swv(C.v.tensor_scalar_max(cexp[:], ap_sb[:], 0.0))
        wait("vector", sA, ln_done)
        swv(C.v.tensor_add(cexp[:], cexp[:], tmp8a[:]))
        swv(C.v.tensor_scalar_mul(cexp[:], cexp[:], -8.0))
        C.v.memset(zeros512[:], 0.0)
        C.v.memset(hc[:], 0.0)
        mark(C.v.memset(pc[:], 1.0), sV, 1)
        derived = sV.n

        scans_done = {-1: 0}
        gates_pe = {-1: 0}
        spill = {}
        last_x1_pe = 0
        for it in range(NTILE):
            if it + 1 < NTILE:
                mark(C.g.dma_start(out=x_fmb[:, :, (it + 1) * ST:(it + 2) * ST],
                                   in_=xT_r[:, :, 3 + (it + 1) * ST:3 + (it + 2) * ST]), sG, 16)
                x_ld[it + 1] = sG.n

            wait("scalar", sV, scans_done[it - 1])
            if it == 0:
                wait("scalar", sV, derived)

            if it == 0:
                wait("tensor", sG, attn_w_ld)
                wait("tensor", sG, halo_ld)
                for m in range(G):
                    mm = None
                    for k in range(G):
                        i = C.t.matmul(ps[0][:, 0:3], wx_sb[:, k, m * 128:(m + 1) * 128],
                                       halob[:, k, :], start=(k == 0), stop=(k == G - 1))
                        mm = i if i is not None else mm
                    pe_n = mark(mm, sPE, 1)
                    wait("scalar", sPE, pe_n)
                    a_n = mark(C.a.activation(out=x1e[:, m, 0:3], in_=ps[0][:, 0:3],
                                              func=AF.Identity, bias=bx_sb[:, m:m + 1]), sA, 1)
                    wait("tensor", sA, a_n)
                wait("vector", sA, sA.n)
                swv(C.v.tensor_scalar_mul(x1e[:, :, 0:3], x1e[:, :, 0:3], cstc("hv")))
                evac[0] = sA.n
            else:
                swv(C.v.tensor_copy(out=x1e[:, :, 0:3], in_=x1h3[:]))

            wait("tensor", sG, x_ld[it])
            wait("tensor", sG, attn_w_ld)
            wait("scalar", sV, scans_done[it - 1])
            for m in range(G):
                bank = m % 3
                if bank in evac:
                    wait("tensor", sA, evac[bank])
                mm = None
                for k in range(G):
                    i = C.t.matmul(ps[bank][:], wx_sb[:, k, m * 128:(m + 1) * 128],
                                   x_fmb[:, k, it * ST:(it + 1) * ST],
                                   start=(k == 0), stop=(k == G - 1))
                    mm = i if i is not None else mm
                pe_n = mark(mm, sPE, 1)
                wait("scalar", sPE, pe_n)
                evac[bank] = mark(C.a.activation(out=x1e[:, m, 3:3 + ST], in_=ps[bank][:],
                                                 func=AF.Identity, bias=bx_sb[:, m:m + 1]), sA, 1)
            x1_done = sA.n
            last_x1_pe = sPE.n

            wait("vector", sA, x1_done)
            for g in range(G):
                C.v.tensor_scalar(x2f[:, g], x1e[:, g, 3:3 + ST],
                                  cw_sb[:, g * 4 + 3:g * 4 + 4], cb_sb[:, g:g + 1],
                                  ALU.mult, ALU.add)
                for i3 in range(3):
                    C.v.scalar_tensor_tensor(x2f[:, g], x1e[:, g, i3:i3 + ST],
                                             cw_sb[:, g * 4 + i3:g * 4 + i3 + 1],
                                             x2f[:, g], ALU.mult, ALU.add)
            C.v.tensor_copy(out=x1h3[:], in_=x1e[:, :, ST:ST + 3])
            wait("vector", sPE, gates_pe[it - 1])
            mark(C.v.tensor_copy(out=x2b[:], in_=x2f[:]), sV, 1)
            x2_ready = sV.n

            wait("tensor", sV, x2_ready)
            for wsb, dstbuf, bias in ((wgi_sb, gx, gib_sb), (wga_sb, aamu, gab_sb)):
                for m in range(G):
                    bank = m % 3
                    wait("tensor", sA, evac[bank])
                    mm = None
                    for k in range(G):
                        i = C.t.matmul(ps[bank][:], wsb[:, k, m * 128:(m + 1) * 128],
                                       x2b[:, k, :], start=(k == 0), stop=(k == G - 1))
                        mm = i if i is not None else mm
                    pe_n = mark(mm, sPE, 1)
                    wait("scalar", sPE, pe_n)
                    evac[bank] = mark(C.a.activation(out=dstbuf[:, m], in_=ps[bank][:],
                                                     func=AF.Sigmoid, bias=bias[:, m:m + 1]), sA, 1)
            gates_pe[it] = sPE.n

            wait("scalar", sA, sA.n)
            laste = None
            for g in range(G):
                i = C.a.activation(out=av[:, g], in_=aamu[:, g], func=AF.Exp,
                                   scale=cexp[:, g:g + 1])
                laste = i if i is not None else laste
            a_n = mark(laste, sA, 1)
            wait("scalar", sA, a_n)
            mark(C.a.activation(out=aamu[:], in_=av[:], func=AF.Square), sA, 1)
            sq_done = sA.n
            wait("vector", sA, sq_done)
            C.v.tensor_scalar(aamu[:], aamu[:], -1.0, 1.0, ALU.mult, ALU.add)
            mark(C.v.tensor_scalar_max(aamu[:], aamu[:], 0.0), sV, 1)
            mu_pre = sV.n
            wait("scalar", sV, mu_pre)
            mark(C.a.activation(out=aamu[:], in_=aamu[:], func=AF.Sqrt), sA, 1)
            mu_done = sA.n

            wait("vector", sA, mu_done)
            if it == 0:
                swv(C.v.tensor_scalar_mul(av[:, :, 0:1], av[:, :, 0:1], cstc("hv")))
                swv(C.v.tensor_scalar(aamu[:, :, 0:1], aamu[:, :, 0:1], cstc("hv"),
                                      cstc("hv1m"), ALU.mult, ALU.add))

            if debug_stage == "p1t0" and it == 0:
                mark(C.v.nop(), sV, 1)
                wait("sync", sV, sV.n)
                wait("sync", sA, sA.n)
                for g in range(G):
                    mark(C.s.dma_start(out=out[g * 128:(g + 1) * 128, 0:512], in_=x1e[:, g, 3:3 + ST]), sS, 16)
                    mark(C.s.dma_start(out=out[g * 128:(g + 1) * 128, 512:1024], in_=gx[:, g]), sS, 16)
                    mark(C.s.dma_start(out=out[1024 + g * 128:1024 + (g + 1) * 128, 0:512], in_=av[:, g]), sS, 16)
                    mark(C.s.dma_start(out=out[1024 + g * 128:1024 + (g + 1) * 128, 512:1024], in_=aamu[:, g]), sS, 16)
                wait("gpsimd", sS, sS.n)
                wait("gpsimd", sG, sG.n)
                return
            C.v.tensor_mul(x2f[:], gx[:], x2f[:])
            C.v.tensor_mul(x2f[:], x2f[:], aamu[:])

            for g in range(G):
                gi = (it * G + g) % 2
                if (("sp", gi) in spill):
                    wait("vector", sS, spill[("sp", gi)])
                C.v.tensor_tensor_scan(h_t[:, gi], av[:, g], x2f[:, g],
                                       hc[:, g:g + 1], ALU.mult, ALU.add)
                swv(C.v.tensor_copy(out=hc[:, g:g + 1], in_=h_t[:, gi, ST - 1:ST]))
                C.v.tensor_tensor_scan(p_t[:, gi], av[:, g], zeros512[:],
                                       pc[:, g:g + 1], ALU.mult, ALU.add)
                v_n = swv(C.v.tensor_copy(out=pc[:, g:g + 1], in_=p_t[:, gi, ST - 1:ST]))
                wait("sync", sV, v_n)
                mark(C.s.dma_start(out=h_sp[g, :, it * ST:(it + 1) * ST], in_=h_t[:, gi]), sS, 16)
                mark(C.s.dma_start(out=p_sp[g, :, it * ST:(it + 1) * ST], in_=p_t[:, gi]), sS, 16)
                spill[("sp", gi)] = sS.n
            scans_done[it] = sV.n

        wait("gpsimd", sPE, last_x1_pe)
        for k in range(G):
            mark(C.g.dma_start(out=wx_sb[:, k, :], in_=dd["w_y"][k * 128:(k + 1) * 128, :]), sG, 16)
        wy_ld = sG.n

        phase_barrier()
        wait("gpsimd", sV, scans_done[NTILE - 1])
        mark(C.g.dma_start(out=cin_carry[0:1, :].rearrange("a (p g) -> (a p) g", p=128),
                           in_=pc[:]), sG, 16)
        mark(C.g.dma_start(out=cin_carry[1:2, :].rearrange("a (p g) -> (a p) g", p=128),
                           in_=hc[:]), sG, 16)
        wait("gpsimd", sG, sG.n)
        mark(C.g.collective_compute("AllGather", ALU.bypass, replica_groups=RG,
                                    ins=[cin_carry.opt()], outs=[cout_carry.opt()]), cC1, 1)
        wait("gpsimd", cC1, 1)
        for q in range(4):
            mark(C.g.dma_start(
                out=car[:, q * 4:(q + 1) * 4, :],
                in_=cout_carry.rearrange("r (p g) -> p r g", p=128)[:, q * 4:(q + 1) * 4, :]),
                sG, 16)
        carry_ld = sG.n

        wait("vector", sG, carry_ld)
        for bb in range(2):
            swv(C.v.memset(Hb[bb][:], 0.0))
            for s_ in range(3):
                r = 4 * bb + s_
                swv(C.v.tensor_scalar(tmp8a[:], car[:, 2 * r, :], cstc("msk", s_),
                                      cstc("msk1m", s_), ALU.mult, ALU.add))
                swv(C.v.tensor_scalar_mul(tmp8b[:], car[:, 2 * r + 1, :], cstc("msk", s_)))
                swv(C.v.tensor_mul(Hb[bb][:], Hb[bb][:], tmp8a[:]))
                swv(C.v.tensor_add(Hb[bb][:], Hb[bb][:], tmp8b[:]))
        swv(C.v.tensor_scalar_mul(h_in[:], Hb[1][:], cstc("mb")))
        swv(C.v.scalar_tensor_tensor(tmp8a[:], Hb[0][:], cstc("mb"), Hb[0][:],
                                     ALU.mult, ALU.subtract))
        mark(C.v.tensor_sub(h_in[:], h_in[:], tmp8a[:]), sV, 1)
        hin_done = sV.n

        if debug_stage == "carry":
            wait("sync", sV, hin_done)
            mark(C.s.dma_start(out=out[0:128, 0:8], in_=h_in[:]), sS, 16)
            mark(C.s.dma_start(out=out[128:256, 0:8], in_=Hb[0][:]), sS, 16)
            mark(C.s.dma_start(out=out[128:256, 8:16], in_=Hb[1][:]), sS, 16)
            mark(C.s.dma_start(out=out[128:256, 16:28], in_=cst[:]), sS, 16)
            mark(C.s.dma_start(out=out[128:256, 28:36], in_=tmp8a[:]), sS, 16)
            mark(C.s.dma_start(out=out[0:128, 8:136], in_=car[:].rearrange("p r g -> p (r g)")), sS, 16)
            mark(C.s.dma_start(out=out[0:128, 136:144], in_=hc[:]), sS, 16)
            mark(C.s.dma_start(out=out[0:128, 144:152], in_=pc[:]), sS, 16)
            for g in range(G):
                mark(C.s.dma_start(out=out[256 + g * 128:256 + (g + 1) * 128, 0:512], in_=h_sp[g, :, 0:512]), sS, 16)
            wait("gpsimd", sS, sS.n)
            wait("gpsimd", sG, sG.n)
            return

        phase_barrier()
        wait("sync", sS, sS.n)
        r1_done = {-1: 0}
        tr_pe = {-1: 0}
        hp_ld = {}
        xm_chunk = {}
        for it in range(NTILE):
            wait("tensor", sG, wy_ld)
            wait("scalar", sV, r1_done[it - 1])
            wait("scalar", sPE, tr_pe[it - 1])
            wait("scalar", sV, scans_done[NTILE - 1] if it == 0 else 0)
            for m in range(G):
                bank = m % 3
                wait("tensor", sA, evac[bank])
                mm = None
                for k in range(G):
                    i = C.t.matmul(ps[bank][:], wx_sb[:, k, m * 128:(m + 1) * 128],
                                   x_fmb[:, k, it * ST:(it + 1) * ST],
                                   start=(k == 0), stop=(k == G - 1))
                    mm = i if i is not None else mm
                pe_n = mark(mm, sPE, 1)
                wait("scalar", sPE, pe_n)
                evac[bank] = mark(C.a.activation(out=y_fm[:, m], in_=ps[bank][:],
                                                 func=AF.Gelu, bias=by_sb[:, m:m + 1]), sA, 1)
            y_done = sA.n

            wait("vector", sA, y_done)
            wait("vector", sV, hin_done)
            lastr = None
            for g in range(G):
                j = it * G + g
                gi = j % 2
                if j - 2 in hp_ld:
                    wait("sync", sV, hp_ld[j - 2])
                mark(C.s.dma_start(out=h_t[:, gi], in_=h_sp[g, :, it * ST:(it + 1) * ST]), sS, 16)
                mark(C.s.dma_start(out=p_t[:, gi], in_=p_sp[g, :, it * ST:(it + 1) * ST]), sS, 16)
                mark(C.s.dma_start(out=xf2[:, gi], in_=xT_r[:, g, 3 + it * ST:3 + (it + 1) * ST]), sS, 16)
                s_n = sS.n
                wait("vector", sS, s_n)
                C.v.scalar_tensor_tensor(h_t[:, gi], p_t[:, gi], h_in[:, g:g + 1],
                                         h_t[:, gi], ALU.mult, ALU.add)
                C.v.tensor_mul(y_fm[:, g], y_fm[:, g], h_t[:, gi])
                i = C.v.tensor_add(y_fm[:, g], y_fm[:, g], xf2[:, gi])
                lastr = i if i is not None else lastr
                hp_ld[j] = mark(lastr, sV, 1)
            r1_done[it] = sV.n

            if debug_stage == "p2" and it == 0:
                wait("sync", sV, r1_done[0])
                for g in range(G):
                    mark(C.s.dma_start(out=out[g * 128:(g + 1) * 128, 0:512], in_=y_fm[:, g]), sS, 16)
                mark(C.s.dma_start(out=out[1024:1152, 0:512], in_=h_t[:, 0]), sS, 16)
                mark(C.s.dma_start(out=out[1152:1280, 0:512], in_=h_t[:, 1]), sS, 16)
                mark(C.s.dma_start(out=out[1280:1408, 0:8], in_=h_in[:]), sS, 16)
                wait("gpsimd", sS, sS.n)
                wait("gpsimd", sG, sG.n)
                return
            for cch in range(ST // 128):
                chunk = it * (ST // 128) + cch
                pb = pw[chunk % 2]
                wait("tensor", sV, r1_done[it])
                if ("pw", chunk % 2) in evac:
                    wait("tensor", sV, evac[("pw", chunk % 2)])
                mm = None
                for g in range(G):
                    i = C.t.transpose(pb[:, g * 128:(g + 1) * 128],
                                      y_fm[:, g, cch * 128:(cch + 1) * 128], idn_sb[:])
                    mm = i if i is not None else mm
                pe_n = mark(mm, sPE, 1)
                wait("vector", sPE, pe_n)
                if chunk > 0:
                    wait("vector", sA, xm_chunk[chunk - 1])
                evac[("pw", chunk % 2)] = mark(C.v.tensor_copy(out=r1_tok[:], in_=pb[:]), sV, 1)
                tr_done = sV.n

                wait("scalar", sV, tr_done)
                mark(C.a.activation(out=lnsq[:], in_=r1_tok[:], func=AF.Copy,
                                    accum_out=stat[:, 0:1]), sA, 1)
                a_n = mark(C.a.activation(out=lnsq[:], in_=r1_tok[:], func=AF.Square,
                                          accum_out=stat[:, 1:2]), sA, 1)
                xm_chunk[chunk] = a_n
                wait("vector", sA, a_n)
                swv(C.v.tensor_scalar_mul(stat[:, 2:3], stat[:, 0:1], 1.0 / D))
                swv(C.v.tensor_scalar_mul(stat[:, 3:4], stat[:, 1:2], 1.0 / D))
                swv(C.v.tensor_mul(stat[:, 4:5], stat[:, 2:3], stat[:, 2:3]))
                swv(C.v.tensor_sub(stat[:, 3:4], stat[:, 3:4], stat[:, 4:5]))
                mark(C.v.tensor_scalar(stat[:, 3:4], stat[:, 3:4], EPS, None, ALU.add), sV, 1)
                var_n = sV.n
                wait("scalar", sV, var_n)
                sd_n = mark(C.a.activation(out=stat[:, 5:6], in_=stat[:, 3:4], func=AF.Sqrt), sA, 1)
                wait("vector", sA, sd_n)
                swv(C.v.reciprocal(stat[:, 6:7], stat[:, 5:6]))
                C.v.tensor_scalar(xm_tok[:, chunk, :], r1_tok[:],
                                  stat[:, 2:3], stat[:, 6:7], ALU.subtract, ALU.mult)
                mark(C.v.tensor_copy(out=xm_bf[:], in_=xm_tok[:, chunk, :]), sV, 1)
                xmb_n = sV.n
                wait("sync", sV, xmb_n)
                mark(C.s.dma_start(out=cin_xm[chunk * 128:(chunk + 1) * 128, :],
                                   in_=xm_bf[:]), sS, 16)
            tr_pe[it] = sPE.n

        if debug_stage == "xm":
            wait("sync", sV, sV.n)
            for chunk in range(TC // 128):
                mark(C.s.dma_start(out=out[chunk * 128:(chunk + 1) * 128, :],
                                   in_=xm_tok[:, chunk, :]), sS, 16)
            wait("gpsimd", sS, sS.n)
            wait("gpsimd", sG, sG.n)
            return

        phase_barrier()
        y_pe_last = tr_pe[NTILE - 1]
        xm_fm = x_fmb

        for k in range(G):
            mark(C.g.dma_start(out=projT_sb[:, k, :], in_=dd["projT"][k * 128:(k + 1) * 128, :]), sG, 16)
        for t_, d_ in [(pjb_sb, dd["projb_rep"]), (eb_sb, dd["ebias_rep"]),
                       (io8_sb, dd["iota8"]), (io8p_sb, dd["iota8p"]),
                       (idnb_sb, dd["idnb"])]:
            mark(C.g.dma_start(out=t_[:], in_=d_), sG, 16)
        rconsts_ld = sG.n

        xm_all_written = sS.n
        wait("gpsimd", sS, xm_all_written)
        mark(C.g.collective_compute("AllGather", ALU.bypass, replica_groups=RG,
                                    ins=[cin_xm.opt()], outs=[cout_xm.opt()]), cC2a, 1)

        wait("sync", sPE, y_pe_last)
        wait("sync", sS, xm_all_written)
        for g in range(G):
            for cq in range(4):
                mark(C.s.dma_start_transpose(
                    xm_fm[:, g, cq * 512:(cq + 1) * 512],
                    cin_xm[cq * 512:(cq + 1) * 512, g * 128:(g + 1) * 128]), sS, 16)
        xmfm_ld = sS.n

        for chunk in range(TC // 128):
            wait("tensor", sS, xmfm_ld)
            wait("tensor", sG, rconsts_ld)
            wait("tensor", sA, evac.get(0, 0))
            wait("tensor", sV, evac.get(("z",), 0))
            mm = None
            for k in range(G):
                i = C.t.matmul(ps[0][:, 0:8], xm_fm[:, k, chunk * 128:(chunk + 1) * 128],
                               projT_sb[:, k, :], start=(k == 0), stop=(k == G - 1))
                mm = i if i is not None else mm
            pe_n = mark(mm, sPE, 1)
            wait("vector", sPE, pe_n)
            evac[("z",)] = swv(C.v.tensor_copy(out=zch[:], in_=ps[0][:, 0:8]))
            swv(C.v.tensor_add(zch[:], zch[:], pjb_sb[:]))
            zn = sV.n
            wait("scalar", sV, zn)
            a_n = mark(C.a.activation(out=sg8[:], in_=zch[:], func=AF.Sigmoid), sA, 1)
            wait("vector", sA, a_n)
            swv(C.v.tensor_add(sg8[:], sg8[:], eb_sb[:]))
            swv(C.v.tensor_scalar_max(sg8[:], sg8[:], 1e-12))
            swv(C.v.tensor_reduce(m1c[:], sg8[:], AX.X, ALU.max))
            swv(C.v.tensor_scalar(eq8[:], sg8[:], m1c[:, 0:1], None, ALU.is_equal))
            swv(C.v.scalar_tensor_tensor(t8[:], eq8[:], -1000.0, io8p_sb[:], ALU.mult, ALU.add))
            swv(C.v.tensor_reduce(i1c[:], t8[:], AX.X, ALU.min))
            swv(C.v.tensor_copy(out=e1a[:, chunk:chunk + 1], in_=i1c[:]))
            swv(C.v.tensor_scalar(is8[:], io8_sb[:], e1a[:, chunk:chunk + 1], None, ALU.is_equal))
            swv(C.v.scalar_tensor_tensor(t8b[:], is8[:], -4.0, sg8[:], ALU.mult, ALU.add))
            swv(C.v.tensor_reduce(m2c[:], t8b[:], AX.X, ALU.max))
            swv(C.v.tensor_scalar(eq8[:], t8b[:], m2c[:, 0:1], None, ALU.is_equal))
            swv(C.v.scalar_tensor_tensor(t8[:], eq8[:], -1000.0, io8p_sb[:], ALU.mult, ALU.add))
            swv(C.v.tensor_reduce(i2c[:], t8[:], AX.X, ALU.min))
            swv(C.v.tensor_copy(out=e2a[:, chunk:chunk + 1], in_=i2c[:]))
            swv(C.v.tensor_add(wsc[:], m1c[:], m2c[:]))
            swv(C.v.reciprocal(wic[:], wsc[:]))
            swv(C.v.tensor_mul(w1a[:, chunk:chunk + 1], m1c[:], wic[:]))
            swv(C.v.tensor_mul(w2a[:, chunk:chunk + 1], m2c[:], wic[:]))
        route_done = sV.n

        pair_dma = 0
        for ri, arrq in enumerate((e1a, e2a, w1a, w2a)):
            wait("tensor", sV, route_done)
            wait("tensor", sV, evac.get(("pair",), 0))
            wait("tensor", sA, evac.get(1, 0))
            mm = C.t.transpose(ps[1][0:16, 0:128], arrq[:], idn_sb[:])
            pe_n = mark(mm, sPE, 1)
            wait("vector", sPE, pe_n)
            wait("vector", sS, pair_dma)
            evac[("pair",)] = swv(C.v.tensor_copy(out=pr16[0:16, :], in_=ps[1][0:16, 0:128]))
            evac[1] = sA.n
            wait("sync", sV, sV.n)
            mark(C.s.dma_start(out=cin_pair[ri:ri + 1, :].rearrange("a (c p) -> (a c) p", c=16),
                               in_=pr16[0:16, :]), sS, 16)
            pair_dma = sS.n
        pair_written = sS.n
        wait("gpsimd", sS, pair_written)
        mark(C.g.collective_compute("AllGather", ALU.bypass, replica_groups=RG,
                                    ins=[cin_pair.opt()], outs=[cout_pair.opt()]), cC2b, 1)

        if debug_stage == "routing":
            wait("gpsimd", cC2b, 1)
            mark(C.g.dma_start(out=out[0:128, 0:16], in_=w1a[:]), sG, 16)
            mark(C.g.dma_start(out=out[0:128, 16:32], in_=w2a[:]), sG, 16)
            mark(C.g.dma_start(out=out[0:128, 32:48], in_=e1a[:]), sG, 16)
            mark(C.g.dma_start(out=out[0:128, 48:64], in_=e2a[:]), sG, 16)
            for r in range(32):
                mark(C.g.dma_start(out=out[256 + r * 2:256 + r * 2 + 1, 0:1024], in_=cout_pair[r:r + 1, 0:1024]), sG, 16)
                mark(C.g.dma_start(out=out[256 + r * 2 + 1:256 + r * 2 + 2, 0:1024], in_=cout_pair[r:r + 1, 1024:2048]), sG, 16)
            wait("gpsimd", sS, sS.n)
            wait("gpsimd", sG, sG.n)
            return

        wait("gpsimd", sPE, y_pe_last)
        for k in range(G):
            mark(C.g.dma_start(out=sh1T_sb[:, k, :], in_=dd["sh1T"][k * 128:(k + 1) * 128, :]), sG, 16)
        sh1_ld = sG.n
        mark(C.g.dma_start(out=sh1b_sb[:], in_=dd["sh1b2d"]), sG, 16)
        mark(C.g.dma_start(out=fc1b_sb[:], in_=dd["fc1b2d"]), sG, 16)
        smallb_ld = sG.n

        phase_barrier()
        wait("tensor", sV, evac.get(("z",), 0))
        wait("tensor", sV, evac.get(("pair",), 0))
        hsh_ev = {}
        for itl in range(NTILE):
            for m in range(SHD // 128):
                bank = m % 3
                wait("tensor", sG, sh1_ld)
                wait("tensor", sS, xmfm_ld)
                wait("tensor", sA, evac.get(bank, 0))
                mm = None
                for k in range(G):
                    i = C.t.matmul(ps[bank][:], sh1T_sb[:, k, m * 128:(m + 1) * 128],
                                   xm_fm[:, k, itl * ST:(itl + 1) * ST],
                                   start=(k == 0), stop=(k == G - 1))
                    mm = i if i is not None else mm
                pe_n = mark(mm, sPE, 1)
                wait("scalar", sPE, pe_n)
                wait("scalar", sG, smallb_ld)
                hb = m % 2
                if ("hshb", hb) in hsh_ev:
                    wait("scalar", sS, hsh_ev[("hshb", hb)])
                evac[bank] = mark(C.a.activation(out=hshb[:, hb], in_=ps[bank][:],
                                                 func=AF.Gelu, bias=sh1b_sb[:, m:m + 1]), sA, 1)
                wait("sync", sA, evac[bank])
                mark(C.s.dma_start(out=hsh_sp[m * 128:(m + 1) * 128, itl * ST:(itl + 1) * ST],
                                   in_=hshb[:, hb]), sS, 16)
                hsh_ev[("hshb", hb)] = sS.n
        sh1_pe_last = sPE.n
        hsh_written = sS.n

        wait("gpsimd", cC2b, 1)
        for qi, dst in ((0, eA), (1, eB), (2, wAt), (3, wBt)):
            for r in range(NCORE):
                mark(C.g.dma_start(
                    out=dst[r * 16:(r + 1) * 16, :],
                    in_=cout_pair[4 * r + qi:4 * r + qi + 1, :]
                        .rearrange("a (i j) -> (a i) j", i=16)), sG, 16)
        ptiles_ld = sG.n
        wait("gpsimd", sV, route_done)
        mark(C.g.dma_start(out=io_ref[:], in_=dd["iota_ref"]), sG, 16)
        mark(C.g.dma_start(out=io_j[:], in_=dd["iota_j"]), sG, 16)
        mark(C.g.dma_start(out=io_slot[:], in_=dd["iota_slot"]), sG, 16)
        sel_ld = sG.n

        wait("vector", sG, ptiles_ld)
        wait("vector", sG, sel_ld)
        for half, (eT, wT) in enumerate(((eA, wAt), (eB, wBt))):
            sl = slice(half * 128, (half + 1) * 128)
            swv(C.v.tensor_scalar(selb[:, sl], eT[:], cstc("mye"), None, ALU.is_equal))
            swv(C.v.tensor_mul(wm[:, sl], wT[:], selb[:, sl]))
            swv(C.v.tensor_add(wm[:, sl], wm[:, sl], selb[:, sl]))
            swv(C.v.tensor_scalar(wm[:, sl], wm[:, sl], -1.0, None, ALU.add))

        def count_round(src, scal, op):
            swv(C.v.tensor_scalar(selb[:], src, scal, None, op, ALU.add,
                                  accum_out=sv("cnt1")))
            v_n = sV.n
            wait("gpsimd", sV, v_n)
            mark(C.g.dma_start(out=cnt_dram.rearrange("(p a) -> p a", a=1), in_=sv("cnt1")), sG, 16)
            wait("gpsimd", sG, sG.n)
            mark(C.g.dma_start(out=cntb, in_=cnt_dram.rearrange("(a p) -> a p", a=1).to_broadcast([128, 128])), sG, 16)
            g_n = sG.n
            wait("vector", sG, g_n)
            swv(C.v.tensor_reduce(sv("ctot"), cntb, AX.X, ALU.add))

        count_round(wm[:], -0.5, ALU.is_gt)
        swv(C.v.tensor_copy(out=sv("asg"), in_=sv("ctot")))
        swv(C.v.memset(sv("lo"), 1e-14))
        swv(C.v.memset(sv("hi"), 2.0))
        for rnd in range(30):
            if rnd < 10:
                swv(C.v.tensor_mul(sv("mid"), sv("lo"), sv("hi")))
                mn = sV.n
                wait("scalar", sV, mn)
                a_n = mark(C.a.activation(out=sv("mid"), in_=sv("mid"), func=AF.Sqrt), sA, 1)
                wait("vector", sA, a_n)
            else:
                swv(C.v.tensor_add(sv("mid"), sv("lo"), sv("hi")))
                swv(C.v.tensor_scalar_mul(sv("mid"), sv("mid"), 0.5))
            count_round(wm[:], sv("mid"), ALU.is_gt)
            swv(C.v.tensor_scalar(sv("ge"), sv("ctot"), float(CAP), None, ALU.is_ge))
            swv(C.v.tensor_sub(sv("t1"), sv("mid"), sv("lo")))
            swv(C.v.tensor_mul(sv("t1"), sv("t1"), sv("ge")))
            swv(C.v.tensor_add(sv("lo"), sv("lo"), sv("t1")))
            swv(C.v.tensor_scalar(sv("t2"), sv("ge"), -1.0, 1.0, ALU.mult, ALU.add))
            swv(C.v.tensor_sub(sv("t3"), sv("mid"), sv("hi")))
            swv(C.v.tensor_mul(sv("t3"), sv("t3"), sv("t2")))
            swv(C.v.tensor_add(sv("hi"), sv("hi"), sv("t3")))
        swv(C.v.tensor_scalar(sv("ov"), sv("asg"), float(CAP), None, ALU.is_gt))
        swv(C.v.tensor_mul(sv("t1"), sv("ov"), sv("hi")))
        swv(C.v.tensor_scalar(sv("t2"), sv("ov"), 1.0, -1.0, ALU.mult, ALU.add))
        swv(C.v.tensor_add(sv("theta"), sv("t1"), sv("t2")))
        count_round(wm[:], sv("theta"), ALU.is_gt)
        swv(C.v.tensor_copy(out=sv("cntgt"), in_=sv("ctot")))
        swv(C.v.tensor_scalar(sv("neq"), sv("cntgt"), -1.0, float(CAP), ALU.mult, ALU.add))
        swv(C.v.tensor_scalar(keptb[:], wm[:], sv("theta"), None, ALU.is_gt))
        swv(C.v.tensor_scalar(eqm[:], wm[:], sv("theta"), None, ALU.is_equal))
        swv(C.v.tensor_scalar(selb[:], eqm[:], -1e9, 1e9, ALU.mult, ALU.add))
        swv(C.v.tensor_mul(mskio[:], eqm[:], io_ref[:]))
        swv(C.v.tensor_add(mskio[:], mskio[:], selb[:]))
        swv(C.v.memset(sv("lo"), -1.0))
        swv(C.v.memset(sv("hi"), 32768.0))
        for rnd in range(17):
            swv(C.v.tensor_add(sv("mid"), sv("lo"), sv("hi")))
            swv(C.v.tensor_scalar_mul(sv("mid"), sv("mid"), 0.5))
            count_round(mskio[:], sv("mid"), ALU.is_le)
            swv(C.v.tensor_tensor(sv("ge"), sv("ctot"), sv("neq"), ALU.is_ge))
            swv(C.v.tensor_sub(sv("t1"), sv("mid"), sv("hi")))
            swv(C.v.tensor_mul(sv("t1"), sv("t1"), sv("ge")))
            swv(C.v.tensor_add(sv("hi"), sv("hi"), sv("t1")))
            swv(C.v.tensor_scalar(sv("t2"), sv("ge"), -1.0, 1.0, ALU.mult, ALU.add))
            swv(C.v.tensor_sub(sv("t3"), sv("mid"), sv("lo")))
            swv(C.v.tensor_mul(sv("t3"), sv("t3"), sv("t2")))
            swv(C.v.tensor_add(sv("lo"), sv("lo"), sv("t3")))
        swv(C.v.tensor_scalar(selb[:], io_ref[:], sv("hi"), None, ALU.is_le))
        swv(C.v.tensor_mul(selb[:], selb[:], eqm[:]))
        swv(C.v.tensor_scalar(eqm[:], wm[:], -0.5, None, ALU.is_gt))
        swv(C.v.tensor_mul(selb[:], selb[:], eqm[:]))
        swv(C.v.tensor_add(keptb[:], keptb[:], selb[:]))
        swv(C.v.memset(dshs[:], 1.0))
        swv(C.v.tensor_tensor_scan(pos[:], dshs[:], keptb[:], 0.0, ALU.mult, ALU.add))
        swv(C.v.tensor_copy(out=sv("m_p"), in_=pos[:, 255:256]))
        swv(C.v.tensor_sub(rexc[:], pos[:], keptb[:]))
        wait("gpsimd", sV, sV.n)
        mark(C.g.dma_start(out=cnt_dram.rearrange("(p a) -> p a", a=1), in_=sv("m_p")), sG, 16)
        wait("gpsimd", sG, sG.n)
        mark(C.g.dma_start(out=mrow, in_=cnt_dram.rearrange("(a p) -> a p", a=1)), sG, 16)
        g_n = sG.n
        wait("vector", sG, g_n)
        swv(C.v.tensor_tensor_scan(brow, dshs[0:1, 0:128], mrow, 0.0, ALU.mult, ALU.add))
        swv(C.v.memset(sv("t3"), 0.0))
        v_n = sV.n
        wait("gpsimd", sV, v_n)
        mark(C.g.dma_start(out=base_dram[1:129].rearrange("(a p) -> a p", a=1), in_=brow), sG, 16)
        mark(C.g.dma_start(out=base_dram[0:1].rearrange("(a p) -> a p", a=1), in_=sv("t3")[0:1, :]), sG, 16)
        wait("gpsimd", sG, sG.n)
        mark(C.g.dma_start(out=sv("bcol"), in_=base_dram[0:128].rearrange("(p a) -> p a", a=1)), sG, 16)
        mark(C.g.dma_start(out=base_bc, in_=base_dram[0:128].rearrange("(a p) -> a p", a=1).to_broadcast([128, 128])), sG, 16)
        base_ld = sG.n
        wait("vector", sG, base_ld)
        swv(C.v.tensor_scalar(slotm[:], rexc[:], sv("bcol"), None, ALU.add))
        swv(C.v.tensor_scalar(rowm[:], slotm[:], cstc("ecap"), -SENT_ROW, ALU.add, ALU.add))
        swv(C.v.tensor_mul(rowm[:], rowm[:], keptb[:]))
        swv(C.v.tensor_scalar(rowm[:], rowm[:], SENT_ROW, None, ALU.add))
        v_n = sV.n
        wait("gpsimd", sV, v_n)
        mark(C.g.dma_start(out=cin_slot[0:1, :].rearrange("a (p j) -> (a p) j", p=128),
                           in_=rowm[:]), sG, 16)
        wait("gpsimd", sG, sG.n)
        mark(C.g.collective_compute("AllGather", ALU.bypass, replica_groups=RG,
                                    ins=[cin_slot.opt()], outs=[cout_slot.opt()]), sems["cSlot"], 1)
        swv(C.v.tensor_scalar_mul(tokv[:, 0:128], io_ref[:, 0:128], 0.5))
        swv(C.v.tensor_scalar(tokv[:, 128:256], io_ref[:, 128:256], 0.5, -0.5, ALU.mult, ALU.add))
        swv(C.v.tensor_scalar(arr_[:], tokv[:], -16384.0, None, ALU.add))
        swv(C.v.tensor_mul(arr_[:], arr_[:], keptb[:]))
        swv(C.v.tensor_scalar(arr_[:], arr_[:], 16384.0, None, ALU.add))
        swv(C.v.tensor_sub(delt[:], io_j[:], rexc[:]))
        for bbit in range(8):
            sft = 1 << bbit
            w_ = 256 - sft
            swv(C.v.tensor_scalar(mskio[:], delt[:], 0.5 / sft, None, ALU.mult))
            swv(C.v.tensor_copy(out=bitu[:], in_=mskio[:]))
            swv(C.v.tensor_copy(out=dshs[:], in_=bitu[:]))
            swv(C.v.tensor_sub(dshs[:], mskio[:], dshs[:]))
            swv(C.v.tensor_mul(dshs[:], dshs[:], dshs[:]))
            swv(C.v.tensor_scalar(dshs[:], dshs[:], 0.0625, None, ALU.is_ge))
            if debug_stage == "fulldbg" and bbit == 0:
                wait("gpsimd", sV, sV.n)
                mark(C.g.dma_start(out=dbg_sel2[:, 1300:1556], in_=mskio[:]), sG, 16)
                mark(C.g.dma_start(out=dbg_sel2[:, 1556:1812], in_=dshs[:]), sG, 16)
                mark(C.g.dma_start(out=dbg_sel2[:, 1812:2068], in_=delt[:]), sG, 16)
                wait("vector", sG, sG.n)
            swv(C.v.tensor_copy(out=bitu[:], in_=dshs[:]))
            swv(C.v.scalar_tensor_tensor(mskio[:], dshs[:], -float(sft), delt[:],
                                         ALU.mult, ALU.add))
            swv(C.v.select(arr_[:, 0:w_], bitu[:, sft:256], arr_[:, sft:256], arr_[:, 0:w_]))
            swv(C.v.select(delt[:, 0:w_], bitu[:, sft:256], mskio[:, sft:256], mskio[:, 0:w_]))
            if sft > 0:
                swv(C.v.tensor_copy(out=delt[:, w_:256], in_=mskio[:, w_:256]))
        swv(C.v.tensor_scalar(dshs[:], io_j[:], sv("m_p"), None, ALU.is_ge))
        swv(C.v.tensor_scalar(selb[:], arr_[:], -1.0, 16384.0, ALU.mult, ALU.add))
        swv(C.v.tensor_mul(selb[:], selb[:], dshs[:]))
        swv(C.v.tensor_add(arr_[:], arr_[:], selb[:]))
        v_n = sV.n
        wait("gpsimd", sV, v_n)
        mark(C.g.dma_start(out=padded_dram.rearrange("(p j) -> p j", p=128), in_=arr_[:]), sG, 16)
        padded_w = sG.n
        if debug_stage == "fulldbg":
            wait("gpsimd", sG, padded_w)
            mark(C.g.dma_start(out=dbg_sel2[:, 0:16], in_=xf2v[:, 456:472]), sG, 16)
            mark(C.g.dma_start(out=dbg_sel2[:, 16:272], in_=keptb[:]), sG, 16)
            mark(C.g.dma_start(out=dbg_sel2[:, 272:528], in_=arr_[:]), sG, 16)
            mark(C.g.dma_start(out=dbg_sel2[:, 528:784], in_=io_j[:]), sG, 16)
            mark(C.g.dma_start(out=dbg_sel2[:, 784:1040], in_=delt[:]), sG, 16)
            mark(C.g.dma_start(out=dbg_sel2[:, 1040:1296], in_=rexc[:]), sG, 16)
        for c40 in range(40):
            jc = io_slot[:, c40:c40 + 1]
            swv(C.v.tensor_scalar(cntb[:], base_bc[:], jc, None, ALU.is_le, ALU.add,
                                  accum_out=sv("cnt1")))
            swv(C.v.tensor_mul(dshs[:, 0:128], cntb[:], base_bc[:]))
            swv(C.v.tensor_reduce(sv("t1"), dshs[:, 0:128], AX.X, ALU.max))
            swv(C.v.tensor_scalar(sv("t2"), sv("cnt1"), 256.0, -256.0, ALU.mult, ALU.add))
            swv(C.v.tensor_sub(sv("t3"), jc, sv("t1")))
            swv(C.v.tensor_add(srcx[:, c40:c40 + 1], sv("t2"), sv("t3")))
        swv(C.v.tensor_scalar_min(srcx[:], srcx[:], 32767.0))
        swv(C.v.tensor_scalar_max(srcx[:], srcx[:], 0.0))
        swv(C.v.tensor_copy(out=srcu[:], in_=srcx[:]))
        v_n = sV.n
        wait("gpsimd", sV, v_n)
        wait("gpsimd", sG, padded_w)
        for c40 in range(40):
            mark(C.g.indirect_dma_start(out=tokf[:, c40:c40 + 1], out_offset=None,
                                        in_=padded_dram.rearrange("(n a) -> n a", a=1),
                                        in_offset=IOA(srcu[:, c40:c40 + 1], 0)), sG, 16)
        tok_g = sG.n
        wait("vector", sG, tok_g)
        swv(C.v.tensor_scalar_min(tokf[:], tokf[:], 16383.0))
        swv(C.v.tensor_copy(out=toku[:], in_=tokf[:]))
        tok_ready = sV.n

        if debug_stage == "sel":
            wait("sync", sV, tok_ready)
            mark(C.s.dma_start(out=out[0:128, 0:16], in_=xf2v[:, 456:472]), sS, 16)
            mark(C.s.dma_start(out=out[128:256, 0:256], in_=rowm[:]), sS, 16)
            mark(C.s.dma_start(out=out[384:512, 0:40], in_=tokf[:]), sS, 16)
            mark(C.s.dma_start(out=out[512:640, 0:256], in_=keptb[:]), sS, 16)
            mark(C.s.dma_start(out=out[640:768, 0:40], in_=srcx[:]), sS, 16)
            wait("gpsimd", sS, sS.n)
            wait("gpsimd", sG, sG.n)
            return

        phase_barrier()
        wait("gpsimd", sPE, sh1_pe_last)
        for k in range(G):
            mark(C.g.dma_start(out=fc1T_sb[:, k, :], in_=dd["fc1T"][k * 128:(k + 1) * 128, :]), sG, 16)
        fc1_ld = sG.n
        for k in range(16):
            mark(C.g.dma_start(out=fc2T_sb[:, k, :], in_=dd["fc2T"][k * 128:(k + 1) * 128, :]), sG, 16)
        fc2_ld = sG.n
        wait("gpsimd", sV, tok_ready)
        wait("gpsimd", sG, padded_w)
        mark(C.g.dma_start(out=fc2b_rp[:], in_=dd["fc2b_rep"]), sG, 16)
        mark(C.g.dma_start(out=sh2b_rp[:], in_=dd["sh2b_rep"]), sG, 16)
        rep_ld = sG.n

        oe_w = {}
        fc2_pe = {-1: 0}
        for ce in range(CAP // 512):
            wait("gpsimd", cC2a, 1)
            wait("gpsimd", sV, tok_ready)
            for q4 in range(4):
                cc = ce * 4 + q4
                xg = xeg[:, cc % 2, :]
                if ("xeg", cc % 2) in evac:
                    wait("gpsimd", sPE, evac[("xeg", cc % 2)])
                mark(C.g.indirect_dma_start(out=xg, out_offset=None,
                                            in_=cout_xm,
                                            in_offset=IOA(toku[:, cc:cc + 1], 0)), sG, 16)
                g_n = sG.n
                wait("tensor", sG, g_n)
                wait("tensor", sG, rconsts_ld)
                wait("tensor", sV, evac.get(("pb16",), 0))
                mm = None
                for g in range(G):
                    i = C.t.transpose(pb16[:, g * 128:(g + 1) * 128],
                                      xg.rearrange("p (g n) -> p g n", g=G)[:, g, :],
                                      idnb_sb[:])
                    mm = i if i is not None else mm
                pe_n = mark(mm, sPE, 1)
                evac[("xeg", cc % 2)] = pe_n
                wait("vector", sPE, pe_n)
                evac[("pb16",)] = swv(C.v.tensor_copy(
                    out=xe_fm[:, :, q4 * 128:(q4 + 1) * 128],
                    in_=pb16.rearrange("p (g n) -> p g n", g=G)))
            xe_ready = sV.n
            wait("tensor", sG, fc1_ld)
            wait("tensor", sV, xe_ready)
            wait("scalar", sPE, fc2_pe[ce - 1])
            for m in range(16):
                bank = m % 3
                wait("tensor", sA, evac.get(bank, 0))
                mm = None
                for k in range(G):
                    i = C.t.matmul(ps[bank][:], fc1T_sb[:, k, m * 128:(m + 1) * 128],
                                   xe_fm[:, k, :], start=(k == 0), stop=(k == G - 1))
                    mm = i if i is not None else mm
                pe_n = mark(mm, sPE, 1)
                wait("scalar", sPE, pe_n)
                wait("scalar", sG, smallb_ld)
                evac[bank] = mark(C.a.activation(out=he[:, m, :], in_=ps[bank][:],
                                                 func=AF.Gelu, bias=fc1b_sb[:, m:m + 1]), sA, 1)
            he_done = sA.n
            wait("tensor", sG, fc2_ld)
            wait("tensor", sA, he_done)
            for m4 in range(4):
                pwb = pw[m4 % 2]
                if ("pw2", m4 % 2) in evac:
                    wait("tensor", sV, evac[("pw2", m4 % 2)])
                mm = None
                for k in range(16):
                    for nn_ in range(2):
                        i = C.t.matmul(pwb[:, nn_ * 512:(nn_ + 1) * 512],
                                       he[:, k, m4 * 128:(m4 + 1) * 128],
                                       fc2T_sb[:, k, nn_ * 512:(nn_ + 1) * 512],
                                       start=(k == 0), stop=(k == 15))
                        mm = i if i is not None else mm
                pe_n = mark(mm, sPE, 1)
                wait("vector", sPE, pe_n)
                wait("vector", sG, rep_ld)
                if ("oeb", m4 % 2) in oe_w:
                    wait("vector", sS, oe_w[("oeb", m4 % 2)])
                ob = oeb[m4 % 2]
                ev = swv(C.v.tensor_add(ob[:], pwb[:], fc2b_rp[:]))
                evac[("pw2", m4 % 2)] = ev
                wait("sync", sV, ev)
                mark(C.s.dma_start(out=cin_oe[(ce * 4 + m4) * 128:(ce * 4 + m4 + 1) * 128, :],
                                   in_=ob[:]), sS, 16)
                oe_w[("oeb", m4 % 2)] = sS.n
            fc2_pe[ce] = sPE.n
        oe_written = sS.n
        fc_pe_last = sPE.n

        wait("gpsimd", sS, oe_written)
        mark(C.g.collective_compute("AllGather", ALU.bypass, replica_groups=RG,
                                    ins=[cin_oe.opt()], outs=[cout_oe.opt()]), sems["cOe"], 1)

        wait("gpsimd", sPE, fc_pe_last)
        for k in range(32):
            mark(C.g.dma_start(out=sh2T_sb[:, k, :], in_=dd["sh2T"][k * 128:(k + 1) * 128, :]), sG, 16)
        sh2_ld = sG.n

        phase_barrier()
        comb_done = {-1: 0}
        for chunk in range(TC // 128):
            hb2 = hsh2[chunk % 2]
            wait("sync", sPE, evac.get(("hsh2", chunk % 2), 0))
            wait("sync", sS, hsh_written)
            wait("sync", sPE, fc_pe_last)
            mark(C.s.dma_start(out=hb2[:],
                               in_=hsh_sp.rearrange("(k p) t -> p k t", p=128)[:, :, chunk * 128:(chunk + 1) * 128]), sS, 16)
            h2_ld = sS.n
            wait("tensor", sS, h2_ld)
            wait("tensor", sG, sh2_ld)
            pwb = pw[chunk % 2]
            wait("tensor", sV, evac.get(("pw2", chunk % 2), 0))
            if ("pw3", chunk % 2) in evac:
                wait("tensor", sV, evac[("pw3", chunk % 2)])
            mm = None
            for k in range(32):
                for nn_ in range(2):
                    i = C.t.matmul(pwb[:, nn_ * 512:(nn_ + 1) * 512], hb2[:, k, :],
                                   sh2T_sb[:, k, nn_ * 512:(nn_ + 1) * 512],
                                   start=(k == 0), stop=(k == 31))
                    mm = i if i is not None else mm
            pe_n = mark(mm, sPE, 1)
            evac[("hsh2", chunk % 2)] = pe_n

            wait("gpsimd", sems["cSlot"], 1)
            wait("gpsimd", sems["cOe"], 1)
            swv(C.v.tensor_scalar(p4("fo"), cstc("mye"), 4096.0, float(chunk * 256), ALU.mult, ALU.add))
            swv(C.v.tensor_add(p4("fo"), p4("fo"), cstc("iota128")))
            swv(C.v.scalar_tensor_tensor(p4("ixA"), e1a[:, chunk:chunk + 1], 32768.0, p4("fo"), ALU.mult, ALU.add))
            swv(C.v.scalar_tensor_tensor(p4("ixB"), e2a[:, chunk:chunk + 1], 32768.0, p4("fo"), ALU.mult, ALU.add))
            swv(C.v.tensor_scalar(p4("ixB"), p4("ixB"), 128.0, None, ALU.add))
            swv(C.v.tensor_copy(out=ixAu[:], in_=p4("ixA")))
            swv(C.v.tensor_copy(out=ixBu[:], in_=p4("ixB")))
            v_n = sV.n
            wait("gpsimd", sV, v_n)
            mark(C.g.indirect_dma_start(out=p4("svA"), out_offset=None,
                                        in_=cout_slot_flat.rearrange("(n a) -> n a", a=1),
                                        in_offset=IOA(ixAu[:], 0)), sG, 16)
            mark(C.g.indirect_dma_start(out=p4("svB"), out_offset=None,
                                        in_=cout_slot_flat.rearrange("(n a) -> n a", a=1),
                                        in_offset=IOA(ixBu[:], 0)), sG, 16)
            g_n = sG.n
            wait("vector", sG, g_n)
            swv(C.v.tensor_scalar(p4("kA"), p4("svA"), SENT_ROW, None, ALU.is_lt))
            swv(C.v.tensor_scalar(p4("kB"), p4("svB"), SENT_ROW, None, ALU.is_lt))
            swv(C.v.tensor_scalar_min(p4("svA"), p4("svA"), SENT_ROW - 1.0))
            swv(C.v.tensor_scalar_min(p4("svB"), p4("svB"), SENT_ROW - 1.0))
            swv(C.v.tensor_copy(out=rAu[:], in_=p4("svA")))
            swv(C.v.tensor_copy(out=rBu[:], in_=p4("svB")))
            swv(C.v.tensor_mul(p4("wAe"), w1a[:, chunk:chunk + 1], p4("kA")))
            swv(C.v.tensor_mul(p4("wBe"), w2a[:, chunk:chunk + 1], p4("kB")))
            v_n = sV.n
            wait("gpsimd", sV, v_n)
            wait("gpsimd", sV, comb_done[chunk - 1])
            Arow = h_t.rearrange("p a n -> p (a n)")
            Brow = p_t.rearrange("p a n -> p (a n)")
            mark(C.g.indirect_dma_start(out=Arow, out_offset=None, in_=cout_oe,
                                        in_offset=IOA(rAu[:], 0)), sG, 16)
            mark(C.g.indirect_dma_start(out=Brow, out_offset=None, in_=cout_oe,
                                        in_offset=IOA(rBu[:], 0)), sG, 16)
            gAB = sG.n
            wait("vector", sPE, pe_n)
            wait("vector", sG, rep_ld)
            wait("vector", sS, oe_written)
            shb = sht[chunk % 2]
            evac[("pw3", chunk % 2)] = swv(C.v.tensor_add(shb[:], pwb[:], sh2b_rp[:]))
            wait("vector", sG, gAB)
            C.v.scalar_tensor_tensor(r1_tok[:], Arow, p4("wAe"), xm_tok[:, chunk, :],
                                     ALU.mult, ALU.add)
            swv(C.v.scalar_tensor_tensor(r1_tok[:], Brow, p4("wBe"), r1_tok[:],
                                         ALU.mult, ALU.add))
            mark(C.v.tensor_add(r1_tok[:], r1_tok[:], shb[:]), sV, 1)
            comb_done[chunk] = sV.n
            r2_n = sV.n

            wait("scalar", sV, r2_n)
            mark(C.a.activation(out=lnsq[:], in_=r1_tok[:], func=AF.Copy,
                                accum_out=stat[:, 0:1]), sA, 1)
            a_n = mark(C.a.activation(out=lnsq[:], in_=r1_tok[:], func=AF.Square,
                                      accum_out=stat[:, 1:2]), sA, 1)
            wait("vector", sA, a_n)
            swv(C.v.tensor_scalar_mul(stat[:, 2:3], stat[:, 0:1], 1.0 / D))
            swv(C.v.tensor_scalar_mul(stat[:, 3:4], stat[:, 1:2], 1.0 / D))
            swv(C.v.tensor_mul(stat[:, 4:5], stat[:, 2:3], stat[:, 2:3]))
            swv(C.v.tensor_sub(stat[:, 3:4], stat[:, 3:4], stat[:, 4:5]))
            mark(C.v.tensor_scalar(stat[:, 3:4], stat[:, 3:4], EPS, None, ALU.add), sV, 1)
            var_n = sV.n
            wait("scalar", sV, var_n)
            sd_n = mark(C.a.activation(out=stat[:, 5:6], in_=stat[:, 3:4], func=AF.Sqrt), sA, 1)
            wait("vector", sA, sd_n)
            swv(C.v.reciprocal(stat[:, 6:7], stat[:, 5:6]))
            if chunk > 0:
                wait("vector", sS, outw)
            mark(C.v.tensor_scalar(otok[:], r1_tok[:], stat[:, 2:3], stat[:, 6:7],
                                   ALU.subtract, ALU.mult), sV, 1)
            o_n = sV.n
            wait("sync", sV, o_n)
            mark(C.s.dma_start(out=out[chunk * 128:(chunk + 1) * 128, :], in_=otok[:]), sS, 16)
            outw = sS.n

        if debug_stage == "fulldbg":
            wait("sync", sV, sV.n)
            wait("sync", sPE, sPE.n)
            mark(C.s.dma_start(out=dbg_xe, in_=AR_X[:, 0:6144]), sS, 16)
            mark(C.s.dma_start(out=dbg_he, in_=AR_X[:, 6144:14336]), sS, 16)
            mark(C.s.dma_start(out=dbg_xall, in_=cout_xm), sS, 16)
            mark(C.s.dma_start(out=dbg_tok, in_=tokf[:]), sS, 16)
            mark(C.s.dma_start(out=dbg_pair, in_=cout_pair), sS, 16)
            mark(C.s.dma_start(out=dbg_slot, in_=cout_slot), sS, 16)
            mark(C.s.dma_start(out=dbg_oe, in_=cin_oe), sS, 16)
            mark(C.s.dma_start(out=dbg_xm, in_=cin_xm), sS, 16)
            mark(C.s.dma_start(out=dbg_hsh, in_=hsh_sp[:, 0:128]), sS, 16)

        wait("gpsimd", sS, sS.n)
        wait("gpsimd", sG, sG.n)
        return

    def walk(name):
        C.cur = name
        C.reset()
        program()

    with nc.Block() as block:
        block.sync(lambda e: walk("sync"))
        block.gpsimd(lambda e: walk("gpsimd"))
        block.vector(lambda e: walk("vector"))
        block.scalar(lambda e: walk("scalar"))
        block.tensor(lambda e: walk("tensor"))
    ctx.close()
    return nc


def kernel(**inputs):
    maps = _prep_inputs(inputs)
    nc = _build()
    res = run_bass_kernel_spmd(nc, maps, core_ids=list(range(NCORE)))
    outs = [res.results[c]["out"] for c in range(NCORE)]
    return np.stack(outs).reshape(B, T, D).astype(np.float32)


# version 07
# speedup vs baseline: 1.1934x; 1.1934x over previous
import contextlib

import numpy as np
import ml_dtypes
import concourse.bass as bass
import concourse.mybir as mybir
from concourse.bass_utils import run_bass_kernel_spmd
from bass_rust import IndirectOffsetOnAxis as IOA

F32 = mybir.dt.float32
BF16 = mybir.dt.bfloat16
U32 = mybir.dt.uint32
AF = mybir.ActivationFunctionType
ALU = mybir.AluOpType
AX = mybir.AxisListType

NCORE = 8
B, T, D = 2, 8192, 1024
TC = 2048
ST = 512
NTILE = TC // ST
G = 8
E, CAP, EH, SHD = 8, 5120, 2048, 4096
NTOK = B * T
EPS = 1e-5
RG = [list(range(NCORE))]
SENT_ROW = float(E * CAP)

ENGINES = ("sync", "gpsimd", "vector", "scalar", "tensor")


class Sem:
    __slots__ = ("h", "n")

    def __init__(self, h):
        self.h = h
        self.n = 0


class EngineProxy:
    def __init__(self, ctx, name):
        self._ctx = ctx
        self._name = name

    def __getattr__(self, op):
        ctx, name = self._ctx, self._name

        def f(*a, **kw):
            if ctx.cur == name:
                return getattr(ctx.eng[name], op)(*a, **kw)
            return None

        return f


class Ctx:
    def __init__(self, nc):
        self.nc = nc
        self.eng = {
            "sync": nc.sync, "gpsimd": nc.gpsimd, "vector": nc.vector,
            "scalar": nc.scalar, "tensor": nc.tensor,
        }
        self.cur = None
        self.s = EngineProxy(self, "sync")
        self.g = EngineProxy(self, "gpsimd")
        self.v = EngineProxy(self, "vector")
        self.a = EngineProxy(self, "scalar")
        self.t = EngineProxy(self, "tensor")
        self.sems = []

    def sem(self, handle):
        s = Sem(handle)
        self.sems.append(s)
        return s

    def reset(self):
        for s in self.sems:
            s.n = 0

    @staticmethod
    def mark(instr, sem, k=1):
        sem.n += k
        if instr is not None:
            instr.then_inc(sem.h, k)
        return sem.n

    def wait(self, eng_name, sem, val=None):
        if self.cur == eng_name:
            self.eng[eng_name].wait_ge(sem.h, sem.n if val is None else val)


def _chan2d(v):
    return np.ascontiguousarray(np.asarray(v).reshape(-1, 128).T)


def _prep_inputs(inp):
    x = np.asarray(inp["L0_attn_input_bxtxd"])
    maps = []
    shared = {
        "w_x": np.ascontiguousarray(inp["L0_attn_w_x"]),
        "gi_w": np.ascontiguousarray(inp["L0_attn_gi_w"]),
        "ga_w": np.ascontiguousarray(inp["L0_attn_ga_w"]),
        "w_y": np.ascontiguousarray(inp["L0_attn_w_y"]),
        "bx2d": _chan2d(inp["L0_attn_b_x"]),
        "by2d": _chan2d(inp["L0_attn_b_y"]),
        "cb2d": _chan2d(inp["L0_attn_conv_b"]),
        "gib2d": _chan2d(inp["L0_attn_gi_b"]),
        "gab2d": _chan2d(inp["L0_attn_ga_b"]),
        "ap2d": _chan2d(inp["L0_attn_a_param"]),
        "cw2d": np.ascontiguousarray(
            np.asarray(inp["L0_attn_conv_w"]).reshape(4, G, 128)
            .transpose(2, 1, 0).reshape(128, G * 4)),
        "projT": np.ascontiguousarray(np.asarray(inp["L0_ffn_proj_w"]).T),
        "projb_rep": np.ascontiguousarray(
            np.tile(np.asarray(inp["L0_ffn_proj_b"])[None, :], (128, 1))),
        "ebias_rep": np.ascontiguousarray(
            np.tile(np.asarray(inp["L0_ffn_expert_bias"])[None, :], (128, 1))),
        "sh1T": np.ascontiguousarray(np.asarray(inp["L0_ffn_sh1_w"]).T),
        "sh1b2d": _chan2d(inp["L0_ffn_sh1_b"]),
        "sh2T": np.ascontiguousarray(np.asarray(inp["L0_ffn_sh2_w"]).T),
        "sh2b_rep": np.ascontiguousarray(
            np.tile(np.asarray(inp["L0_ffn_sh2_b"])[None, :], (128, 1))),
        "idn": np.eye(128, dtype=np.float32),
        "idnb": np.eye(128, dtype=ml_dtypes.bfloat16),
        "iota8": np.tile(np.arange(8, dtype=np.float32)[None, :], (128, 1)),
        "iota8p": np.tile(np.arange(8, dtype=np.float32)[None, :] + 1000.0, (128, 1)),
        "iota_ref": (np.arange(128)[:, None] * 256
                     + np.concatenate([2 * np.arange(128), 2 * np.arange(128) + 1])[None, :]
                     ).astype(np.float32),
        "iota_j": np.tile(np.arange(256, dtype=np.float32)[None, :], (128, 1)),
        "iota_slot": (np.arange(40)[None, :] * 128
                      + np.arange(128)[:, None]).astype(np.float32),
        "iota16x256": np.tile((np.arange(16, dtype=np.float32) * 256)[None, :], (128, 1)),
    }
    fc1 = np.asarray(inp["L0_ffn_fc1_w"])
    fc1b = np.asarray(inp["L0_ffn_fc1_b"])
    fc2 = np.asarray(inp["L0_ffn_fc2_w"])
    fc2b = np.asarray(inp["L0_ffn_fc2_b"])

    for c in range(NCORE):
        b, k = c // 4, c % 4
        t0 = k * TC
        xs = np.zeros((TC + 3, D), np.float32)
        lo = max(t0 - 3, 0)
        xs[3 - (t0 - lo):] = x[b, lo:t0 + TC]
        hv = 0.0 if k == 0 else 1.0
        msk = np.zeros((128, 3), np.float32)
        msk[:, :k] = 1.0
        consts = np.concatenate([
            np.full((128, 1), hv, np.float32),
            np.full((128, 1), float(b), np.float32),
            msk,
            1.0 - msk,
            np.full((128, 1), float(c), np.float32),
            np.full((128, 1), float(c * CAP), np.float32),
            np.arange(128, dtype=np.float32).reshape(128, 1),
            np.full((128, 1), 1.0 - hv, np.float32),
        ], axis=1).astype(np.float32)
        m = dict(shared)
        m.update({
            "xT": np.ascontiguousarray(xs.T),
            "x_tok": np.ascontiguousarray(x[b, t0:t0 + TC]),
            "fc1T": np.ascontiguousarray(fc1[c].T),
            "fc1b2d": _chan2d(fc1b[c]),
            "fc2T": np.ascontiguousarray(fc2[c].T),
            "fc2b_rep": np.ascontiguousarray(np.tile(fc2b[c][None, :], (128, 1))),
            "consts": consts,
        })
        maps.append(m)
    return maps


def _build(debug_stage=None):
    nc = bass.Bass("TRN2", target_bir_lowering=False, debug=False)
    C = Ctx(nc)

    dd = {}

    def din(name, shape, dt=F32):
        dd[name] = nc.dram_tensor(name, shape, dt, kind="ExternalInput").ap()

    din("xT", [D, TC + 3]); din("x_tok", [TC, D])
    for n in ("w_x", "gi_w", "ga_w", "w_y"):
        din(n, [D, D])
    for n in ("bx2d", "by2d", "cb2d", "gib2d", "gab2d", "ap2d"):
        din(n, [128, G])
    din("cw2d", [128, G * 4])
    din("projT", [D, E]); din("projb_rep", [128, E]); din("ebias_rep", [128, E])
    din("sh1T", [D, SHD]); din("sh1b2d", [128, SHD // 128])
    din("sh2T", [SHD, D]); din("sh2b_rep", [128, D])
    din("fc1T", [D, EH]); din("fc1b2d", [128, EH // 128])
    din("fc2T", [EH, D]); din("fc2b_rep", [128, D])
    din("idn", [128, 128])
    din("idnb", [128, 128], BF16)
    din("iota8", [128, E]); din("iota8p", [128, E])
    din("iota_ref", [128, 256]); din("iota_j", [128, 256]); din("iota_slot", [128, 40])
    din("iota16x256", [128, 16])
    din("consts", [128, 12])

    out = nc.dram_tensor("out", [TC, D], F32, kind="ExternalOutput").ap()
    if debug_stage == "fulldbg":
        dbg_xe = nc.dram_tensor("dbg_xe", [128, 6144], BF16, kind="ExternalOutput").ap()
        dbg_he = nc.dram_tensor("dbg_he", [128, 8192], BF16, kind="ExternalOutput").ap()
        dbg_xall = nc.dram_tensor("dbg_xall", [NTOK, D], BF16, kind="ExternalOutput").ap()
        dbg_tok = nc.dram_tensor("dbg_tok", [128, 40], F32, kind="ExternalOutput").ap()
        dbg_pair = nc.dram_tensor("dbg_pair", [4 * NCORE, TC], F32, kind="ExternalOutput").ap()
        dbg_sel2 = nc.dram_tensor("dbg_sel2", [128, 2200], F32, kind="ExternalOutput").ap()
        dbg_slot = nc.dram_tensor("dbg_slot", [NCORE, 128 * 256], F32, kind="ExternalOutput").ap()
        dbg_oe = nc.dram_tensor("dbg_oe", [CAP, D], F32, kind="ExternalOutput").ap()
        dbg_xm = nc.dram_tensor("dbg_xm", [TC, D], BF16, kind="ExternalOutput").ap()
        dbg_hsh = nc.dram_tensor("dbg_hsh", [SHD, 128], BF16, kind="ExternalOutput").ap()

    h_sp = nc.dram_tensor("h_sp", [G, 128, TC], F32).ap()
    p_sp = nc.dram_tensor("p_sp", [G, 128, TC], F32).ap()
    cin_carry = nc.dram_tensor("cin_carry", [2, D], F32).ap()
    cout_carry = nc.dram_tensor("cout_carry", [2 * NCORE, D], F32).ap()
    cin_xm = nc.dram_tensor("cin_xm", [TC, D], BF16).ap()
    cout_xm = nc.dram_tensor("cout_xm", [NTOK, D], BF16).ap()
    cin_pair = nc.dram_tensor("cin_pair", [4, TC], F32).ap()
    cout_pair = nc.dram_tensor("cout_pair", [4 * NCORE, TC], F32).ap()
    cnt_dram = nc.dram_tensor("cnt_dram", [128], F32).ap()
    base_dram = nc.dram_tensor("base_dram", [130], F32).ap()
    padded_dram = nc.dram_tensor("padded_dram", [128 * 256], F32).ap()
    cin_slot = nc.dram_tensor("cin_slot", [1, 128 * 256], F32).ap()
    cout_slot = nc.dram_tensor("cout_slot", [NCORE, 128 * 256], F32).ap()
    cin_oe = nc.dram_tensor("cin_oe", [CAP, D], F32).ap()
    cout_oe = nc.dram_tensor("cout_oe", [NCORE * CAP, D], F32).ap()
    hsh_sp = nc.dram_tensor("hsh_sp", [SHD, TC], BF16).ap()

    ctx = contextlib.ExitStack()

    def sbuf(name, shape, dt=F32):
        return ctx.enter_context(nc.sbuf_tensor(name, shape, dt))

    def psum(name, shape, dt=F32):
        return ctx.enter_context(nc.psum_tensor(name, shape, dt))

    AR_W = sbuf("AR_W", [128, 32768], BF16)
    AR_A = sbuf("AR_A", [128, 16384], F32)
    AR_X = sbuf("AR_X", [128, 16384], BF16)
    AR_B = sbuf("AR_B", [128, 4120], F32)

    def vw(ar, lo, n, g=None):
        a = ar[:, lo:lo + n]
        if g is not None:
            a = a.rearrange("p (g n) -> p g n", g=g)
        return a

    wx_sb = vw(AR_W, 0, 8192, G)
    wgi_sb = vw(AR_W, 8192, 8192, G)
    wga_sb = vw(AR_W, 16384, 8192, G)
    gx = vw(AR_A, 0, 4096, G)
    aamu = vw(AR_A, 4096, 4096, G)
    av = vw(AR_A, 8192, 4096, G)
    x2f = vw(AR_A, 12288, 4096, G)
    xm_tok = AR_A[:, :].rearrange("p (c n) -> p c n", c=TC // 128)
    x_fmb = vw(AR_X, 0, 16384, G)
    x1e = AR_B[:, 0:4120].rearrange("p (g n) -> p g n", g=G)
    y_fm = vw(AR_B, 0, 4096, G)

    halob = sbuf("halob", [128, G, 3], BF16)
    x1h3 = sbuf("x1h3", [128, G, 3], F32)
    x2b = sbuf("x2b", [128, G, ST], BF16)
    h_t = sbuf("h_t", [128, 2, ST], F32)
    p_t = sbuf("p_t", [128, 2, ST], F32)
    hc = sbuf("hc", [128, G], F32)
    pc = sbuf("pc", [128, G], F32)
    zeros512 = sbuf("zeros512", [128, ST], F32)
    car = sbuf("car", [128, 2 * NCORE, G], F32)
    h_in = sbuf("h_in", [128, G], F32)
    Hb = [sbuf(f"Hb_{i}", [128, G], F32) for i in range(2)]
    tmp8a = sbuf("tmp8a", [128, G], F32)
    tmp8b = sbuf("tmp8b", [128, G], F32)
    cexp = sbuf("cexp", [128, G], F32)
    idn_sb = sbuf("idn_sb", [128, 128], F32)
    bx_sb = sbuf("bx_sb", [128, G], F32)
    by_sb = sbuf("by_sb", [128, G], F32)
    cb_sb = sbuf("cb_sb", [128, G], F32)
    gib_sb = sbuf("gib_sb", [128, G], F32)
    gab_sb = sbuf("gab_sb", [128, G], F32)
    ap_sb = sbuf("ap_sb", [128, G], F32)
    cw_sb = sbuf("cw_sb", [128, G * 4], F32)
    cst = sbuf("cst", [128, 12], F32)
    xf2 = sbuf("xf2", [128, 2, ST], F32)
    xm_bf = sbuf("xm_bf", [128, D], BF16)
    stat = sbuf("stat", [128, 8], F32)
    x2b_f32 = x2b.rearrange("p g n -> p (g n)").bitcast(F32)
    lnsq = x2b_f32[:, 0:D]
    r1_tok = x2b_f32[:, D:2 * D]

    projT_sb = sbuf("projT_sb", [128, G, 8], BF16)
    pjb_sb = sbuf("pjb_sb", [128, 8], F32)
    eb_sb = sbuf("eb_sb", [128, 8], F32)
    io8_sb = sbuf("io8_sb", [128, 8], F32)
    io8p_sb = sbuf("io8p_sb", [128, 8], F32)
    idnb_sb = sbuf("idnb_sb", [128, 128], BF16)
    e1a = sbuf("e1a", [128, 16], F32)
    e2a = sbuf("e2a", [128, 16], F32)
    w1a = sbuf("w1a", [128, 16], F32)
    w2a = sbuf("w2a", [128, 16], F32)
    zall = AR_B[:, 0:128].rearrange("p (a b) -> p a b", b=8)
    sgall = AR_B[:, 128:256].rearrange("p (a b) -> p a b", b=8)
    sgal2 = AR_B[:, 256:384].rearrange("p (a b) -> p a b", b=8)
    zt = AR_B[:, 384:448].rearrange("p (a b) -> p a b", b=4)
    m1a = sbuf("m1a", [128, 16], F32)
    m2a = sbuf("m2a", [128, 16], F32)
    sel4m = sbuf("sel4m", [128, 4], F32)
    cnt4 = sbuf("cnt4", [128, 4], F32)
    tot4 = sbuf("tot4", [128, 4], F32)
    ge4 = sbuf("ge4", [128, 4], F32)
    sva_all = sbuf("sva_all", [128, 16], F32)
    svb_all = sbuf("svb_all", [128, 16], F32)
    rAu_all = sbuf("rAu_all", [128, 16], U32)
    rBu_all = sbuf("rBu_all", [128, 16], U32)
    ixu_all = sbuf("ixu_all", [128, 32], U32)
    kAB = sbuf("kAB", [128, 32], F32)
    wABe = sbuf("wABe", [128, 32], F32)
    eA = eB = wAt = wBt = None
    fc1b_sb = sbuf("fc1b_sb", [128, 16], F32)
    sh1b_sb = sbuf("sh1b_sb", [128, 32], F32)
    hshb = sbuf("hshb", [128, 2, 512], BF16)

    arB = AR_B
    def bview(lo, n, g=None):
        a = arB[:, lo:lo + n]
        if g is not None:
            a = a.rearrange("p (g n) -> p g n", g=g)
        return a
    wm = bview(0, 256)
    eA = bview(768, 128)
    eB = bview(896, 128)
    wAt = bview(1024, 128)
    wBt = bview(1152, 128)
    io_ref = bview(256, 256)
    io_j = bview(512, 256)
    pos = bview(768, 256)
    rexc = bview(1024, 256)
    arr_ = bview(1280, 256)
    delt = bview(1536, 256)
    dshs = bview(1792, 256)
    selb = bview(2048, 256)
    bitu = sbuf("bitu_t", [128, 256], U32)
    mskio = bview(2304, 256)
    tokv = bview(2560, 256)
    eqm = bview(2816, 256)
    keptb = bview(3072, 256)
    rowm = bview(3328, 256)
    slotm = bview(3584, 256)
    fc2b_rp = bview(0, 1024)
    sh2b_rp = bview(1024, 1024)
    oeb = [bview(2048, 1024), bview(3072, 1024)]
    sht = [bview(0, 1024), bview(2048, 1024)]
    otok = bview(3072, 1024)

    xf2v = xf2.rearrange("p a n -> p (a n)")
    cntb = xf2v[:, 0:128]
    pr16 = xf2v[:, 0:128]
    base_bc = xf2v[:, 128:256]
    srcx = xf2v[:, 256:296]
    tokf = xf2v[:, 296:336]
    srcu = sbuf("srcu", [128, 40], U32)
    toku = sbuf("toku", [128, 40], U32)
    io_slot = xf2v[:, 416:456]
    io16x = xf2v[:, 748:764]
    _sc = {}
    for _i, _n in enumerate(("lo", "hi", "mid", "ge", "neq", "asg", "ov", "theta",
                             "cntgt", "m_p", "t1", "t2", "t3", "cnt1", "ctot", "bcol")):
        _sc[_n] = xf2v[:, 456 + _i:457 + _i]
    def sv(n):
        return _sc[n]
    mrow = xf2v[0:1, 472:600]
    brow = xf2v[0:1, 600:728]
    _p4 = {}
    for _i, _n in enumerate(("fo", "ixA", "ixB", "svA", "svB", "kA", "kB",
                             "wAe", "wBe")):
        _p4[_n] = xf2v[:, 728 + _i:729 + _i]
    def p4(n):
        return _p4[n]
    idxu4 = sbuf("idxu4", [128, 4], U32)
    ixAu = idxu4[:, 0:1]
    ixBu = idxu4[:, 1:2]
    rAu = idxu4[:, 2:3]
    rBu = idxu4[:, 3:4]
    sh1T_sb = AR_W[:, 0:32768].rearrange("p (g n) -> p g n", g=G)
    fc1T_sb = AR_W[:, 0:16384].rearrange("p (g n) -> p g n", g=G)
    fc2T_sb = AR_W[:, 16384:32768].rearrange("p (g n) -> p g n", g=16)
    sh2T_sb = AR_W[:, 0:32768].rearrange("p (g n) -> p g n", g=32)
    xeg = AR_X[:, 0:2048].rearrange("p (a n) -> p a n", a=2)
    xe_fm = AR_X[:, 2048:6144].rearrange("p (g n) -> p g n", g=G)
    he = AR_X[:, 6144:14336].rearrange("p (g n) -> p g n", g=16)
    hsh2 = [AR_X[:, 0:4096].rearrange("p (g n) -> p g n", g=32),
            AR_X[:, 4096:8192].rearrange("p (g n) -> p g n", g=32)]
    cout_slot_flat = cout_slot.rearrange("r n -> (r n)")

    ps = [psum(f"ps{i}", [128, ST], F32) for i in range(3)]
    pw = [psum(f"pw{i}", [128, D], F32) for i in range(2)]
    pb16 = psum("pb16", [128, D], BF16)

    sems = {}
    for name in ("sS", "sG", "sV", "sA", "sPE", "cC1", "cC2a", "cC2b",
                 "cSlot", "cOe"):
        sems[name] = C.sem(ctx.enter_context(nc.semaphore(name)))
    sS, sG, sV, sA, sPE = (sems[k] for k in ("sS", "sG", "sV", "sA", "sPE"))
    cC1 = sems["cC1"]
    cC2a = sems["cC2a"]
    cC2b = sems["cC2b"]

    CID = {"hv": 0, "mb": 1, "msk": 2, "msk1m": 5, "mye": 8, "ecap": 9,
           "iota128": 10, "hv1m": 11}

    def cstc(name, off=0):
        i = CID[name] + off
        return cst[:, i:i + 1]

    xT_r = dd["xT"].rearrange("(g p) t -> p g t", p=128)

    def program():
        mark, wait = C.mark, C.wait
        evac = {}

        def swv(i):
            n = mark(i, sV, 1)
            wait("vector", sV, n)
            return n

        _cur_scope = [None]

        def _scope(name):
            if _cur_scope[0] is not None:
                pn, pid = _cur_scope[0]
                C.nc.leave_named_scope(pn, pid, False)
            sid, _ = C.nc.enter_named_scope(name, False)
            _cur_scope[0] = (name, sid)

        def phase_barrier():
            for e in ENGINES:
                for sem in (sS, sG, sV, sA, sPE):
                    wait(e, sem, sem.n)

        _scope("setup")
        for t_, d_ in [(idn_sb, dd["idn"]), (bx_sb, dd["bx2d"]), (by_sb, dd["by2d"]),
                       (cb_sb, dd["cb2d"]), (gib_sb, dd["gib2d"]), (gab_sb, dd["gab2d"]),
                       (ap_sb, dd["ap2d"]), (cw_sb, dd["cw2d"]), (cst, dd["consts"])]:
            mark(C.s.dma_start(out=t_[:], in_=d_), sS, 16)
        consts_ld = sS.n

        for wsb, wd in [(wx_sb, dd["w_x"]), (wgi_sb, dd["gi_w"]), (wga_sb, dd["ga_w"])]:
            for k in range(G):
                mark(C.g.dma_start(out=wsb[:, k, :], in_=wd[k * 128:(k + 1) * 128, :]), sG, 16)
        attn_w_ld = sG.n

        mark(C.g.dma_start(out=x_fmb[:, :, 0:ST], in_=xT_r[:, :, 3:3 + ST]), sG, 16)
        x_ld = {0: sG.n}
        mark(C.g.dma_start(out=halob[:], in_=xT_r[:, :, 0:3]), sG, 16)
        halo_ld = sG.n

        wait("scalar", sS, consts_ld)
        wait("vector", sS, consts_ld)
        a_n = mark(C.a.activation(out=tmp8a[:], in_=ap_sb[:], func=AF.Abs), sA, 1)
        wait("scalar", sA, a_n)
        a_n = mark(C.a.activation(out=tmp8b[:], in_=tmp8a[:], func=AF.Exp, scale=-1.0), sA, 1)
        wait("scalar", sA, a_n)
        mark(C.a.activation(out=tmp8a[:], in_=tmp8b[:], func=AF.Ln, bias=1.0), sA, 1)
        ln_done = sA.n
        swv(C.v.tensor_scalar_max(cexp[:], ap_sb[:], 0.0))
        wait("vector", sA, ln_done)
        swv(C.v.tensor_add(cexp[:], cexp[:], tmp8a[:]))
        swv(C.v.tensor_scalar_mul(cexp[:], cexp[:], -8.0))
        C.v.memset(zeros512[:], 0.0)
        C.v.memset(hc[:], 0.0)
        mark(C.v.memset(pc[:], 1.0), sV, 1)
        derived = sV.n

        _scope("attn1")
        scans_done = {-1: 0}
        gates_pe = {-1: 0}
        spill = {}
        last_x1_pe = 0
        for it in range(NTILE):
            if it + 1 < NTILE:
                mark(C.g.dma_start(out=x_fmb[:, :, (it + 1) * ST:(it + 2) * ST],
                                   in_=xT_r[:, :, 3 + (it + 1) * ST:3 + (it + 2) * ST]), sG, 16)
                x_ld[it + 1] = sG.n

            wait("scalar", sV, scans_done[it - 1])
            if it == 0:
                wait("scalar", sV, derived)

            if it == 0:
                wait("tensor", sG, attn_w_ld)
                wait("tensor", sG, halo_ld)
                for m in range(G):
                    mm = None
                    for k in range(G):
                        i = C.t.matmul(ps[0][:, 0:3], wx_sb[:, k, m * 128:(m + 1) * 128],
                                       halob[:, k, :], start=(k == 0), stop=(k == G - 1))
                        mm = i if i is not None else mm
                    pe_n = mark(mm, sPE, 1)
                    wait("scalar", sPE, pe_n)
                    a_n = mark(C.a.activation(out=x1e[:, m, 0:3], in_=ps[0][:, 0:3],
                                              func=AF.Identity, bias=bx_sb[:, m:m + 1]), sA, 1)
                    wait("tensor", sA, a_n)
                wait("vector", sA, sA.n)
                swv(C.v.tensor_scalar_mul(x1e[:, :, 0:3], x1e[:, :, 0:3], cstc("hv")))
                evac[0] = sA.n
            else:
                swv(C.v.tensor_copy(out=x1e[:, :, 0:3], in_=x1h3[:]))

            wait("tensor", sG, x_ld[it])
            wait("tensor", sG, attn_w_ld)
            wait("scalar", sV, scans_done[it - 1])
            for m in range(G):
                bank = m % 3
                if bank in evac:
                    wait("tensor", sA, evac[bank])
                mm = None
                for k in range(G):
                    i = C.t.matmul(ps[bank][:], wx_sb[:, k, m * 128:(m + 1) * 128],
                                   x_fmb[:, k, it * ST:(it + 1) * ST],
                                   start=(k == 0), stop=(k == G - 1))
                    mm = i if i is not None else mm
                pe_n = mark(mm, sPE, 1)
                wait("scalar", sPE, pe_n)
                evac[bank] = mark(C.a.activation(out=x1e[:, m, 3:3 + ST], in_=ps[bank][:],
                                                 func=AF.Identity, bias=bx_sb[:, m:m + 1]), sA, 1)
            x1_done = sA.n
            last_x1_pe = sPE.n

            wait("vector", sA, x1_done)
            for g in range(G):
                C.v.tensor_scalar(x2f[:, g], x1e[:, g, 3:3 + ST],
                                  cw_sb[:, g * 4 + 3:g * 4 + 4], cb_sb[:, g:g + 1],
                                  ALU.mult, ALU.add)
                for i3 in range(3):
                    C.v.scalar_tensor_tensor(x2f[:, g], x1e[:, g, i3:i3 + ST],
                                             cw_sb[:, g * 4 + i3:g * 4 + i3 + 1],
                                             x2f[:, g], ALU.mult, ALU.add)
            C.v.tensor_copy(out=x1h3[:], in_=x1e[:, :, ST:ST + 3])
            wait("vector", sPE, gates_pe[it - 1])
            mark(C.v.tensor_copy(out=x2b[:], in_=x2f[:]), sV, 1)
            x2_ready = sV.n

            wait("tensor", sV, x2_ready)
            for wsb, dstbuf, bias in ((wgi_sb, gx, gib_sb), (wga_sb, aamu, gab_sb)):
                for m in range(G):
                    bank = m % 3
                    wait("tensor", sA, evac[bank])
                    mm = None
                    for k in range(G):
                        i = C.t.matmul(ps[bank][:], wsb[:, k, m * 128:(m + 1) * 128],
                                       x2b[:, k, :], start=(k == 0), stop=(k == G - 1))
                        mm = i if i is not None else mm
                    pe_n = mark(mm, sPE, 1)
                    wait("scalar", sPE, pe_n)
                    evac[bank] = mark(C.a.activation(out=dstbuf[:, m], in_=ps[bank][:],
                                                     func=AF.Sigmoid, bias=bias[:, m:m + 1]), sA, 1)
            gates_pe[it] = sPE.n

            wait("scalar", sA, sA.n)
            laste = None
            for g in range(G):
                i = C.a.activation(out=av[:, g], in_=aamu[:, g], func=AF.Exp,
                                   scale=cexp[:, g:g + 1])
                laste = i if i is not None else laste
            a_n = mark(laste, sA, 1)
            wait("scalar", sA, a_n)
            mark(C.a.activation(out=aamu[:], in_=av[:], func=AF.Square), sA, 1)
            sq_done = sA.n
            wait("vector", sA, sq_done)
            C.v.tensor_scalar(aamu[:], aamu[:], -1.0, 1.0, ALU.mult, ALU.add)
            mark(C.v.tensor_scalar_max(aamu[:], aamu[:], 0.0), sV, 1)
            mu_pre = sV.n
            wait("scalar", sV, mu_pre)
            mark(C.a.activation(out=aamu[:], in_=aamu[:], func=AF.Sqrt), sA, 1)
            mu_done = sA.n

            wait("vector", sA, mu_done)
            if it == 0:
                swv(C.v.tensor_scalar_mul(av[:, :, 0:1], av[:, :, 0:1], cstc("hv")))
                swv(C.v.tensor_scalar(aamu[:, :, 0:1], aamu[:, :, 0:1], cstc("hv"),
                                      cstc("hv1m"), ALU.mult, ALU.add))

            if debug_stage == "p1t0" and it == 0:
                mark(C.v.nop(), sV, 1)
                wait("sync", sV, sV.n)
                wait("sync", sA, sA.n)
                for g in range(G):
                    mark(C.s.dma_start(out=out[g * 128:(g + 1) * 128, 0:512], in_=x1e[:, g, 3:3 + ST]), sS, 16)
                    mark(C.s.dma_start(out=out[g * 128:(g + 1) * 128, 512:1024], in_=gx[:, g]), sS, 16)
                    mark(C.s.dma_start(out=out[1024 + g * 128:1024 + (g + 1) * 128, 0:512], in_=av[:, g]), sS, 16)
                    mark(C.s.dma_start(out=out[1024 + g * 128:1024 + (g + 1) * 128, 512:1024], in_=aamu[:, g]), sS, 16)
                wait("gpsimd", sS, sS.n)
                wait("gpsimd", sG, sG.n)
                return
            C.v.tensor_mul(x2f[:], gx[:], x2f[:])
            C.v.tensor_mul(x2f[:], x2f[:], aamu[:])

            for g in range(G):
                gi = (it * G + g) % 2
                if (("sp", gi) in spill):
                    wait("vector", sS, spill[("sp", gi)])
                C.v.tensor_tensor_scan(h_t[:, gi], av[:, g], x2f[:, g],
                                       hc[:, g:g + 1], ALU.mult, ALU.add)
                swv(C.v.tensor_copy(out=hc[:, g:g + 1], in_=h_t[:, gi, ST - 1:ST]))
                C.v.tensor_tensor_scan(p_t[:, gi], av[:, g], zeros512[:],
                                       pc[:, g:g + 1], ALU.mult, ALU.add)
                v_n = swv(C.v.tensor_copy(out=pc[:, g:g + 1], in_=p_t[:, gi, ST - 1:ST]))
                wait("sync", sV, v_n)
                mark(C.s.dma_start(out=h_sp[g, :, it * ST:(it + 1) * ST], in_=h_t[:, gi]), sS, 16)
                mark(C.s.dma_start(out=p_sp[g, :, it * ST:(it + 1) * ST], in_=p_t[:, gi]), sS, 16)
                spill[("sp", gi)] = sS.n
            scans_done[it] = sV.n

        wait("gpsimd", sPE, last_x1_pe)
        for k in range(G):
            mark(C.g.dma_start(out=wx_sb[:, k, :], in_=dd["w_y"][k * 128:(k + 1) * 128, :]), sG, 16)
        wy_ld = sG.n

        phase_barrier()
        _scope("carry")
        wait("gpsimd", sV, scans_done[NTILE - 1])
        mark(C.g.dma_start(out=cin_carry[0:1, :].rearrange("a (p g) -> (a p) g", p=128),
                           in_=pc[:]), sG, 16)
        mark(C.g.dma_start(out=cin_carry[1:2, :].rearrange("a (p g) -> (a p) g", p=128),
                           in_=hc[:]), sG, 16)
        wait("gpsimd", sG, sG.n)
        mark(C.g.collective_compute("AllGather", ALU.bypass, replica_groups=RG,
                                    ins=[cin_carry.opt()], outs=[cout_carry.opt()]), cC1, 1)
        wait("gpsimd", cC1, 1)
        for q in range(4):
            mark(C.g.dma_start(
                out=car[:, q * 4:(q + 1) * 4, :],
                in_=cout_carry.rearrange("r (p g) -> p r g", p=128)[:, q * 4:(q + 1) * 4, :]),
                sG, 16)
        carry_ld = sG.n

        wait("vector", sG, carry_ld)
        for bb in range(2):
            swv(C.v.memset(Hb[bb][:], 0.0))
            for s_ in range(3):
                r = 4 * bb + s_
                swv(C.v.tensor_scalar(tmp8a[:], car[:, 2 * r, :], cstc("msk", s_),
                                      cstc("msk1m", s_), ALU.mult, ALU.add))
                swv(C.v.tensor_scalar_mul(tmp8b[:], car[:, 2 * r + 1, :], cstc("msk", s_)))
                swv(C.v.tensor_mul(Hb[bb][:], Hb[bb][:], tmp8a[:]))
                swv(C.v.tensor_add(Hb[bb][:], Hb[bb][:], tmp8b[:]))
        swv(C.v.tensor_scalar_mul(h_in[:], Hb[1][:], cstc("mb")))
        swv(C.v.scalar_tensor_tensor(tmp8a[:], Hb[0][:], cstc("mb"), Hb[0][:],
                                     ALU.mult, ALU.subtract))
        mark(C.v.tensor_sub(h_in[:], h_in[:], tmp8a[:]), sV, 1)
        hin_done = sV.n

        if debug_stage == "carry":
            wait("sync", sV, hin_done)
            mark(C.s.dma_start(out=out[0:128, 0:8], in_=h_in[:]), sS, 16)
            mark(C.s.dma_start(out=out[128:256, 0:8], in_=Hb[0][:]), sS, 16)
            mark(C.s.dma_start(out=out[128:256, 8:16], in_=Hb[1][:]), sS, 16)
            mark(C.s.dma_start(out=out[128:256, 16:28], in_=cst[:]), sS, 16)
            mark(C.s.dma_start(out=out[128:256, 28:36], in_=tmp8a[:]), sS, 16)
            mark(C.s.dma_start(out=out[0:128, 8:136], in_=car[:].rearrange("p r g -> p (r g)")), sS, 16)
            mark(C.s.dma_start(out=out[0:128, 136:144], in_=hc[:]), sS, 16)
            mark(C.s.dma_start(out=out[0:128, 144:152], in_=pc[:]), sS, 16)
            for g in range(G):
                mark(C.s.dma_start(out=out[256 + g * 128:256 + (g + 1) * 128, 0:512], in_=h_sp[g, :, 0:512]), sS, 16)
            wait("gpsimd", sS, sS.n)
            wait("gpsimd", sG, sG.n)
            return

        phase_barrier()
        _scope("attn2")
        wait("sync", sS, sS.n)
        r1_done = {-1: 0}
        tr_pe = {-1: 0}
        hp_ld = {}
        xm_chunk = {}
        for it in range(NTILE):
            wait("tensor", sG, wy_ld)
            wait("scalar", sV, r1_done[it - 1])
            wait("scalar", sPE, tr_pe[it - 1])
            wait("scalar", sV, scans_done[NTILE - 1] if it == 0 else 0)
            for m in range(G):
                bank = m % 3
                wait("tensor", sA, evac[bank])
                mm = None
                for k in range(G):
                    i = C.t.matmul(ps[bank][:], wx_sb[:, k, m * 128:(m + 1) * 128],
                                   x_fmb[:, k, it * ST:(it + 1) * ST],
                                   start=(k == 0), stop=(k == G - 1))
                    mm = i if i is not None else mm
                pe_n = mark(mm, sPE, 1)
                wait("scalar", sPE, pe_n)
                evac[bank] = mark(C.a.activation(out=y_fm[:, m], in_=ps[bank][:],
                                                 func=AF.Gelu, bias=by_sb[:, m:m + 1]), sA, 1)
            y_done = sA.n

            wait("vector", sA, y_done)
            wait("vector", sV, hin_done)
            lastr = None
            for g in range(G):
                j = it * G + g
                gi = j % 2
                if j - 2 in hp_ld:
                    wait("sync", sV, hp_ld[j - 2])
                mark(C.s.dma_start(out=h_t[:, gi], in_=h_sp[g, :, it * ST:(it + 1) * ST]), sS, 16)
                mark(C.s.dma_start(out=p_t[:, gi], in_=p_sp[g, :, it * ST:(it + 1) * ST]), sS, 16)
                mark(C.s.dma_start(out=xf2[:, gi], in_=xT_r[:, g, 3 + it * ST:3 + (it + 1) * ST]), sS, 16)
                s_n = sS.n
                wait("vector", sS, s_n)
                C.v.scalar_tensor_tensor(h_t[:, gi], p_t[:, gi], h_in[:, g:g + 1],
                                         h_t[:, gi], ALU.mult, ALU.add)
                C.v.tensor_mul(y_fm[:, g], y_fm[:, g], h_t[:, gi])
                i = C.v.tensor_add(y_fm[:, g], y_fm[:, g], xf2[:, gi])
                lastr = i if i is not None else lastr
                hp_ld[j] = mark(lastr, sV, 1)
            r1_done[it] = sV.n

            if debug_stage == "p2" and it == 0:
                wait("sync", sV, r1_done[0])
                for g in range(G):
                    mark(C.s.dma_start(out=out[g * 128:(g + 1) * 128, 0:512], in_=y_fm[:, g]), sS, 16)
                mark(C.s.dma_start(out=out[1024:1152, 0:512], in_=h_t[:, 0]), sS, 16)
                mark(C.s.dma_start(out=out[1152:1280, 0:512], in_=h_t[:, 1]), sS, 16)
                mark(C.s.dma_start(out=out[1280:1408, 0:8], in_=h_in[:]), sS, 16)
                wait("gpsimd", sS, sS.n)
                wait("gpsimd", sG, sG.n)
                return
            for cch in range(ST // 128):
                chunk = it * (ST // 128) + cch
                pb = pw[chunk % 2]
                wait("tensor", sV, r1_done[it])
                if ("pw", chunk % 2) in evac:
                    wait("tensor", sV, evac[("pw", chunk % 2)])
                mm = None
                for g in range(G):
                    i = C.t.transpose(pb[:, g * 128:(g + 1) * 128],
                                      y_fm[:, g, cch * 128:(cch + 1) * 128], idn_sb[:])
                    mm = i if i is not None else mm
                pe_n = mark(mm, sPE, 1)
                wait("vector", sPE, pe_n)
                if chunk > 0:
                    wait("vector", sA, xm_chunk[chunk - 1])
                evac[("pw", chunk % 2)] = mark(C.v.tensor_copy(out=r1_tok[:], in_=pb[:]), sV, 1)
                tr_done = sV.n

                wait("scalar", sV, tr_done)
                mark(C.a.activation(out=lnsq[:], in_=r1_tok[:], func=AF.Copy,
                                    accum_out=stat[:, 0:1]), sA, 1)
                a_n = mark(C.a.activation(out=lnsq[:], in_=r1_tok[:], func=AF.Square,
                                          accum_out=stat[:, 1:2]), sA, 1)
                xm_chunk[chunk] = a_n
                wait("vector", sA, a_n)
                swv(C.v.tensor_scalar_mul(stat[:, 2:3], stat[:, 0:1], 1.0 / D))
                swv(C.v.tensor_scalar_mul(stat[:, 3:4], stat[:, 1:2], 1.0 / D))
                swv(C.v.tensor_mul(stat[:, 4:5], stat[:, 2:3], stat[:, 2:3]))
                swv(C.v.tensor_sub(stat[:, 3:4], stat[:, 3:4], stat[:, 4:5]))
                mark(C.v.tensor_scalar(stat[:, 3:4], stat[:, 3:4], EPS, None, ALU.add), sV, 1)
                var_n = sV.n
                wait("scalar", sV, var_n)
                sd_n = mark(C.a.activation(out=stat[:, 5:6], in_=stat[:, 3:4], func=AF.Sqrt), sA, 1)
                wait("vector", sA, sd_n)
                swv(C.v.reciprocal(stat[:, 6:7], stat[:, 5:6]))
                C.v.tensor_scalar(xm_tok[:, chunk, :], r1_tok[:],
                                  stat[:, 2:3], stat[:, 6:7], ALU.subtract, ALU.mult)
                mark(C.v.tensor_copy(out=xm_bf[:], in_=xm_tok[:, chunk, :]), sV, 1)
                xmb_n = sV.n
                wait("sync", sV, xmb_n)
                mark(C.s.dma_start(out=cin_xm[chunk * 128:(chunk + 1) * 128, :],
                                   in_=xm_bf[:]), sS, 16)
            tr_pe[it] = sPE.n

        if debug_stage == "xm":
            wait("sync", sV, sV.n)
            for chunk in range(TC // 128):
                mark(C.s.dma_start(out=out[chunk * 128:(chunk + 1) * 128, :],
                                   in_=xm_tok[:, chunk, :]), sS, 16)
            wait("gpsimd", sS, sS.n)
            wait("gpsimd", sG, sG.n)
            return

        phase_barrier()
        _scope("routing")
        y_pe_last = tr_pe[NTILE - 1]
        xm_fm = x_fmb

        for k in range(G):
            mark(C.g.dma_start(out=projT_sb[:, k, :], in_=dd["projT"][k * 128:(k + 1) * 128, :]), sG, 16)
        for t_, d_ in [(pjb_sb, dd["projb_rep"]), (eb_sb, dd["ebias_rep"]),
                       (io8_sb, dd["iota8"]), (io8p_sb, dd["iota8p"]),
                       (idnb_sb, dd["idnb"])]:
            mark(C.g.dma_start(out=t_[:], in_=d_), sG, 16)
        rconsts_ld = sG.n

        xm_all_written = sS.n
        wait("gpsimd", sS, xm_all_written)
        mark(C.g.collective_compute("AllGather", ALU.bypass, replica_groups=RG,
                                    ins=[cin_xm.opt()], outs=[cout_xm.opt()]), cC2a, 1)

        wait("sync", sPE, y_pe_last)
        wait("sync", sS, xm_all_written)
        for g in range(G):
            for cq in range(4):
                mark(C.s.dma_start_transpose(
                    xm_fm[:, g, cq * 512:(cq + 1) * 512],
                    cin_xm[cq * 512:(cq + 1) * 512, g * 128:(g + 1) * 128]), sS, 16)
        xmfm_ld = sS.n

        wait("tensor", sS, xmfm_ld)
        wait("tensor", sG, rconsts_ld)
        wait("tensor", sA, evac.get(1, 0))
        mm = None
        for chunk in range(TC // 128):
            for k in range(G):
                i = C.t.matmul(ps[1][:, chunk * 8:(chunk + 1) * 8],
                               xm_fm[:, k, chunk * 128:(chunk + 1) * 128],
                               projT_sb[:, k, :], start=(k == 0), stop=(k == G - 1))
                mm = i if i is not None else mm
        pe_n = mark(mm, sPE, 1)
        wait("vector", sPE, pe_n)
        evac[("z",)] = swv(C.v.tensor_copy(out=zall[:].rearrange("p a b -> p (a b)"),
                                           in_=ps[1][:, 0:128]))

        def b8(t2d):
            return t2d.rearrange("p (a b) -> p a b", a=1).to_broadcast([128, 16, 8])

        def bc(t2d):
            return t2d.rearrange("p (a b) -> p a b", b=1).to_broadcast([128, 16, 8])

        def max8(srcT, dstT, op):
            swv(C.v.tensor_tensor(zt[:], srcT[:, :, 0:4], srcT[:, :, 4:8], op))
            swv(C.v.tensor_tensor(zt[:, :, 0:2], zt[:, :, 0:2], zt[:, :, 2:4], op))
            swv(C.v.tensor_tensor(dstT.rearrange("p (a b) -> p a b", b=1),
                                  zt[:, :, 0:1], zt[:, :, 1:2], op))

        swv(C.v.tensor_tensor(zall[:], zall[:], b8(pjb_sb), ALU.add))
        zn = sV.n
        wait("scalar", sV, zn)
        a_n = mark(C.a.activation(out=sgall[:], in_=zall[:], func=AF.Sigmoid), sA, 1)
        wait("vector", sA, a_n)
        swv(C.v.tensor_tensor(sgall[:], sgall[:], b8(eb_sb), ALU.add))
        swv(C.v.tensor_scalar_max(sgall[:], sgall[:], 1e-12))
        max8(sgall, m1a, ALU.max)
        swv(C.v.tensor_tensor(sgal2[:], sgall[:], bc(m1a), ALU.is_equal))
        swv(C.v.scalar_tensor_tensor(sgal2[:], sgal2[:], -1000.0, b8(io8p_sb), ALU.mult, ALU.add))
        max8(sgal2, e1a.rearrange("p n -> p n"), ALU.min)
        swv(C.v.tensor_tensor(sgal2[:], b8(io8_sb), bc(e1a), ALU.is_equal))
        swv(C.v.scalar_tensor_tensor(sgal2[:], sgal2[:], -4.0, sgall[:], ALU.mult, ALU.add))
        max8(sgal2, m2a, ALU.max)
        swv(C.v.tensor_tensor(sgal2[:], sgal2[:], bc(m2a), ALU.is_equal))
        swv(C.v.scalar_tensor_tensor(sgal2[:], sgal2[:], -1000.0, b8(io8p_sb), ALU.mult, ALU.add))
        max8(sgal2, e2a.rearrange("p n -> p n"), ALU.min)
        swv(C.v.tensor_add(w1a[:], m1a[:], m2a[:]))
        swv(C.v.reciprocal(w2a[:], w1a[:]))
        swv(C.v.tensor_mul(w1a[:], m1a[:], w2a[:]))
        swv(C.v.tensor_mul(w2a[:], m2a[:], w2a[:]))
        route_done = sV.n

        pair_dma = 0
        for ri, arrq in enumerate((e1a, e2a, w1a, w2a)):
            wait("tensor", sV, route_done)
            wait("tensor", sV, evac.get(("pair",), 0))
            wait("tensor", sV, evac.get(("z",), 0))
            mm = C.t.transpose(ps[1][0:16, 0:128], arrq[:], idn_sb[:])
            pe_n = mark(mm, sPE, 1)
            wait("vector", sPE, pe_n)
            wait("vector", sS, pair_dma)
            evac[("pair",)] = swv(C.v.tensor_copy(out=pr16[0:16, :], in_=ps[1][0:16, 0:128]))
            wait("sync", sV, sV.n)
            mark(C.s.dma_start(out=cin_pair[ri:ri + 1, :].rearrange("a (c p) -> (a c) p", c=16),
                               in_=pr16[0:16, :]), sS, 16)
            pair_dma = sS.n
        pair_written = sS.n
        wait("gpsimd", sS, pair_written)
        mark(C.g.collective_compute("AllGather", ALU.bypass, replica_groups=RG,
                                    ins=[cin_pair.opt()], outs=[cout_pair.opt()]), cC2b, 1)

        if debug_stage == "routing":
            wait("gpsimd", cC2b, 1)
            mark(C.g.dma_start(out=out[0:128, 0:16], in_=w1a[:]), sG, 16)
            mark(C.g.dma_start(out=out[0:128, 16:32], in_=w2a[:]), sG, 16)
            mark(C.g.dma_start(out=out[0:128, 32:48], in_=e1a[:]), sG, 16)
            mark(C.g.dma_start(out=out[0:128, 48:64], in_=e2a[:]), sG, 16)
            for r in range(32):
                mark(C.g.dma_start(out=out[256 + r * 2:256 + r * 2 + 1, 0:1024], in_=cout_pair[r:r + 1, 0:1024]), sG, 16)
                mark(C.g.dma_start(out=out[256 + r * 2 + 1:256 + r * 2 + 2, 0:1024], in_=cout_pair[r:r + 1, 1024:2048]), sG, 16)
            wait("gpsimd", sS, sS.n)
            wait("gpsimd", sG, sG.n)
            return

        wait("gpsimd", sPE, y_pe_last)
        for k in range(G):
            mark(C.g.dma_start(out=sh1T_sb[:, k, :], in_=dd["sh1T"][k * 128:(k + 1) * 128, :]), sG, 16)
        sh1_ld = sG.n
        mark(C.g.dma_start(out=sh1b_sb[:], in_=dd["sh1b2d"]), sG, 16)
        mark(C.g.dma_start(out=fc1b_sb[:], in_=dd["fc1b2d"]), sG, 16)
        smallb_ld = sG.n

        phase_barrier()
        _scope("sh1")
        wait("tensor", sV, evac.get(("z",), 0))
        wait("tensor", sV, evac.get(("pair",), 0))
        hsh_ev = {}
        for itl in range(NTILE):
            for m in range(SHD // 128):
                bank = m % 3
                wait("tensor", sG, sh1_ld)
                wait("tensor", sS, xmfm_ld)
                wait("tensor", sA, evac.get(bank, 0))
                mm = None
                for k in range(G):
                    i = C.t.matmul(ps[bank][:], sh1T_sb[:, k, m * 128:(m + 1) * 128],
                                   xm_fm[:, k, itl * ST:(itl + 1) * ST],
                                   start=(k == 0), stop=(k == G - 1))
                    mm = i if i is not None else mm
                pe_n = mark(mm, sPE, 1)
                wait("scalar", sPE, pe_n)
                wait("scalar", sG, smallb_ld)
                hb = m % 2
                if ("hshb", hb) in hsh_ev:
                    wait("scalar", sS, hsh_ev[("hshb", hb)])
                evac[bank] = mark(C.a.activation(out=hshb[:, hb], in_=ps[bank][:],
                                                 func=AF.Gelu, bias=sh1b_sb[:, m:m + 1]), sA, 1)
                wait("sync", sA, evac[bank])
                mark(C.s.dma_start(out=hsh_sp[m * 128:(m + 1) * 128, itl * ST:(itl + 1) * ST],
                                   in_=hshb[:, hb]), sS, 16)
                hsh_ev[("hshb", hb)] = sS.n
        sh1_pe_last = sPE.n
        hsh_written = sS.n

        _scope("selection")
        wait("gpsimd", cC2b, 1)
        for qi, dst in ((0, eA), (1, eB), (2, wAt), (3, wBt)):
            for r in range(NCORE):
                mark(C.g.dma_start(
                    out=dst[r * 16:(r + 1) * 16, :],
                    in_=cout_pair[4 * r + qi:4 * r + qi + 1, :]
                        .rearrange("a (i j) -> (a i) j", i=16)), sG, 16)
        ptiles_ld = sG.n
        wait("gpsimd", sV, route_done)
        mark(C.g.dma_start(out=io_ref[:], in_=dd["iota_ref"]), sG, 16)
        mark(C.g.dma_start(out=io_j[:], in_=dd["iota_j"]), sG, 16)
        mark(C.g.dma_start(out=io_slot[:], in_=dd["iota_slot"]), sG, 16)
        mark(C.g.dma_start(out=io16x[:], in_=dd["iota16x256"]), sG, 16)
        sel_ld = sG.n

        wait("vector", sG, ptiles_ld)
        wait("vector", sG, sel_ld)
        for half, (eT, wT) in enumerate(((eA, wAt), (eB, wBt))):
            sl = slice(half * 128, (half + 1) * 128)
            swv(C.v.tensor_scalar(selb[:, sl], eT[:], cstc("mye"), None, ALU.is_equal))
            swv(C.v.tensor_mul(wm[:, sl], wT[:], selb[:, sl]))
            swv(C.v.tensor_add(wm[:, sl], wm[:, sl], selb[:, sl]))
            swv(C.v.tensor_scalar(wm[:, sl], wm[:, sl], -1.0, None, ALU.add))

        swv(C.v.memset(dshs[:], 1.0))
        swv(C.v.memset(cnt4[:], 0.0))

        def probes(src, op, ncols):
            lastp = None
            for q in range(ncols):
                i = C.v.tensor_scalar(selb[:], src, sel4m[:, q:q + 1], None, op,
                                      ALU.add, accum_out=cnt4[:, q:q + 1])
                lastp = i if i is not None else lastp
            n = mark(lastp, sV, 1)
            wait("vector", sV, n)
            wait("tensor", sV, n)
            wait("tensor", sV, evac.get(("sel",), 0))
            wait("tensor", sA, evac.get(2, 0))
            mm = C.t.matmul(ps[2][:, 0:4], dshs[:, 0:128], cnt4[:], start=True, stop=True)
            pe_n = mark(mm, sPE, 1)
            wait("vector", sPE, pe_n)
            evac[("sel",)] = swv(C.v.tensor_copy(out=tot4[:], in_=ps[2][:, 0:4]))

        def upd_lohi(ncols, pred_ge_keeps_lo):
            for q in range(ncols):
                qq = q if pred_ge_keeps_lo else ncols - 1 - q
                g_ = ge4[:, qq:qq + 1]
                m_ = sel4m[:, qq:qq + 1]
                if pred_ge_keeps_lo:
                    swv(C.v.tensor_sub(sv("t1"), m_, sv("lo")))
                    swv(C.v.tensor_mul(sv("t1"), sv("t1"), g_))
                    swv(C.v.tensor_add(sv("lo"), sv("lo"), sv("t1")))
                else:
                    swv(C.v.tensor_sub(sv("t1"), m_, sv("hi")))
                    swv(C.v.tensor_mul(sv("t1"), sv("t1"), g_))
                    swv(C.v.tensor_add(sv("hi"), sv("hi"), sv("t1")))

        swv(C.v.memset(sel4m[:, 0:1], -0.5))
        probes(wm[:], ALU.is_gt, 1)
        swv(C.v.tensor_copy(out=sv("asg"), in_=tot4[:, 0:1]))
        swv(C.v.memset(sv("lo"), 1e-14))
        swv(C.v.memset(sv("hi"), 2.0))
        for rnd in range(16):
            if rnd < 6:
                swv(C.v.tensor_mul(sv("t1"), sv("lo"), sv("hi")))
                n = sV.n
                wait("scalar", sV, n)
                a_n = mark(C.a.activation(out=sel4m[:, 1:2], in_=sv("t1"), func=AF.Sqrt), sA, 1)
                wait("vector", sA, a_n)
                swv(C.v.tensor_mul(sv("t1"), sv("lo"), sel4m[:, 1:2]))
                swv(C.v.tensor_mul(sv("t2"), sel4m[:, 1:2], sv("hi")))
                n = sV.n
                wait("scalar", sV, n)
                C.a.activation(out=sel4m[:, 0:1], in_=sv("t1"), func=AF.Sqrt)
                a_n = mark(C.a.activation(out=sel4m[:, 2:3], in_=sv("t2"), func=AF.Sqrt), sA, 1)
                wait("vector", sA, a_n)
            else:
                swv(C.v.tensor_sub(sv("t1"), sv("hi"), sv("lo")))
                swv(C.v.tensor_scalar_mul(sv("t1"), sv("t1"), 0.25))
                swv(C.v.tensor_add(sel4m[:, 0:1], sv("lo"), sv("t1")))
                swv(C.v.tensor_add(sel4m[:, 1:2], sel4m[:, 0:1], sv("t1")))
                swv(C.v.tensor_add(sel4m[:, 2:3], sel4m[:, 1:2], sv("t1")))
            probes(wm[:], ALU.is_gt, 3)
            swv(C.v.tensor_scalar(ge4[:, 0:3], tot4[:, 0:3], float(CAP), None, ALU.is_ge))
            upd_lohi(3, True)
            swv(C.v.tensor_scalar(ge4[:, 0:3], ge4[:, 0:3], -1.0, 1.0, ALU.mult, ALU.add))
            upd_lohi(3, False)
        swv(C.v.tensor_scalar(sv("ov"), sv("asg"), float(CAP), None, ALU.is_gt))
        swv(C.v.tensor_mul(sv("t1"), sv("ov"), sv("hi")))
        swv(C.v.tensor_scalar(sv("t2"), sv("ov"), 1.0, -1.0, ALU.mult, ALU.add))
        swv(C.v.tensor_add(sv("theta"), sv("t1"), sv("t2")))
        swv(C.v.tensor_copy(out=sel4m[:, 0:1], in_=sv("theta")))
        probes(wm[:], ALU.is_gt, 1)
        swv(C.v.tensor_copy(out=sv("cntgt"), in_=tot4[:, 0:1]))
        swv(C.v.tensor_scalar(sv("neq"), sv("cntgt"), -1.0, float(CAP), ALU.mult, ALU.add))
        swv(C.v.tensor_scalar(keptb[:], wm[:], sv("theta"), None, ALU.is_gt))
        swv(C.v.tensor_scalar(eqm[:], wm[:], sv("theta"), None, ALU.is_equal))
        swv(C.v.tensor_scalar(selb[:], eqm[:], -1e9, 1e9, ALU.mult, ALU.add))
        swv(C.v.tensor_mul(mskio[:], eqm[:], io_ref[:]))
        swv(C.v.tensor_add(mskio[:], mskio[:], selb[:]))
        swv(C.v.memset(sv("lo"), -1.0))
        swv(C.v.memset(sv("hi"), 32768.0))
        for rnd in range(9):
            swv(C.v.tensor_sub(sv("t1"), sv("hi"), sv("lo")))
            swv(C.v.tensor_scalar_mul(sv("t1"), sv("t1"), 0.25))
            swv(C.v.tensor_add(sel4m[:, 0:1], sv("lo"), sv("t1")))
            swv(C.v.tensor_add(sel4m[:, 1:2], sel4m[:, 0:1], sv("t1")))
            swv(C.v.tensor_add(sel4m[:, 2:3], sel4m[:, 1:2], sv("t1")))
            probes(mskio[:], ALU.is_le, 3)
            swv(C.v.tensor_tensor(ge4[:, 0:3], tot4[:, 0:3],
                                  sv("neq").to_broadcast([128, 3]), ALU.is_ge))
            upd_lohi(3, False)
            swv(C.v.tensor_scalar(ge4[:, 0:3], ge4[:, 0:3], -1.0, 1.0, ALU.mult, ALU.add))
            upd_lohi(3, True)
        swv(C.v.tensor_scalar(selb[:], io_ref[:], sv("hi"), None, ALU.is_le))
        swv(C.v.tensor_mul(selb[:], selb[:], eqm[:]))
        swv(C.v.tensor_scalar(eqm[:], wm[:], -0.5, None, ALU.is_gt))
        swv(C.v.tensor_mul(selb[:], selb[:], eqm[:]))
        swv(C.v.tensor_add(keptb[:], keptb[:], selb[:]))
        swv(C.v.memset(dshs[:], 1.0))
        swv(C.v.tensor_tensor_scan(pos[:], dshs[:], keptb[:], 0.0, ALU.mult, ALU.add))
        swv(C.v.tensor_copy(out=sv("m_p"), in_=pos[:, 255:256]))
        swv(C.v.tensor_sub(rexc[:], pos[:], keptb[:]))
        wait("gpsimd", sV, sV.n)
        mark(C.g.dma_start(out=cnt_dram.rearrange("(p a) -> p a", a=1), in_=sv("m_p")), sG, 16)
        wait("gpsimd", sG, sG.n)
        mark(C.g.dma_start(out=mrow, in_=cnt_dram.rearrange("(a p) -> a p", a=1)), sG, 16)
        g_n = sG.n
        wait("vector", sG, g_n)
        swv(C.v.tensor_tensor_scan(brow, dshs[0:1, 0:128], mrow, 0.0, ALU.mult, ALU.add))
        swv(C.v.memset(sv("t3"), 0.0))
        v_n = sV.n
        wait("gpsimd", sV, v_n)
        mark(C.g.dma_start(out=base_dram[1:129].rearrange("(a p) -> a p", a=1), in_=brow), sG, 16)
        mark(C.g.dma_start(out=base_dram[0:1].rearrange("(a p) -> a p", a=1), in_=sv("t3")[0:1, :]), sG, 16)
        wait("gpsimd", sG, sG.n)
        mark(C.g.dma_start(out=sv("bcol"), in_=base_dram[0:128].rearrange("(p a) -> p a", a=1)), sG, 16)
        mark(C.g.dma_start(out=base_bc, in_=base_dram[0:128].rearrange("(a p) -> a p", a=1).to_broadcast([128, 128])), sG, 16)
        base_ld = sG.n
        wait("vector", sG, base_ld)
        swv(C.v.tensor_scalar(slotm[:], rexc[:], sv("bcol"), None, ALU.add))
        swv(C.v.tensor_scalar(rowm[:], slotm[:], cstc("ecap"), -SENT_ROW, ALU.add, ALU.add))
        swv(C.v.tensor_mul(rowm[:], rowm[:], keptb[:]))
        swv(C.v.tensor_scalar(rowm[:], rowm[:], SENT_ROW, None, ALU.add))
        v_n = sV.n
        wait("gpsimd", sV, v_n)
        mark(C.g.dma_start(out=cin_slot[0:1, :].rearrange("a (p j) -> (a p) j", p=128),
                           in_=rowm[:]), sG, 16)
        wait("gpsimd", sG, sG.n)
        mark(C.g.collective_compute("AllGather", ALU.bypass, replica_groups=RG,
                                    ins=[cin_slot.opt()], outs=[cout_slot.opt()]), sems["cSlot"], 1)
        swv(C.v.tensor_scalar_mul(tokv[:, 0:128], io_ref[:, 0:128], 0.5))
        swv(C.v.tensor_scalar(tokv[:, 128:256], io_ref[:, 128:256], 0.5, -0.5, ALU.mult, ALU.add))
        swv(C.v.tensor_scalar(arr_[:], tokv[:], -16384.0, None, ALU.add))
        swv(C.v.tensor_mul(arr_[:], arr_[:], keptb[:]))
        swv(C.v.tensor_scalar(arr_[:], arr_[:], 16384.0, None, ALU.add))
        swv(C.v.tensor_sub(delt[:], io_j[:], rexc[:]))
        for bbit in range(8):
            sft = 1 << bbit
            w_ = 256 - sft
            swv(C.v.tensor_scalar(mskio[:], delt[:], 0.5 / sft, None, ALU.mult))
            swv(C.v.tensor_copy(out=bitu[:], in_=mskio[:]))
            swv(C.v.tensor_copy(out=dshs[:], in_=bitu[:]))
            swv(C.v.tensor_sub(dshs[:], mskio[:], dshs[:]))
            swv(C.v.tensor_mul(dshs[:], dshs[:], dshs[:]))
            swv(C.v.tensor_scalar(dshs[:], dshs[:], 0.0625, None, ALU.is_ge))
            if debug_stage == "fulldbg" and bbit == 0:
                wait("gpsimd", sV, sV.n)
                mark(C.g.dma_start(out=dbg_sel2[:, 1300:1556], in_=mskio[:]), sG, 16)
                mark(C.g.dma_start(out=dbg_sel2[:, 1556:1812], in_=dshs[:]), sG, 16)
                mark(C.g.dma_start(out=dbg_sel2[:, 1812:2068], in_=delt[:]), sG, 16)
                wait("vector", sG, sG.n)
            swv(C.v.tensor_copy(out=bitu[:], in_=dshs[:]))
            swv(C.v.scalar_tensor_tensor(mskio[:], dshs[:], -float(sft), delt[:],
                                         ALU.mult, ALU.add))
            swv(C.v.select(arr_[:, 0:w_], bitu[:, sft:256], arr_[:, sft:256], arr_[:, 0:w_]))
            swv(C.v.select(delt[:, 0:w_], bitu[:, sft:256], mskio[:, sft:256], mskio[:, 0:w_]))
            if sft > 0:
                swv(C.v.tensor_copy(out=delt[:, w_:256], in_=mskio[:, w_:256]))
        swv(C.v.tensor_scalar(dshs[:], io_j[:], sv("m_p"), None, ALU.is_ge))
        swv(C.v.tensor_scalar(selb[:], arr_[:], -1.0, 16384.0, ALU.mult, ALU.add))
        swv(C.v.tensor_mul(selb[:], selb[:], dshs[:]))
        swv(C.v.tensor_add(arr_[:], arr_[:], selb[:]))
        v_n = sV.n
        wait("gpsimd", sV, v_n)
        mark(C.g.dma_start(out=padded_dram.rearrange("(p j) -> p j", p=128), in_=arr_[:]), sG, 16)
        padded_w = sG.n
        if debug_stage == "fulldbg":
            wait("gpsimd", sG, padded_w)
            mark(C.g.dma_start(out=dbg_sel2[:, 0:16], in_=xf2v[:, 456:472]), sG, 16)
            mark(C.g.dma_start(out=dbg_sel2[:, 16:272], in_=keptb[:]), sG, 16)
            mark(C.g.dma_start(out=dbg_sel2[:, 272:528], in_=arr_[:]), sG, 16)
            mark(C.g.dma_start(out=dbg_sel2[:, 528:784], in_=io_j[:]), sG, 16)
            mark(C.g.dma_start(out=dbg_sel2[:, 784:1040], in_=delt[:]), sG, 16)
            mark(C.g.dma_start(out=dbg_sel2[:, 1040:1296], in_=rexc[:]), sG, 16)
        wait("vector", sG, padded_w)
        big20 = bview(0, 2560).rearrange("p (c q) -> p c q", c=20)
        for hh in range(2):
            cols = slice(hh * 20, (hh + 1) * 20)
            base_b20 = base_bc.rearrange("p (a q) -> p a q", a=1).to_broadcast([128, 20, 128])
            ios_b = io_slot[:, cols].rearrange("p (c a) -> p c a", a=1).to_broadcast([128, 20, 128])
            swv(C.v.tensor_tensor(big20[:], base_b20, ios_b, ALU.is_le))
            w_ = 128
            while w_ > 1:
                w_ //= 2
                swv(C.v.tensor_tensor(big20[:, :, 0:w_], big20[:, :, 0:w_],
                                      big20[:, :, w_:2 * w_], ALU.add))
            swv(C.v.tensor_scalar(srcx[:, cols], big20[:, :, 0:1].rearrange("p c a -> p (c a)"),
                                  256.0, -256.0, ALU.mult, ALU.add))
            swv(C.v.tensor_tensor(big20[:], base_b20, ios_b, ALU.is_le))
            swv(C.v.tensor_tensor(big20[:], big20[:], base_b20, ALU.mult))
            w_ = 128
            while w_ > 1:
                w_ //= 2
                swv(C.v.tensor_tensor(big20[:, :, 0:w_], big20[:, :, 0:w_],
                                      big20[:, :, w_:2 * w_], ALU.max))
            swv(C.v.tensor_tensor(srcx[:, cols], srcx[:, cols], io_slot[:, cols], ALU.add))
            swv(C.v.tensor_tensor(srcx[:, cols], srcx[:, cols],
                                  big20[:, :, 0:1].rearrange("p c a -> p (c a)"), ALU.subtract))
        swv(C.v.tensor_scalar_min(srcx[:], srcx[:], 32767.0))
        swv(C.v.tensor_scalar_max(srcx[:], srcx[:], 0.0))
        swv(C.v.tensor_copy(out=srcu[:], in_=srcx[:]))
        v_n = sV.n
        wait("gpsimd", sV, v_n)
        wait("gpsimd", sG, padded_w)
        for c40 in range(40):
            mark(C.g.indirect_dma_start(out=tokf[:, c40:c40 + 1], out_offset=None,
                                        in_=padded_dram.rearrange("(n a) -> n a", a=1),
                                        in_offset=IOA(srcu[:, c40:c40 + 1], 0)), sG, 16)
        tok_g = sG.n
        wait("vector", sG, tok_g)
        swv(C.v.tensor_scalar_min(tokf[:], tokf[:], 16383.0))
        swv(C.v.tensor_copy(out=toku[:], in_=tokf[:]))
        tok_ready = sV.n

        if debug_stage == "sel":
            wait("sync", sV, tok_ready)
            mark(C.s.dma_start(out=out[0:128, 0:16], in_=xf2v[:, 456:472]), sS, 16)
            mark(C.s.dma_start(out=out[128:256, 0:256], in_=rowm[:]), sS, 16)
            mark(C.s.dma_start(out=out[384:512, 0:40], in_=tokf[:]), sS, 16)
            mark(C.s.dma_start(out=out[512:640, 0:256], in_=keptb[:]), sS, 16)
            mark(C.s.dma_start(out=out[640:768, 0:40], in_=srcx[:]), sS, 16)
            wait("gpsimd", sS, sS.n)
            wait("gpsimd", sG, sG.n)
            return

        phase_barrier()
        _scope("expert")
        wait("gpsimd", sPE, sh1_pe_last)
        for k in range(G):
            mark(C.g.dma_start(out=fc1T_sb[:, k, :], in_=dd["fc1T"][k * 128:(k + 1) * 128, :]), sG, 16)
        fc1_ld = sG.n
        for k in range(16):
            mark(C.g.dma_start(out=fc2T_sb[:, k, :], in_=dd["fc2T"][k * 128:(k + 1) * 128, :]), sG, 16)
        fc2_ld = sG.n
        wait("gpsimd", sV, tok_ready)
        wait("gpsimd", sG, padded_w)
        mark(C.g.dma_start(out=fc2b_rp[:], in_=dd["fc2b_rep"]), sG, 16)
        mark(C.g.dma_start(out=sh2b_rp[:], in_=dd["sh2b_rep"]), sG, 16)
        rep_ld = sG.n

        wait("vector", sG, sel_ld)
        swv(C.v.tensor_scalar(sv("t1"), cstc("mye"), 4096.0, None, ALU.mult))
        swv(C.v.tensor_tensor(wABe[:, 0:16], io16x[:],
                              cstc("iota128").to_broadcast([128, 16]), ALU.add))
        swv(C.v.tensor_scalar(wABe[:, 0:16], wABe[:, 0:16], sv("t1"), None, ALU.add))
        swv(C.v.scalar_tensor_tensor(kAB[:, 0:16], e1a[:], 32768.0, wABe[:, 0:16],
                                     ALU.mult, ALU.add))
        swv(C.v.scalar_tensor_tensor(kAB[:, 16:32], e2a[:], 32768.0, wABe[:, 0:16],
                                     ALU.mult, ALU.add))
        swv(C.v.tensor_scalar(kAB[:, 16:32], kAB[:, 16:32], 128.0, None, ALU.add))
        swv(C.v.tensor_copy(out=ixu_all[:], in_=kAB[:]))
        p4pre_v = sV.n
        wait("gpsimd", sV, p4pre_v)
        wait("gpsimd", sems["cSlot"], 1)
        csf = cout_slot_flat.rearrange("(n a) -> n a", a=1)
        for cq in range(16):
            mark(C.g.indirect_dma_start(out=sva_all[:, cq:cq + 1], out_offset=None,
                                        in_=csf, in_offset=IOA(ixu_all[:, cq:cq + 1], 0)), sG, 16)
            mark(C.g.indirect_dma_start(out=svb_all[:, cq:cq + 1], out_offset=None,
                                        in_=csf, in_offset=IOA(ixu_all[:, 16 + cq:17 + cq], 0)), sG, 16)
        slotg = sG.n
        wait("vector", sG, slotg)
        swv(C.v.tensor_scalar(kAB[:, 0:16], sva_all[:], SENT_ROW, None, ALU.is_lt))
        swv(C.v.tensor_scalar(kAB[:, 16:32], svb_all[:], SENT_ROW, None, ALU.is_lt))
        swv(C.v.tensor_scalar_min(sva_all[:], sva_all[:], SENT_ROW - 1.0))
        swv(C.v.tensor_scalar_min(svb_all[:], svb_all[:], SENT_ROW - 1.0))
        swv(C.v.tensor_copy(out=rAu_all[:], in_=sva_all[:]))
        swv(C.v.tensor_copy(out=rBu_all[:], in_=svb_all[:]))
        swv(C.v.tensor_mul(wABe[:, 0:16], w1a[:], kAB[:, 0:16]))
        swv(C.v.tensor_mul(wABe[:, 16:32], w2a[:], kAB[:, 16:32]))
        p4pre_done = sV.n

        oe_w = {}
        fc2_pe = {-1: 0}
        wait("tensor", sV, evac.get(("sel",), 0))
        for ce in range(CAP // 512):
            wait("gpsimd", cC2a, 1)
            wait("gpsimd", sV, tok_ready)
            for q4 in range(4):
                cc = ce * 4 + q4
                xg = xeg[:, cc % 2, :]
                if ("xeg", cc % 2) in evac:
                    wait("gpsimd", sPE, evac[("xeg", cc % 2)])
                mark(C.g.indirect_dma_start(out=xg, out_offset=None,
                                            in_=cout_xm,
                                            in_offset=IOA(toku[:, cc:cc + 1], 0)), sG, 16)
                g_n = sG.n
                wait("tensor", sG, g_n)
                wait("tensor", sG, rconsts_ld)
                wait("tensor", sV, evac.get(("pb16",), 0))
                mm = None
                for g in range(G):
                    i = C.t.transpose(pb16[:, g * 128:(g + 1) * 128],
                                      xg.rearrange("p (g n) -> p g n", g=G)[:, g, :],
                                      idnb_sb[:])
                    mm = i if i is not None else mm
                pe_n = mark(mm, sPE, 1)
                evac[("xeg", cc % 2)] = pe_n
                wait("vector", sPE, pe_n)
                evac[("pb16",)] = swv(C.v.tensor_copy(
                    out=xe_fm[:, :, q4 * 128:(q4 + 1) * 128],
                    in_=pb16.rearrange("p (g n) -> p g n", g=G)))
            xe_ready = sV.n
            wait("tensor", sG, fc1_ld)
            wait("tensor", sV, xe_ready)
            wait("scalar", sPE, fc2_pe[ce - 1])
            for m in range(16):
                bank = m % 3
                wait("tensor", sA, evac.get(bank, 0))
                mm = None
                for k in range(G):
                    i = C.t.matmul(ps[bank][:], fc1T_sb[:, k, m * 128:(m + 1) * 128],
                                   xe_fm[:, k, :], start=(k == 0), stop=(k == G - 1))
                    mm = i if i is not None else mm
                pe_n = mark(mm, sPE, 1)
                wait("scalar", sPE, pe_n)
                wait("scalar", sG, smallb_ld)
                evac[bank] = mark(C.a.activation(out=he[:, m, :], in_=ps[bank][:],
                                                 func=AF.Gelu, bias=fc1b_sb[:, m:m + 1]), sA, 1)
            he_done = sA.n
            wait("tensor", sG, fc2_ld)
            wait("tensor", sA, he_done)
            for m4 in range(4):
                pwb = pw[m4 % 2]
                if ("pw2", m4 % 2) in evac:
                    wait("tensor", sV, evac[("pw2", m4 % 2)])
                mm = None
                for k in range(16):
                    for nn_ in range(2):
                        i = C.t.matmul(pwb[:, nn_ * 512:(nn_ + 1) * 512],
                                       he[:, k, m4 * 128:(m4 + 1) * 128],
                                       fc2T_sb[:, k, nn_ * 512:(nn_ + 1) * 512],
                                       start=(k == 0), stop=(k == 15))
                        mm = i if i is not None else mm
                pe_n = mark(mm, sPE, 1)
                wait("vector", sPE, pe_n)
                wait("vector", sG, rep_ld)
                if ("oeb", m4 % 2) in oe_w:
                    wait("vector", sS, oe_w[("oeb", m4 % 2)])
                ob = oeb[m4 % 2]
                ev = swv(C.v.tensor_add(ob[:], pwb[:], fc2b_rp[:]))
                evac[("pw2", m4 % 2)] = ev
                wait("sync", sV, ev)
                mark(C.s.dma_start(out=cin_oe[(ce * 4 + m4) * 128:(ce * 4 + m4 + 1) * 128, :],
                                   in_=ob[:]), sS, 16)
                oe_w[("oeb", m4 % 2)] = sS.n
            fc2_pe[ce] = sPE.n
        oe_written = sS.n
        fc_pe_last = sPE.n

        wait("gpsimd", sS, oe_written)
        mark(C.g.collective_compute("AllGather", ALU.bypass, replica_groups=RG,
                                    ins=[cin_oe.opt()], outs=[cout_oe.opt()]), sems["cOe"], 1)

        wait("gpsimd", sPE, fc_pe_last)
        for k in range(32):
            mark(C.g.dma_start(out=sh2T_sb[:, k, :], in_=dd["sh2T"][k * 128:(k + 1) * 128, :]), sG, 16)
        sh2_ld = sG.n

        phase_barrier()
        _scope("sh2p4")
        comb_done = {-1: 0}
        arow2 = AR_X[:, 8192:16384].bitcast(F32)
        Arows = [h_t.rearrange("p a n -> p (a n)"), arow2[:, 0:D]]
        Brows = [p_t.rearrange("p a n -> p (a n)"), arow2[:, D:2 * D]]
        gAB = {}

        def issue_rowgather(c):
            bi = c % 2
            mark(C.g.indirect_dma_start(out=Arows[bi], out_offset=None, in_=cout_oe,
                                        in_offset=IOA(rAu_all[:, c:c + 1], 0)), sG, 16)
            mark(C.g.indirect_dma_start(out=Brows[bi], out_offset=None, in_=cout_oe,
                                        in_offset=IOA(rBu_all[:, c:c + 1], 0)), sG, 16)
            gAB[c] = sG.n

        wait("gpsimd", sems["cOe"], 1)
        wait("gpsimd", sV, p4pre_done)
        issue_rowgather(0)
        issue_rowgather(1)
        for chunk in range(TC // 128):
            hb2 = hsh2[chunk % 2]
            wait("sync", sPE, evac.get(("hsh2", chunk % 2), 0))
            wait("sync", sS, hsh_written)
            wait("sync", sPE, fc_pe_last)
            mark(C.s.dma_start(out=hb2[:],
                               in_=hsh_sp.rearrange("(k p) t -> p k t", p=128)[:, :, chunk * 128:(chunk + 1) * 128]), sS, 16)
            h2_ld = sS.n
            wait("tensor", sS, h2_ld)
            wait("tensor", sG, sh2_ld)
            pwb = pw[chunk % 2]
            wait("tensor", sV, evac.get(("pw2", chunk % 2), 0))
            if ("pw3", chunk % 2) in evac:
                wait("tensor", sV, evac[("pw3", chunk % 2)])
            mm = None
            for k in range(32):
                for nn_ in range(2):
                    i = C.t.matmul(pwb[:, nn_ * 512:(nn_ + 1) * 512], hb2[:, k, :],
                                   sh2T_sb[:, k, nn_ * 512:(nn_ + 1) * 512],
                                   start=(k == 0), stop=(k == 31))
                    mm = i if i is not None else mm
            pe_n = mark(mm, sPE, 1)
            evac[("hsh2", chunk % 2)] = pe_n

            wait("vector", sPE, pe_n)
            wait("vector", sG, rep_ld)
            wait("vector", sS, oe_written)
            shb = sht[chunk % 2]
            evac[("pw3", chunk % 2)] = swv(C.v.tensor_add(shb[:], pwb[:], sh2b_rp[:]))
            wait("vector", sG, gAB[chunk])
            C.v.scalar_tensor_tensor(r1_tok[:], Arows[chunk % 2], wABe[:, chunk:chunk + 1],
                                     xm_tok[:, chunk, :], ALU.mult, ALU.add)
            C.v.scalar_tensor_tensor(r1_tok[:], Brows[chunk % 2], wABe[:, 16 + chunk:17 + chunk],
                                     r1_tok[:], ALU.mult, ALU.add)
            mark(C.v.tensor_add(r1_tok[:], r1_tok[:], shb[:]), sV, 1)
            comb_done[chunk] = sV.n
            r2_n = sV.n

            if chunk + 2 < TC // 128:
                wait("gpsimd", sV, comb_done[chunk])
                issue_rowgather(chunk + 2)

            wait("scalar", sV, r2_n)
            mark(C.a.activation(out=lnsq[:], in_=r1_tok[:], func=AF.Copy,
                                accum_out=stat[:, 0:1]), sA, 1)
            a_n = mark(C.a.activation(out=lnsq[:], in_=r1_tok[:], func=AF.Square,
                                      accum_out=stat[:, 1:2]), sA, 1)
            wait("vector", sA, a_n)
            swv(C.v.tensor_scalar_mul(stat[:, 2:3], stat[:, 0:1], 1.0 / D))
            swv(C.v.tensor_scalar_mul(stat[:, 3:4], stat[:, 1:2], 1.0 / D))
            swv(C.v.tensor_mul(stat[:, 4:5], stat[:, 2:3], stat[:, 2:3]))
            swv(C.v.tensor_sub(stat[:, 3:4], stat[:, 3:4], stat[:, 4:5]))
            mark(C.v.tensor_scalar(stat[:, 3:4], stat[:, 3:4], EPS, None, ALU.add), sV, 1)
            var_n = sV.n
            wait("scalar", sV, var_n)
            sd_n = mark(C.a.activation(out=stat[:, 5:6], in_=stat[:, 3:4], func=AF.Sqrt), sA, 1)
            wait("vector", sA, sd_n)
            swv(C.v.reciprocal(stat[:, 6:7], stat[:, 5:6]))
            if chunk > 0:
                wait("vector", sS, outw)
            mark(C.v.tensor_scalar(otok[:], r1_tok[:], stat[:, 2:3], stat[:, 6:7],
                                   ALU.subtract, ALU.mult), sV, 1)
            o_n = sV.n
            wait("sync", sV, o_n)
            mark(C.s.dma_start(out=out[chunk * 128:(chunk + 1) * 128, :], in_=otok[:]), sS, 16)
            outw = sS.n

        if debug_stage == "fulldbg":
            wait("sync", sV, sV.n)
            wait("sync", sPE, sPE.n)
            mark(C.s.dma_start(out=dbg_xe, in_=AR_X[:, 0:6144]), sS, 16)
            mark(C.s.dma_start(out=dbg_he, in_=AR_X[:, 6144:14336]), sS, 16)
            mark(C.s.dma_start(out=dbg_xall, in_=cout_xm), sS, 16)
            mark(C.s.dma_start(out=dbg_tok, in_=tokf[:]), sS, 16)
            mark(C.s.dma_start(out=dbg_pair, in_=cout_pair), sS, 16)
            mark(C.s.dma_start(out=dbg_slot, in_=cout_slot), sS, 16)
            mark(C.s.dma_start(out=dbg_oe, in_=cin_oe), sS, 16)
            mark(C.s.dma_start(out=dbg_xm, in_=cin_xm), sS, 16)
            mark(C.s.dma_start(out=dbg_hsh, in_=hsh_sp[:, 0:128]), sS, 16)

        wait("gpsimd", sS, sS.n)
        wait("gpsimd", sG, sG.n)
        if _cur_scope[0] is not None:
            pn, pid = _cur_scope[0]
            C.nc.leave_named_scope(pn, pid, False)
        return

    def walk(name):
        C.cur = name
        C.reset()
        program()

    with nc.Block() as block:
        block.sync(lambda e: walk("sync"))
        block.gpsimd(lambda e: walk("gpsimd"))
        block.vector(lambda e: walk("vector"))
        block.scalar(lambda e: walk("scalar"))
        block.tensor(lambda e: walk("tensor"))
    ctx.close()
    return nc


def kernel(**inputs):
    maps = _prep_inputs(inputs)
    nc = _build()
    res = run_bass_kernel_spmd(nc, maps, core_ids=list(range(NCORE)))
    outs = [res.results[c]["out"] for c in range(NCORE)]
    return np.stack(outs).reshape(B, T, D).astype(np.float32)


# version 09 (final)
# speedup vs baseline: 1.2495x; 1.0470x over previous
import contextlib

import numpy as np
import ml_dtypes
import concourse.bass as bass
import concourse.mybir as mybir
from concourse.bass_utils import run_bass_kernel_spmd
from bass_rust import IndirectOffsetOnAxis as IOA

F32 = mybir.dt.float32
BF16 = mybir.dt.bfloat16
U32 = mybir.dt.uint32
AF = mybir.ActivationFunctionType
ALU = mybir.AluOpType
AX = mybir.AxisListType

NCORE = 8
B, T, D = 2, 8192, 1024
TC = 2048
ST = 512
NTILE = TC // ST
G = 8
E, CAP, EH, SHD = 8, 5120, 2048, 4096
NTOK = B * T
EPS = 1e-5
RG = [list(range(NCORE))]
SENT_ROW = float(E * CAP)

ENGINES = ("sync", "gpsimd", "vector", "scalar", "tensor")


class Sem:
    __slots__ = ("h", "n")

    def __init__(self, h):
        self.h = h
        self.n = 0


class EngineProxy:
    def __init__(self, ctx, name):
        self._ctx = ctx
        self._name = name

    def __getattr__(self, op):
        ctx, name = self._ctx, self._name

        def f(*a, **kw):
            if ctx.cur == name:
                return getattr(ctx.eng[name], op)(*a, **kw)
            return None

        return f


class Ctx:
    def __init__(self, nc):
        self.nc = nc
        self.eng = {
            "sync": nc.sync, "gpsimd": nc.gpsimd, "vector": nc.vector,
            "scalar": nc.scalar, "tensor": nc.tensor,
        }
        self.cur = None
        self.s = EngineProxy(self, "sync")
        self.g = EngineProxy(self, "gpsimd")
        self.v = EngineProxy(self, "vector")
        self.a = EngineProxy(self, "scalar")
        self.t = EngineProxy(self, "tensor")
        self.sems = []

    def sem(self, handle):
        s = Sem(handle)
        self.sems.append(s)
        return s

    def reset(self):
        for s in self.sems:
            s.n = 0

    @staticmethod
    def mark(instr, sem, k=1):
        sem.n += k
        if instr is not None:
            instr.then_inc(sem.h, k)
        return sem.n

    def wait(self, eng_name, sem, val=None):
        if self.cur == eng_name:
            self.eng[eng_name].wait_ge(sem.h, sem.n if val is None else val)


def _chan2d(v):
    return np.ascontiguousarray(np.asarray(v).reshape(-1, 128).T)


def _prep_inputs(inp):
    x = np.asarray(inp["L0_attn_input_bxtxd"])
    maps = []
    shared = {
        "w_x": np.ascontiguousarray(inp["L0_attn_w_x"]),
        "gi_w": np.ascontiguousarray(inp["L0_attn_gi_w"]),
        "ga_w": np.ascontiguousarray(inp["L0_attn_ga_w"]),
        "w_y": np.ascontiguousarray(inp["L0_attn_w_y"]),
        "bx2d": _chan2d(inp["L0_attn_b_x"]),
        "by2d": _chan2d(inp["L0_attn_b_y"]),
        "cb2d": _chan2d(inp["L0_attn_conv_b"]),
        "gib2d": _chan2d(inp["L0_attn_gi_b"]),
        "gab2d": _chan2d(inp["L0_attn_ga_b"]),
        "ap2d": _chan2d(inp["L0_attn_a_param"]),
        "cw2d": np.ascontiguousarray(
            np.asarray(inp["L0_attn_conv_w"]).reshape(4, G, 128)
            .transpose(2, 1, 0).reshape(128, G * 4)),
        "projT": np.ascontiguousarray(np.asarray(inp["L0_ffn_proj_w"]).T),
        "projb_rep": np.ascontiguousarray(
            np.tile(np.asarray(inp["L0_ffn_proj_b"])[None, :], (128, 1))),
        "ebias_rep": np.ascontiguousarray(
            np.tile(np.asarray(inp["L0_ffn_expert_bias"])[None, :], (128, 1))),
        "sh1T": np.ascontiguousarray(np.asarray(inp["L0_ffn_sh1_w"]).T),
        "sh1b2d": _chan2d(inp["L0_ffn_sh1_b"]),
        "sh2T": np.ascontiguousarray(np.asarray(inp["L0_ffn_sh2_w"]).T),
        "sh2b_rep": np.ascontiguousarray(
            np.tile(np.asarray(inp["L0_ffn_sh2_b"])[None, :], (128, 1))),
        "idn": np.eye(128, dtype=np.float32),
        "idnb": np.eye(128, dtype=ml_dtypes.bfloat16),
        "iota8": np.tile(np.arange(8, dtype=np.float32)[None, :], (128, 1)),
        "iota8p": np.tile(np.arange(8, dtype=np.float32)[None, :] + 1000.0, (128, 1)),
        "iota_ref": (np.arange(128)[:, None] * 256
                     + np.concatenate([2 * np.arange(128), 2 * np.arange(128) + 1])[None, :]
                     ).astype(np.float32),
        "iota_j": np.tile(np.arange(256, dtype=np.float32)[None, :], (128, 1)),
        "iota_slot": (np.arange(40)[None, :] * 128
                      + np.arange(128)[:, None]).astype(np.float32),
        "iota16x256": np.tile((np.arange(16, dtype=np.float32) * 256)[None, :], (128, 1)),
    }
    fc1 = np.asarray(inp["L0_ffn_fc1_w"])
    fc1b = np.asarray(inp["L0_ffn_fc1_b"])
    fc2 = np.asarray(inp["L0_ffn_fc2_w"])
    fc2b = np.asarray(inp["L0_ffn_fc2_b"])

    for c in range(NCORE):
        b, k = c // 4, c % 4
        t0 = k * TC
        xs = np.zeros((TC + 3, D), np.float32)
        lo = max(t0 - 3, 0)
        xs[3 - (t0 - lo):] = x[b, lo:t0 + TC]
        hv = 0.0 if k == 0 else 1.0
        msk = np.zeros((128, 3), np.float32)
        msk[:, :k] = 1.0
        consts = np.concatenate([
            np.full((128, 1), hv, np.float32),
            np.full((128, 1), float(b), np.float32),
            msk,
            1.0 - msk,
            np.full((128, 1), float(c), np.float32),
            np.full((128, 1), float(c * CAP), np.float32),
            np.arange(128, dtype=np.float32).reshape(128, 1),
            np.full((128, 1), 1.0 - hv, np.float32),
        ], axis=1).astype(np.float32)
        m = dict(shared)
        m.update({
            "xT": np.ascontiguousarray(xs.T),
            "x_tok": np.ascontiguousarray(x[b, t0:t0 + TC]),
            "fc1T": np.ascontiguousarray(fc1[c].T),
            "fc1b2d": _chan2d(fc1b[c]),
            "fc2T": np.ascontiguousarray(fc2[c].T),
            "fc2b_rep": np.ascontiguousarray(np.tile(fc2b[c][None, :], (128, 1))),
            "consts": consts,
        })
        maps.append(m)
    return maps


def _build(debug_stage=None):
    nc = bass.Bass("TRN2", target_bir_lowering=False, debug=False)
    C = Ctx(nc)

    dd = {}

    def din(name, shape, dt=F32):
        dd[name] = nc.dram_tensor(name, shape, dt, kind="ExternalInput").ap()

    din("xT", [D, TC + 3]); din("x_tok", [TC, D])
    for n in ("w_x", "gi_w", "ga_w", "w_y"):
        din(n, [D, D])
    for n in ("bx2d", "by2d", "cb2d", "gib2d", "gab2d", "ap2d"):
        din(n, [128, G])
    din("cw2d", [128, G * 4])
    din("projT", [D, E]); din("projb_rep", [128, E]); din("ebias_rep", [128, E])
    din("sh1T", [D, SHD]); din("sh1b2d", [128, SHD // 128])
    din("sh2T", [SHD, D]); din("sh2b_rep", [128, D])
    din("fc1T", [D, EH]); din("fc1b2d", [128, EH // 128])
    din("fc2T", [EH, D]); din("fc2b_rep", [128, D])
    din("idn", [128, 128])
    din("idnb", [128, 128], BF16)
    din("iota8", [128, E]); din("iota8p", [128, E])
    din("iota_ref", [128, 256]); din("iota_j", [128, 256]); din("iota_slot", [128, 40])
    din("iota16x256", [128, 16])
    din("consts", [128, 12])

    out = nc.dram_tensor("out", [TC, D], F32, kind="ExternalOutput").ap()
    if debug_stage == "fulldbg":
        dbg_xe = nc.dram_tensor("dbg_xe", [128, 6144], BF16, kind="ExternalOutput").ap()
        dbg_he = nc.dram_tensor("dbg_he", [128, 8192], BF16, kind="ExternalOutput").ap()
        dbg_xall = nc.dram_tensor("dbg_xall", [NTOK, D], BF16, kind="ExternalOutput").ap()
        dbg_tok = nc.dram_tensor("dbg_tok", [128, 40], F32, kind="ExternalOutput").ap()
        dbg_pair = nc.dram_tensor("dbg_pair", [4 * NCORE, TC], F32, kind="ExternalOutput").ap()
        dbg_sel2 = nc.dram_tensor("dbg_sel2", [128, 2200], F32, kind="ExternalOutput").ap()
        dbg_slot = nc.dram_tensor("dbg_slot", [NCORE, 128 * 256], F32, kind="ExternalOutput").ap()
        dbg_oe = nc.dram_tensor("dbg_oe", [CAP, D], F32, kind="ExternalOutput").ap()
        dbg_xm = nc.dram_tensor("dbg_xm", [TC, D], BF16, kind="ExternalOutput").ap()
        dbg_hsh = nc.dram_tensor("dbg_hsh", [SHD, 128], BF16, kind="ExternalOutput").ap()

    h_sp = nc.dram_tensor("h_sp", [G, 128, TC], F32).ap()
    p_sp = nc.dram_tensor("p_sp", [G, 128, TC], F32).ap()
    cin_carry = nc.dram_tensor("cin_carry", [2, D], F32).ap()
    cout_carry = nc.dram_tensor("cout_carry", [2 * NCORE, D], F32).ap()
    cin_xm = nc.dram_tensor("cin_xm", [TC, D], BF16).ap()
    cout_xm = nc.dram_tensor("cout_xm", [NTOK, D], BF16).ap()
    cin_pair = nc.dram_tensor("cin_pair", [4, TC], F32).ap()
    cout_pair = nc.dram_tensor("cout_pair", [4 * NCORE, TC], F32).ap()
    cnt_dram = nc.dram_tensor("cnt_dram", [128], F32).ap()
    base_dram = nc.dram_tensor("base_dram", [130], F32).ap()
    padded_dram = nc.dram_tensor("padded_dram", [128 * 256], F32).ap()
    cin_slot = nc.dram_tensor("cin_slot", [1, 128 * 256], F32).ap()
    cout_slot = nc.dram_tensor("cout_slot", [NCORE, 128 * 256], F32).ap()
    cin_oes = [nc.dram_tensor(f"cin_oe{s5}", [1024, D], F32).ap() for s5 in range(5)]
    sh2_sp = nc.dram_tensor("sh2_sp", [TC, D], F32).ap()
    fence_in = nc.dram_tensor("fence_in", [1, 16], F32).ap()
    fence_out = [nc.dram_tensor(f"fence_out{i}", [NCORE, 16], F32).ap() for i in range(5)]
    cin_oe = None
    cout_oe = nc.dram_tensor("cout_oe", [NCORE * CAP, D], F32).ap()
    hsh_sp = nc.dram_tensor("hsh_sp", [SHD, TC], BF16).ap()

    ctx = contextlib.ExitStack()

    def sbuf(name, shape, dt=F32):
        return ctx.enter_context(nc.sbuf_tensor(name, shape, dt))

    def psum(name, shape, dt=F32):
        return ctx.enter_context(nc.psum_tensor(name, shape, dt))

    AR_W = sbuf("AR_W", [128, 32768], BF16)
    AR_A = sbuf("AR_A", [128, 16384], F32)
    AR_X = sbuf("AR_X", [128, 16384], BF16)
    AR_B = sbuf("AR_B", [128, 4120], F32)

    def vw(ar, lo, n, g=None):
        a = ar[:, lo:lo + n]
        if g is not None:
            a = a.rearrange("p (g n) -> p g n", g=g)
        return a

    wx_sb = vw(AR_W, 0, 8192, G)
    wgi_sb = vw(AR_W, 8192, 8192, G)
    wga_sb = vw(AR_W, 16384, 8192, G)
    gx = vw(AR_A, 0, 4096, G)
    aamu = vw(AR_A, 4096, 4096, G)
    av = vw(AR_A, 8192, 4096, G)
    x2f = vw(AR_A, 12288, 4096, G)
    xm_tok = AR_A[:, :].rearrange("p (c n) -> p c n", c=TC // 128)
    x_fmb = vw(AR_X, 0, 16384, G)
    x1e = AR_B[:, 0:4120].rearrange("p (g n) -> p g n", g=G)
    y_fm = vw(AR_B, 0, 4096, G)

    halob = sbuf("halob", [128, G, 3], BF16)
    x1h3 = sbuf("x1h3", [128, G, 3], F32)
    x2b = sbuf("x2b", [128, G, ST], BF16)
    h_t = sbuf("h_t", [128, 2, ST], F32)
    p_t = sbuf("p_t", [128, 2, ST], F32)
    hc = sbuf("hc", [128, G], F32)
    pc = sbuf("pc", [128, G], F32)
    zeros512 = sbuf("zeros512", [128, ST], F32)
    car = sbuf("car", [128, 2 * NCORE, G], F32)
    h_in = sbuf("h_in", [128, G], F32)
    Hb = [sbuf(f"Hb_{i}", [128, G], F32) for i in range(2)]
    tmp8a = sbuf("tmp8a", [128, G], F32)
    tmp8b = sbuf("tmp8b", [128, G], F32)
    cexp = sbuf("cexp", [128, G], F32)
    idn_sb = sbuf("idn_sb", [128, 128], F32)
    bx_sb = sbuf("bx_sb", [128, G], F32)
    by_sb = sbuf("by_sb", [128, G], F32)
    cb_sb = sbuf("cb_sb", [128, G], F32)
    gib_sb = sbuf("gib_sb", [128, G], F32)
    gab_sb = sbuf("gab_sb", [128, G], F32)
    ap_sb = sbuf("ap_sb", [128, G], F32)
    cw_sb = sbuf("cw_sb", [128, G * 4], F32)
    cst = sbuf("cst", [128, 12], F32)
    xf2 = sbuf("xf2", [128, 2, ST], F32)
    xm_bf = sbuf("xm_bf", [128, D], BF16)
    stat = sbuf("stat", [128, 8], F32)
    x2b_f32 = x2b.rearrange("p g n -> p (g n)").bitcast(F32)
    lnsq = x2b_f32[:, 0:D]
    r1_tok = x2b_f32[:, D:2 * D]

    projT_sb = sbuf("projT_sb", [128, G, 8], BF16)
    pjb_sb = sbuf("pjb_sb", [128, 8], F32)
    eb_sb = sbuf("eb_sb", [128, 8], F32)
    io8_sb = sbuf("io8_sb", [128, 8], F32)
    io8p_sb = sbuf("io8p_sb", [128, 8], F32)
    idnb_sb = sbuf("idnb_sb", [128, 128], BF16)
    e1a = sbuf("e1a", [128, 16], F32)
    e2a = sbuf("e2a", [128, 16], F32)
    w1a = sbuf("w1a", [128, 16], F32)
    w2a = sbuf("w2a", [128, 16], F32)
    zall = AR_B[:, 0:128].rearrange("p (a b) -> p a b", b=8)
    sgall = AR_B[:, 128:256].rearrange("p (a b) -> p a b", b=8)
    sgal2 = AR_B[:, 256:384].rearrange("p (a b) -> p a b", b=8)
    zt = AR_B[:, 384:448].rearrange("p (a b) -> p a b", b=4)
    m1a = sbuf("m1a", [128, 16], F32)
    m2a = sbuf("m2a", [128, 16], F32)
    sel4m = sbuf("sel4m", [128, 4], F32)
    cnt4 = sbuf("cnt4", [128, 4], F32)
    tot4 = sbuf("tot4", [128, 4], F32)
    ge4 = sbuf("ge4", [128, 4], F32)
    sva_all = sbuf("sva_all", [128, 16], F32)
    svb_all = sbuf("svb_all", [128, 16], F32)
    rAu_all = sbuf("rAu_all", [128, 16], U32)
    rBu_all = sbuf("rBu_all", [128, 16], U32)
    ixu_all = sbuf("ixu_all", [128, 32], U32)
    kAB = sbuf("kAB", [128, 32], F32)
    wABe = sbuf("wABe", [128, 32], F32)
    eA = eB = wAt = wBt = None
    fc1b_sb = sbuf("fc1b_sb", [128, 16], F32)
    sh1b_sb = sbuf("sh1b_sb", [128, 32], F32)
    hshb = sbuf("hshb", [128, 2, 512], BF16)

    arB = AR_B
    def bview(lo, n, g=None):
        a = arB[:, lo:lo + n]
        if g is not None:
            a = a.rearrange("p (g n) -> p g n", g=g)
        return a
    wm = bview(0, 256)
    eA = bview(768, 128)
    eB = bview(896, 128)
    wAt = bview(1024, 128)
    wBt = bview(1152, 128)
    io_ref = bview(256, 256)
    io_j = bview(512, 256)
    pos = bview(768, 256)
    rexc = bview(1024, 256)
    arr_ = bview(1280, 256)
    delt = bview(1536, 256)
    dshs = bview(1792, 256)
    selb = bview(2048, 256)
    bitu = sbuf("bitu_t", [128, 256], U32)
    mskio = bview(2304, 256)
    tokv = bview(2560, 256)
    eqm = bview(2816, 256)
    keptb = bview(3072, 256)
    rowm = bview(3328, 256)
    slotm = bview(3584, 256)
    fc2b_rp = bview(0, 1024)
    sh2b_rp = bview(1024, 1024)
    oeb = [bview(2048, 1024), bview(3072, 1024)]
    sht = [bview(0, 1024), bview(2048, 1024)]
    otok = bview(3072, 1024)

    xf2v = xf2.rearrange("p a n -> p (a n)")
    cntb = xf2v[:, 0:128]
    pr16 = xf2v[:, 0:128]
    base_bc = xf2v[:, 128:256]
    srcx = xf2v[:, 256:296]
    tokf = xf2v[:, 296:336]
    srcu = sbuf("srcu", [128, 40], U32)
    toku = sbuf("toku", [128, 40], U32)
    io_slot = xf2v[:, 416:456]
    io16x = xf2v[:, 748:764]
    _sc = {}
    for _i, _n in enumerate(("lo", "hi", "mid", "ge", "neq", "asg", "ov", "theta",
                             "cntgt", "m_p", "t1", "t2", "t3", "cnt1", "ctot", "bcol")):
        _sc[_n] = xf2v[:, 456 + _i:457 + _i]
    def sv(n):
        return _sc[n]
    mrow = xf2v[0:1, 472:600]
    brow = xf2v[0:1, 600:728]
    _p4 = {}
    for _i, _n in enumerate(("fo", "ixA", "ixB", "svA", "svB", "kA", "kB",
                             "wAe", "wBe")):
        _p4[_n] = xf2v[:, 728 + _i:729 + _i]
    def p4(n):
        return _p4[n]
    idxu4 = sbuf("idxu4", [128, 4], U32)
    ixAu = idxu4[:, 0:1]
    ixBu = idxu4[:, 1:2]
    rAu = idxu4[:, 2:3]
    rBu = idxu4[:, 3:4]
    sh1T_sb = AR_W[:, 0:32768].rearrange("p (g n) -> p g n", g=G)
    fc1T_sb = AR_W[:, 0:16384].rearrange("p (g n) -> p g n", g=G)
    fc2T_sb = AR_W[:, 16384:32768].rearrange("p (g n) -> p g n", g=16)
    sh2T_sb = AR_W[:, 0:32768].rearrange("p (g n) -> p g n", g=32)
    xeg = AR_X[:, 0:2048].rearrange("p (a n) -> p a n", a=2)
    xe_fm = AR_X[:, 2048:6144].rearrange("p (g n) -> p g n", g=G)
    he = AR_X[:, 6144:14336].rearrange("p (g n) -> p g n", g=16)
    hsh2 = [AR_X[:, 0:4096].rearrange("p (g n) -> p g n", g=32),
            AR_X[:, 4096:8192].rearrange("p (g n) -> p g n", g=32)]
    cout_slot_flat = cout_slot.rearrange("r n -> (r n)")

    ps = [psum(f"ps{i}", [128, ST], F32) for i in range(3)]
    pw = [psum(f"pw{i}", [128, D], F32) for i in range(2)]
    pb16 = psum("pb16", [128, D], BF16)

    sems = {}
    for name in ("sS", "sG", "sV", "sA", "sPE", "cC1", "cC2a", "cC2b",
                 "cSlot", "cOe"):
        sems[name] = C.sem(ctx.enter_context(nc.semaphore(name)))
    sS, sG, sV, sA, sPE = (sems[k] for k in ("sS", "sG", "sV", "sA", "sPE"))
    cC1 = sems["cC1"]
    cC2a = sems["cC2a"]
    cC2b = sems["cC2b"]

    CID = {"hv": 0, "mb": 1, "msk": 2, "msk1m": 5, "mye": 8, "ecap": 9,
           "iota128": 10, "hv1m": 11}

    def cstc(name, off=0):
        i = CID[name] + off
        return cst[:, i:i + 1]

    xT_r = dd["xT"].rearrange("(g p) t -> p g t", p=128)

    def program():
        mark, wait = C.mark, C.wait
        evac = {}

        def swv(i):
            n = mark(i, sV, 1)
            wait("vector", sV, n)
            return n

        _cur_scope = [None]

        def _scope(name):
            if _cur_scope[0] is not None:
                pn, pid = _cur_scope[0]
                C.nc.leave_named_scope(pn, pid, False)
            sid, _ = C.nc.enter_named_scope(name, False)
            _cur_scope[0] = (name, sid)

        def phase_barrier():
            for e in ENGINES:
                for sem in (sS, sG, sV, sA, sPE):
                    wait(e, sem, sem.n)

        XMT = False
        _scope("setup")
        for t_, d_ in [(idn_sb, dd["idn"]), (bx_sb, dd["bx2d"]), (by_sb, dd["by2d"]),
                       (cb_sb, dd["cb2d"]), (gib_sb, dd["gib2d"]), (gab_sb, dd["gab2d"]),
                       (ap_sb, dd["ap2d"]), (cw_sb, dd["cw2d"]), (cst, dd["consts"]),
                       (idnb_sb, dd["idnb"])]:
            mark(C.s.dma_start(out=t_[:], in_=d_), sS, 16)
        consts_ld = sS.n

        for wsb, wd in [(wx_sb, dd["w_x"]), (wgi_sb, dd["gi_w"]), (wga_sb, dd["ga_w"])]:
            for k in range(G):
                mark(C.g.dma_start(out=wsb[:, k, :], in_=wd[k * 128:(k + 1) * 128, :]), sG, 16)
        attn_w_ld = sG.n

        mark(C.g.dma_start(out=x_fmb[:, :, 0:ST], in_=xT_r[:, :, 3:3 + ST]), sG, 16)
        x_ld = {0: sG.n}
        mark(C.g.dma_start(out=halob[:], in_=xT_r[:, :, 0:3]), sG, 16)
        halo_ld = sG.n

        wait("scalar", sS, consts_ld)
        wait("vector", sS, consts_ld)
        a_n = mark(C.a.activation(out=tmp8a[:], in_=ap_sb[:], func=AF.Abs), sA, 1)
        wait("scalar", sA, a_n)
        a_n = mark(C.a.activation(out=tmp8b[:], in_=tmp8a[:], func=AF.Exp, scale=-1.0), sA, 1)
        wait("scalar", sA, a_n)
        mark(C.a.activation(out=tmp8a[:], in_=tmp8b[:], func=AF.Ln, bias=1.0), sA, 1)
        ln_done = sA.n
        swv(C.v.tensor_scalar_max(cexp[:], ap_sb[:], 0.0))
        wait("vector", sA, ln_done)
        swv(C.v.tensor_add(cexp[:], cexp[:], tmp8a[:]))
        swv(C.v.tensor_scalar_mul(cexp[:], cexp[:], -8.0))
        C.v.memset(zeros512[:], 0.0)
        C.v.memset(hc[:], 0.0)
        mark(C.v.memset(pc[:], 1.0), sV, 1)
        derived = sV.n

        _scope("attn1")
        scans_done = {-1: 0}
        gates_pe = {-1: 0}
        spill = {}
        last_x1_pe = 0
        for it in range(NTILE):
            if it + 1 < NTILE:
                mark(C.g.dma_start(out=x_fmb[:, :, (it + 1) * ST:(it + 2) * ST],
                                   in_=xT_r[:, :, 3 + (it + 1) * ST:3 + (it + 2) * ST]), sG, 16)
                x_ld[it + 1] = sG.n

            wait("scalar", sV, scans_done[it - 1])
            if it == 0:
                wait("scalar", sV, derived)

            if it == 0:
                wait("tensor", sG, attn_w_ld)
                wait("tensor", sG, halo_ld)
                for m in range(G):
                    mm = None
                    for k in range(G):
                        i = C.t.matmul(ps[0][:, 0:3], wx_sb[:, k, m * 128:(m + 1) * 128],
                                       halob[:, k, :], start=(k == 0), stop=(k == G - 1))
                        mm = i if i is not None else mm
                    pe_n = mark(mm, sPE, 1)
                    wait("scalar", sPE, pe_n)
                    a_n = mark(C.a.activation(out=x1e[:, m, 0:3], in_=ps[0][:, 0:3],
                                              func=AF.Identity, bias=bx_sb[:, m:m + 1]), sA, 1)
                    wait("tensor", sA, a_n)
                wait("vector", sA, sA.n)
                swv(C.v.tensor_scalar_mul(x1e[:, :, 0:3], x1e[:, :, 0:3], cstc("hv")))
                evac[0] = sA.n
            else:
                swv(C.v.tensor_copy(out=x1e[:, :, 0:3], in_=x1h3[:]))

            wait("tensor", sG, x_ld[it])
            wait("tensor", sG, attn_w_ld)
            wait("scalar", sV, scans_done[it - 1])
            for m in range(G):
                bank = m % 3
                if bank in evac:
                    wait("tensor", sA, evac[bank])
                mm = None
                for k in range(G):
                    i = C.t.matmul(ps[bank][:], wx_sb[:, k, m * 128:(m + 1) * 128],
                                   x_fmb[:, k, it * ST:(it + 1) * ST],
                                   start=(k == 0), stop=(k == G - 1))
                    mm = i if i is not None else mm
                pe_n = mark(mm, sPE, 1)
                wait("scalar", sPE, pe_n)
                evac[bank] = mark(C.a.activation(out=x1e[:, m, 3:3 + ST], in_=ps[bank][:],
                                                 func=AF.Identity, bias=bx_sb[:, m:m + 1]), sA, 1)
            x1_done = sA.n
            last_x1_pe = sPE.n

            wait("vector", sA, x1_done)
            for g in range(G):
                C.v.tensor_scalar(x2f[:, g], x1e[:, g, 3:3 + ST],
                                  cw_sb[:, g * 4 + 3:g * 4 + 4], cb_sb[:, g:g + 1],
                                  ALU.mult, ALU.add)
                for i3 in range(3):
                    C.v.scalar_tensor_tensor(x2f[:, g], x1e[:, g, i3:i3 + ST],
                                             cw_sb[:, g * 4 + i3:g * 4 + i3 + 1],
                                             x2f[:, g], ALU.mult, ALU.add)
            C.v.tensor_copy(out=x1h3[:], in_=x1e[:, :, ST:ST + 3])
            wait("vector", sPE, gates_pe[it - 1])
            mark(C.v.tensor_copy(out=x2b[:], in_=x2f[:]), sV, 1)
            x2_ready = sV.n

            wait("tensor", sV, x2_ready)
            for wsb, dstbuf, bias in ((wgi_sb, gx, gib_sb), (wga_sb, aamu, gab_sb)):
                for m in range(G):
                    bank = m % 3
                    wait("tensor", sA, evac[bank])
                    mm = None
                    for k in range(G):
                        i = C.t.matmul(ps[bank][:], wsb[:, k, m * 128:(m + 1) * 128],
                                       x2b[:, k, :], start=(k == 0), stop=(k == G - 1))
                        mm = i if i is not None else mm
                    pe_n = mark(mm, sPE, 1)
                    wait("scalar", sPE, pe_n)
                    evac[bank] = mark(C.a.activation(out=dstbuf[:, m], in_=ps[bank][:],
                                                     func=AF.Sigmoid, bias=bias[:, m:m + 1]), sA, 1)
            gates_pe[it] = sPE.n

            wait("scalar", sA, sA.n)
            laste = None
            for g in range(G):
                i = C.a.activation(out=av[:, g], in_=aamu[:, g], func=AF.Exp,
                                   scale=cexp[:, g:g + 1])
                laste = i if i is not None else laste
            a_n = mark(laste, sA, 1)
            wait("scalar", sA, a_n)
            mark(C.a.activation(out=aamu[:], in_=av[:], func=AF.Square), sA, 1)
            sq_done = sA.n
            wait("vector", sA, sq_done)
            C.v.tensor_scalar(aamu[:], aamu[:], -1.0, 1.0, ALU.mult, ALU.add)
            mark(C.v.tensor_scalar_max(aamu[:], aamu[:], 0.0), sV, 1)
            mu_pre = sV.n
            wait("scalar", sV, mu_pre)
            mark(C.a.activation(out=aamu[:], in_=aamu[:], func=AF.Sqrt), sA, 1)
            mu_done = sA.n

            wait("vector", sA, mu_done)
            if it == 0:
                swv(C.v.tensor_scalar_mul(av[:, :, 0:1], av[:, :, 0:1], cstc("hv")))
                swv(C.v.tensor_scalar(aamu[:, :, 0:1], aamu[:, :, 0:1], cstc("hv"),
                                      cstc("hv1m"), ALU.mult, ALU.add))

            if debug_stage == "p1t0" and it == 0:
                mark(C.v.nop(), sV, 1)
                wait("sync", sV, sV.n)
                wait("sync", sA, sA.n)
                for g in range(G):
                    mark(C.s.dma_start(out=out[g * 128:(g + 1) * 128, 0:512], in_=x1e[:, g, 3:3 + ST]), sS, 16)
                    mark(C.s.dma_start(out=out[g * 128:(g + 1) * 128, 512:1024], in_=gx[:, g]), sS, 16)
                    mark(C.s.dma_start(out=out[1024 + g * 128:1024 + (g + 1) * 128, 0:512], in_=av[:, g]), sS, 16)
                    mark(C.s.dma_start(out=out[1024 + g * 128:1024 + (g + 1) * 128, 512:1024], in_=aamu[:, g]), sS, 16)
                wait("gpsimd", sS, sS.n)
                wait("gpsimd", sG, sG.n)
                return
            C.v.tensor_mul(x2f[:], gx[:], x2f[:])
            C.v.tensor_mul(x2f[:], x2f[:], aamu[:])

            for g in range(G):
                gi = (it * G + g) % 2
                if (("sp", gi) in spill):
                    wait("vector", sS, spill[("sp", gi)])
                C.v.tensor_tensor_scan(h_t[:, gi], av[:, g], x2f[:, g],
                                       hc[:, g:g + 1], ALU.mult, ALU.add)
                swv(C.v.tensor_copy(out=hc[:, g:g + 1], in_=h_t[:, gi, ST - 1:ST]))
                C.v.tensor_tensor_scan(p_t[:, gi], av[:, g], zeros512[:],
                                       pc[:, g:g + 1], ALU.mult, ALU.add)
                v_n = swv(C.v.tensor_copy(out=pc[:, g:g + 1], in_=p_t[:, gi, ST - 1:ST]))
                wait("sync", sV, v_n)
                mark(C.s.dma_start(out=h_sp[g, :, it * ST:(it + 1) * ST], in_=h_t[:, gi]), sS, 16)
                mark(C.s.dma_start(out=p_sp[g, :, it * ST:(it + 1) * ST], in_=p_t[:, gi]), sS, 16)
                spill[("sp", gi)] = sS.n
            scans_done[it] = sV.n

        wait("gpsimd", sPE, last_x1_pe)
        for k in range(G):
            mark(C.g.dma_start(out=wx_sb[:, k, :], in_=dd["w_y"][k * 128:(k + 1) * 128, :]), sG, 16)
        wy_ld = sG.n

        wait("gpsimd", sV, derived)
        mark(C.g.dma_start(out=fence_in, in_=zeros512[0:1, 0:16]), sG, 16)
        phase_barrier()
        _scope("carry")
        wait("gpsimd", sV, scans_done[NTILE - 1])
        mark(C.g.dma_start(out=cin_carry[0:1, :].rearrange("a (p g) -> (a p) g", p=128),
                           in_=pc[:]), sG, 16)
        mark(C.g.dma_start(out=cin_carry[1:2, :].rearrange("a (p g) -> (a p) g", p=128),
                           in_=hc[:]), sG, 16)
        wait("gpsimd", sG, sG.n)
        mark(C.g.collective_compute("AllGather", ALU.bypass, replica_groups=RG,
                                    ins=[cin_carry.opt()], outs=[cout_carry.opt()]), cC1, 1)

        def emit_fence(i, sem):
            mark(C.g.collective_compute("AllGather", ALU.bypass, replica_groups=RG,
                                        ins=[fence_in.opt()], outs=[fence_out[i].opt()]),
                 sem, 1)

        emit_fence(0, cC1)
        wait("gpsimd", cC1, 2)
        for q in range(4):
            mark(C.g.dma_start(
                out=car[:, q * 4:(q + 1) * 4, :],
                in_=cout_carry.rearrange("r (p g) -> p r g", p=128)[:, q * 4:(q + 1) * 4, :]),
                sG, 16)
        carry_ld = sG.n

        wait("vector", sG, carry_ld)
        for bb in range(2):
            swv(C.v.memset(Hb[bb][:], 0.0))
            for s_ in range(3):
                r = 4 * bb + s_
                swv(C.v.tensor_scalar(tmp8a[:], car[:, 2 * r, :], cstc("msk", s_),
                                      cstc("msk1m", s_), ALU.mult, ALU.add))
                swv(C.v.tensor_scalar_mul(tmp8b[:], car[:, 2 * r + 1, :], cstc("msk", s_)))
                swv(C.v.tensor_mul(Hb[bb][:], Hb[bb][:], tmp8a[:]))
                swv(C.v.tensor_add(Hb[bb][:], Hb[bb][:], tmp8b[:]))
        swv(C.v.tensor_scalar_mul(h_in[:], Hb[1][:], cstc("mb")))
        swv(C.v.scalar_tensor_tensor(tmp8a[:], Hb[0][:], cstc("mb"), Hb[0][:],
                                     ALU.mult, ALU.subtract))
        mark(C.v.tensor_sub(h_in[:], h_in[:], tmp8a[:]), sV, 1)
        hin_done = sV.n

        if debug_stage == "carry":
            wait("sync", sV, hin_done)
            mark(C.s.dma_start(out=out[0:128, 0:8], in_=h_in[:]), sS, 16)
            mark(C.s.dma_start(out=out[128:256, 0:8], in_=Hb[0][:]), sS, 16)
            mark(C.s.dma_start(out=out[128:256, 8:16], in_=Hb[1][:]), sS, 16)
            mark(C.s.dma_start(out=out[128:256, 16:28], in_=cst[:]), sS, 16)
            mark(C.s.dma_start(out=out[128:256, 28:36], in_=tmp8a[:]), sS, 16)
            mark(C.s.dma_start(out=out[0:128, 8:136], in_=car[:].rearrange("p r g -> p (r g)")), sS, 16)
            mark(C.s.dma_start(out=out[0:128, 136:144], in_=hc[:]), sS, 16)
            mark(C.s.dma_start(out=out[0:128, 144:152], in_=pc[:]), sS, 16)
            for g in range(G):
                mark(C.s.dma_start(out=out[256 + g * 128:256 + (g + 1) * 128, 0:512], in_=h_sp[g, :, 0:512]), sS, 16)
            wait("gpsimd", sS, sS.n)
            wait("gpsimd", sG, sG.n)
            return

        phase_barrier()
        _scope("attn2")
        wait("sync", sS, sS.n)
        r1_done = {-1: 0}
        tr_pe = {-1: 0}
        hp_ld = {}
        xm_chunk = {}
        pb16_ev = {-1: 0}
        xmdma = {}
        for it in range(NTILE):
            wait("tensor", sG, wy_ld)
            wait("scalar", sV, r1_done[it - 1])
            wait("scalar", sPE, tr_pe[it - 1])
            wait("scalar", sV, scans_done[NTILE - 1] if it == 0 else 0)
            for m in range(G):
                bank = m % 3
                wait("tensor", sA, evac[bank])
                mm = None
                for k in range(G):
                    i = C.t.matmul(ps[bank][:], wx_sb[:, k, m * 128:(m + 1) * 128],
                                   x_fmb[:, k, it * ST:(it + 1) * ST],
                                   start=(k == 0), stop=(k == G - 1))
                    mm = i if i is not None else mm
                pe_n = mark(mm, sPE, 1)
                wait("scalar", sPE, pe_n)
                evac[bank] = mark(C.a.activation(out=y_fm[:, m], in_=ps[bank][:],
                                                 func=AF.Gelu, bias=by_sb[:, m:m + 1]), sA, 1)
            y_done = sA.n

            wait("vector", sA, y_done)
            wait("vector", sV, hin_done)
            lastr = None
            for g in range(G):
                j = it * G + g
                gi = j % 2
                if j - 2 in hp_ld:
                    wait("sync", sV, hp_ld[j - 2])
                mark(C.s.dma_start(out=h_t[:, gi], in_=h_sp[g, :, it * ST:(it + 1) * ST]), sS, 16)
                mark(C.s.dma_start(out=p_t[:, gi], in_=p_sp[g, :, it * ST:(it + 1) * ST]), sS, 16)
                mark(C.s.dma_start(out=xf2[:, gi], in_=xT_r[:, g, 3 + it * ST:3 + (it + 1) * ST]), sS, 16)
                s_n = sS.n
                wait("vector", sS, s_n)
                C.v.scalar_tensor_tensor(h_t[:, gi], p_t[:, gi], h_in[:, g:g + 1],
                                         h_t[:, gi], ALU.mult, ALU.add)
                C.v.tensor_mul(y_fm[:, g], y_fm[:, g], h_t[:, gi])
                i = C.v.tensor_add(y_fm[:, g], y_fm[:, g], xf2[:, gi])
                lastr = i if i is not None else lastr
                hp_ld[j] = mark(lastr, sV, 1)
            r1_done[it] = sV.n

            if debug_stage == "p2" and it == 0:
                wait("sync", sV, r1_done[0])
                for g in range(G):
                    mark(C.s.dma_start(out=out[g * 128:(g + 1) * 128, 0:512], in_=y_fm[:, g]), sS, 16)
                mark(C.s.dma_start(out=out[1024:1152, 0:512], in_=h_t[:, 0]), sS, 16)
                mark(C.s.dma_start(out=out[1152:1280, 0:512], in_=h_t[:, 1]), sS, 16)
                mark(C.s.dma_start(out=out[1280:1408, 0:8], in_=h_in[:]), sS, 16)
                wait("gpsimd", sS, sS.n)
                wait("gpsimd", sG, sG.n)
                return
            for cch in range(ST // 128):
                chunk = it * (ST // 128) + cch
                pb = pw[chunk % 2]
                wait("tensor", sV, r1_done[it])
                if ("pw", chunk % 2) in evac:
                    wait("tensor", sV, evac[("pw", chunk % 2)])
                mm = None
                for g in range(G):
                    i = C.t.transpose(pb[:, g * 128:(g + 1) * 128],
                                      y_fm[:, g, cch * 128:(cch + 1) * 128], idn_sb[:])
                    mm = i if i is not None else mm
                pe_n = mark(mm, sPE, 1)
                wait("vector", sPE, pe_n)
                if chunk > 0:
                    wait("vector", sA, xm_chunk[chunk - 1])
                evac[("pw", chunk % 2)] = mark(C.v.tensor_copy(out=r1_tok[:], in_=pb[:]), sV, 1)
                tr_done = sV.n

                wait("scalar", sV, tr_done)
                mark(C.a.activation(out=lnsq[:], in_=r1_tok[:], func=AF.Copy,
                                    accum_out=stat[:, 0:1]), sA, 1)
                a_n = mark(C.a.activation(out=lnsq[:], in_=r1_tok[:], func=AF.Square,
                                          accum_out=stat[:, 1:2]), sA, 1)
                xm_chunk[chunk] = a_n
                wait("vector", sA, a_n)
                swv(C.v.tensor_scalar_mul(stat[:, 2:3], stat[:, 0:1], 1.0 / D))
                swv(C.v.tensor_scalar_mul(stat[:, 3:4], stat[:, 1:2], 1.0 / D))
                swv(C.v.tensor_mul(stat[:, 4:5], stat[:, 2:3], stat[:, 2:3]))
                swv(C.v.tensor_sub(stat[:, 3:4], stat[:, 3:4], stat[:, 4:5]))
                mark(C.v.tensor_scalar(stat[:, 3:4], stat[:, 3:4], EPS, None, ALU.add), sV, 1)
                var_n = sV.n
                wait("scalar", sV, var_n)
                sd_n = mark(C.a.activation(out=stat[:, 5:6], in_=stat[:, 3:4], func=AF.Sqrt), sA, 1)
                wait("vector", sA, sd_n)
                swv(C.v.reciprocal(stat[:, 6:7], stat[:, 5:6]))
                C.v.tensor_scalar(xm_tok[:, chunk, :], r1_tok[:],
                                  stat[:, 2:3], stat[:, 6:7], ALU.subtract, ALU.mult)
                wait("vector", sS, xmdma.get(chunk - 1, 0))
                mark(C.v.tensor_copy(out=xm_bf[:], in_=xm_tok[:, chunk, :]), sV, 1)
                xmb_n = sV.n
                wait("sync", sV, xmb_n)
                mark(C.s.dma_start(out=cin_xm[chunk * 128:(chunk + 1) * 128, :],
                                   in_=xm_bf[:]), sS, 16)
                xmdma[chunk] = sS.n
                if XMT:
                    wait("tensor", sV, xmb_n)
                    if chunk == 0:
                        wait("tensor", sS, consts_ld)
                    wait("tensor", sV, pb16_ev[chunk - 1])
                    mm = None
                    for g in range(G):
                        i = C.t.transpose(pb16[:, g * 128:(g + 1) * 128],
                                          xm_bf[:, g * 128:(g + 1) * 128], idnb_sb[:])
                        mm = i if i is not None else mm
                    xmt_pe = mark(mm, sPE, 1)
                    wait("vector", sPE, xmt_pe)
                    pb16_ev[chunk] = swv(C.v.tensor_copy(
                        out=x_fmb[:, :, chunk * 128:(chunk + 1) * 128],
                        in_=pb16.rearrange("p (g n) -> p g n", g=G)))
            tr_pe[it] = sPE.n

        if debug_stage == "xm":
            wait("sync", sV, sV.n)
            for chunk in range(TC // 128):
                mark(C.s.dma_start(out=out[chunk * 128:(chunk + 1) * 128, :],
                                   in_=xm_tok[:, chunk, :]), sS, 16)
            wait("gpsimd", sS, sS.n)
            wait("gpsimd", sG, sG.n)
            return

        phase_barrier()
        _scope("routing")
        y_pe_last = tr_pe[NTILE - 1]
        xm_fm = x_fmb

        for k in range(G):
            mark(C.g.dma_start(out=projT_sb[:, k, :], in_=dd["projT"][k * 128:(k + 1) * 128, :]), sG, 16)
        for t_, d_ in [(pjb_sb, dd["projb_rep"]), (eb_sb, dd["ebias_rep"]),
                       (io8_sb, dd["iota8"]), (io8p_sb, dd["iota8p"])]:
            mark(C.g.dma_start(out=t_[:], in_=d_), sG, 16)
        rconsts_ld = sG.n

        xm_all_written = sS.n
        wait("gpsimd", sS, xm_all_written)
        mark(C.g.collective_compute("AllGather", ALU.bypass, replica_groups=RG,
                                    ins=[cin_xm.opt()], outs=[cout_xm.opt()]), cC2a, 1)
        emit_fence(1, cC2a)

        if XMT:
            xmfm_done = ("v", pb16_ev[TC // 128 - 1])
            evac[("pb16",)] = pb16_ev[TC // 128 - 1]
        else:
            wait("sync", sPE, y_pe_last)
            wait("sync", sS, xm_all_written)
            for g in range(G):
                for cq in range(4):
                    mark(C.s.dma_start_transpose(
                        xm_fm[:, g, cq * 512:(cq + 1) * 512],
                        cin_xm[cq * 512:(cq + 1) * 512, g * 128:(g + 1) * 128]), sS, 16)
            xmfm_done = ("s", sS.n)

        def wait_xmfm(eng):
            wait(eng, sV if xmfm_done[0] == "v" else sS, xmfm_done[1])

        wait_xmfm("tensor")
        wait("tensor", sG, rconsts_ld)
        wait("tensor", sA, evac.get(1, 0))
        mm = None
        for chunk in range(TC // 128):
            for k in range(G):
                i = C.t.matmul(ps[1][:, chunk * 8:(chunk + 1) * 8],
                               xm_fm[:, k, chunk * 128:(chunk + 1) * 128],
                               projT_sb[:, k, :], start=(k == 0), stop=(k == G - 1))
                mm = i if i is not None else mm
        pe_n = mark(mm, sPE, 1)
        wait("vector", sPE, pe_n)
        evac[("z",)] = swv(C.v.tensor_copy(out=zall[:].rearrange("p a b -> p (a b)"),
                                           in_=ps[1][:, 0:128]))

        def b8(t2d):
            return t2d.rearrange("p (a b) -> p a b", a=1).to_broadcast([128, 16, 8])

        def bc(t2d):
            return t2d.rearrange("p (a b) -> p a b", b=1).to_broadcast([128, 16, 8])

        def max8(srcT, dstT, op):
            swv(C.v.tensor_tensor(zt[:], srcT[:, :, 0:4], srcT[:, :, 4:8], op))
            swv(C.v.tensor_tensor(zt[:, :, 0:2], zt[:, :, 0:2], zt[:, :, 2:4], op))
            swv(C.v.tensor_tensor(dstT.rearrange("p (a b) -> p a b", b=1),
                                  zt[:, :, 0:1], zt[:, :, 1:2], op))

        swv(C.v.tensor_tensor(zall[:], zall[:], b8(pjb_sb), ALU.add))
        zn = sV.n
        wait("scalar", sV, zn)
        a_n = mark(C.a.activation(out=sgall[:], in_=zall[:], func=AF.Sigmoid), sA, 1)
        wait("vector", sA, a_n)
        swv(C.v.tensor_tensor(sgall[:], sgall[:], b8(eb_sb), ALU.add))
        swv(C.v.tensor_scalar_max(sgall[:], sgall[:], 1e-12))
        max8(sgall, m1a, ALU.max)
        swv(C.v.tensor_tensor(sgal2[:], sgall[:], bc(m1a), ALU.is_equal))
        swv(C.v.scalar_tensor_tensor(sgal2[:], sgal2[:], -1000.0, b8(io8p_sb), ALU.mult, ALU.add))
        max8(sgal2, e1a.rearrange("p n -> p n"), ALU.min)
        swv(C.v.tensor_tensor(sgal2[:], b8(io8_sb), bc(e1a), ALU.is_equal))
        swv(C.v.scalar_tensor_tensor(sgal2[:], sgal2[:], -4.0, sgall[:], ALU.mult, ALU.add))
        max8(sgal2, m2a, ALU.max)
        swv(C.v.tensor_tensor(sgal2[:], sgal2[:], bc(m2a), ALU.is_equal))
        swv(C.v.scalar_tensor_tensor(sgal2[:], sgal2[:], -1000.0, b8(io8p_sb), ALU.mult, ALU.add))
        max8(sgal2, e2a.rearrange("p n -> p n"), ALU.min)
        swv(C.v.tensor_add(w1a[:], m1a[:], m2a[:]))
        swv(C.v.reciprocal(w2a[:], w1a[:]))
        swv(C.v.tensor_mul(w1a[:], m1a[:], w2a[:]))
        swv(C.v.tensor_mul(w2a[:], m2a[:], w2a[:]))
        route_done = sV.n

        pair_dma = 0
        for ri, arrq in enumerate((e1a, e2a, w1a, w2a)):
            wait("tensor", sV, route_done)
            wait("tensor", sV, evac.get(("pair",), 0))
            wait("tensor", sV, evac.get(("z",), 0))
            mm = C.t.transpose(ps[1][0:16, 0:128], arrq[:], idn_sb[:])
            pe_n = mark(mm, sPE, 1)
            wait("vector", sPE, pe_n)
            wait("vector", sS, pair_dma)
            evac[("pair",)] = swv(C.v.tensor_copy(out=pr16[0:16, :], in_=ps[1][0:16, 0:128]))
            wait("sync", sV, sV.n)
            mark(C.s.dma_start(out=cin_pair[ri:ri + 1, :].rearrange("a (c p) -> (a c) p", c=16),
                               in_=pr16[0:16, :]), sS, 16)
            pair_dma = sS.n
        pair_written = sS.n
        wait("gpsimd", sS, pair_written)
        mark(C.g.collective_compute("AllGather", ALU.bypass, replica_groups=RG,
                                    ins=[cin_pair.opt()], outs=[cout_pair.opt()]), cC2b, 1)
        emit_fence(2, cC2b)

        if debug_stage == "routing":
            wait("gpsimd", cC2b, 2)
            mark(C.g.dma_start(out=out[0:128, 0:16], in_=w1a[:]), sG, 16)
            mark(C.g.dma_start(out=out[0:128, 16:32], in_=w2a[:]), sG, 16)
            mark(C.g.dma_start(out=out[0:128, 32:48], in_=e1a[:]), sG, 16)
            mark(C.g.dma_start(out=out[0:128, 48:64], in_=e2a[:]), sG, 16)
            for r in range(32):
                mark(C.g.dma_start(out=out[256 + r * 2:256 + r * 2 + 1, 0:1024], in_=cout_pair[r:r + 1, 0:1024]), sG, 16)
                mark(C.g.dma_start(out=out[256 + r * 2 + 1:256 + r * 2 + 2, 0:1024], in_=cout_pair[r:r + 1, 1024:2048]), sG, 16)
            wait("gpsimd", sS, sS.n)
            wait("gpsimd", sG, sG.n)
            return

        wait("gpsimd", sPE, y_pe_last)
        for k in range(G):
            mark(C.g.dma_start(out=sh1T_sb[:, k, :], in_=dd["sh1T"][k * 128:(k + 1) * 128, :]), sG, 16)
        sh1_ld = sG.n
        mark(C.g.dma_start(out=sh1b_sb[:], in_=dd["sh1b2d"]), sG, 16)
        mark(C.g.dma_start(out=fc1b_sb[:], in_=dd["fc1b2d"]), sG, 16)
        smallb_ld = sG.n

        phase_barrier()
        _scope("sh1")
        wait("tensor", sV, evac.get(("z",), 0))
        wait("tensor", sV, evac.get(("pair",), 0))
        hsh_ev = {}
        for itl in range(NTILE):
            for m in range(SHD // 128):
                bank = m % 3
                wait("tensor", sG, sh1_ld)
                wait_xmfm("tensor")
                wait("tensor", sA, evac.get(bank, 0))
                mm = None
                for k in range(G):
                    i = C.t.matmul(ps[bank][:], sh1T_sb[:, k, m * 128:(m + 1) * 128],
                                   xm_fm[:, k, itl * ST:(itl + 1) * ST],
                                   start=(k == 0), stop=(k == G - 1))
                    mm = i if i is not None else mm
                pe_n = mark(mm, sPE, 1)
                wait("scalar", sPE, pe_n)
                wait("scalar", sG, smallb_ld)
                hb = m % 2
                if ("hshb", hb) in hsh_ev:
                    wait("scalar", sS, hsh_ev[("hshb", hb)])
                evac[bank] = mark(C.a.activation(out=hshb[:, hb], in_=ps[bank][:],
                                                 func=AF.Gelu, bias=sh1b_sb[:, m:m + 1]), sA, 1)
                wait("sync", sA, evac[bank])
                mark(C.s.dma_start(out=hsh_sp[m * 128:(m + 1) * 128, itl * ST:(itl + 1) * ST],
                                   in_=hshb[:, hb]), sS, 16)
                hsh_ev[("hshb", hb)] = sS.n
        sh1_pe_last = sPE.n
        hsh_written = sS.n

        _scope("selection")
        wait("gpsimd", cC2b, 2)
        for qi, dst in ((0, eA), (1, eB), (2, wAt), (3, wBt)):
            for r in range(NCORE):
                mark(C.g.dma_start(
                    out=dst[r * 16:(r + 1) * 16, :],
                    in_=cout_pair[4 * r + qi:4 * r + qi + 1, :]
                        .rearrange("a (i j) -> (a i) j", i=16)), sG, 16)
        ptiles_ld = sG.n
        wait("gpsimd", sV, route_done)
        mark(C.g.dma_start(out=io_ref[:], in_=dd["iota_ref"]), sG, 16)
        mark(C.g.dma_start(out=io_j[:], in_=dd["iota_j"]), sG, 16)
        mark(C.g.dma_start(out=io_slot[:], in_=dd["iota_slot"]), sG, 16)
        mark(C.g.dma_start(out=io16x[:], in_=dd["iota16x256"]), sG, 16)
        sel_ld = sG.n

        wait("vector", sG, ptiles_ld)
        wait("vector", sG, sel_ld)
        for half, (eT, wT) in enumerate(((eA, wAt), (eB, wBt))):
            sl = slice(half * 128, (half + 1) * 128)
            swv(C.v.tensor_scalar(selb[:, sl], eT[:], cstc("mye"), None, ALU.is_equal))
            swv(C.v.tensor_mul(wm[:, sl], wT[:], selb[:, sl]))
            swv(C.v.tensor_add(wm[:, sl], wm[:, sl], selb[:, sl]))
            swv(C.v.tensor_scalar(wm[:, sl], wm[:, sl], -1.0, None, ALU.add))

        swv(C.v.memset(dshs[:], 1.0))
        swv(C.v.memset(cnt4[:], 0.0))

        def probes(src, op, ncols):
            lastp = None
            for q in range(ncols):
                i = C.v.tensor_scalar(selb[:], src, sel4m[:, q:q + 1], None, op,
                                      ALU.add, accum_out=cnt4[:, q:q + 1])
                lastp = i if i is not None else lastp
            n = mark(lastp, sV, 1)
            wait("vector", sV, n)
            wait("tensor", sV, n)
            wait("tensor", sV, evac.get(("sel",), 0))
            wait("tensor", sA, evac.get(2, 0))
            mm = C.t.matmul(ps[2][:, 0:4], dshs[:, 0:128], cnt4[:], start=True, stop=True)
            pe_n = mark(mm, sPE, 1)
            wait("vector", sPE, pe_n)
            evac[("sel",)] = swv(C.v.tensor_copy(out=tot4[:], in_=ps[2][:, 0:4]))

        def upd_lohi(ncols, pred_ge_keeps_lo):
            for q in range(ncols):
                qq = q if pred_ge_keeps_lo else ncols - 1 - q
                g_ = ge4[:, qq:qq + 1]
                m_ = sel4m[:, qq:qq + 1]
                if pred_ge_keeps_lo:
                    swv(C.v.tensor_sub(sv("t1"), m_, sv("lo")))
                    swv(C.v.tensor_mul(sv("t1"), sv("t1"), g_))
                    swv(C.v.tensor_add(sv("lo"), sv("lo"), sv("t1")))
                else:
                    swv(C.v.tensor_sub(sv("t1"), m_, sv("hi")))
                    swv(C.v.tensor_mul(sv("t1"), sv("t1"), g_))
                    swv(C.v.tensor_add(sv("hi"), sv("hi"), sv("t1")))

        swv(C.v.memset(sel4m[:, 0:1], -0.5))
        probes(wm[:], ALU.is_gt, 1)
        swv(C.v.tensor_copy(out=sv("asg"), in_=tot4[:, 0:1]))
        swv(C.v.memset(sv("lo"), 1e-14))
        swv(C.v.memset(sv("hi"), 2.0))
        for rnd in range(16):
            if rnd < 6:
                swv(C.v.tensor_mul(sv("t1"), sv("lo"), sv("hi")))
                n = sV.n
                wait("scalar", sV, n)
                a_n = mark(C.a.activation(out=sel4m[:, 1:2], in_=sv("t1"), func=AF.Sqrt), sA, 1)
                wait("vector", sA, a_n)
                swv(C.v.tensor_mul(sv("t1"), sv("lo"), sel4m[:, 1:2]))
                swv(C.v.tensor_mul(sv("t2"), sel4m[:, 1:2], sv("hi")))
                n = sV.n
                wait("scalar", sV, n)
                C.a.activation(out=sel4m[:, 0:1], in_=sv("t1"), func=AF.Sqrt)
                a_n = mark(C.a.activation(out=sel4m[:, 2:3], in_=sv("t2"), func=AF.Sqrt), sA, 1)
                wait("vector", sA, a_n)
            else:
                swv(C.v.tensor_sub(sv("t1"), sv("hi"), sv("lo")))
                swv(C.v.tensor_scalar_mul(sv("t1"), sv("t1"), 0.25))
                swv(C.v.tensor_add(sel4m[:, 0:1], sv("lo"), sv("t1")))
                swv(C.v.tensor_add(sel4m[:, 1:2], sel4m[:, 0:1], sv("t1")))
                swv(C.v.tensor_add(sel4m[:, 2:3], sel4m[:, 1:2], sv("t1")))
            probes(wm[:], ALU.is_gt, 3)
            swv(C.v.tensor_scalar(ge4[:, 0:3], tot4[:, 0:3], float(CAP), None, ALU.is_ge))
            upd_lohi(3, True)
            swv(C.v.tensor_scalar(ge4[:, 0:3], ge4[:, 0:3], -1.0, 1.0, ALU.mult, ALU.add))
            upd_lohi(3, False)
        swv(C.v.tensor_scalar(sv("ov"), sv("asg"), float(CAP), None, ALU.is_gt))
        swv(C.v.tensor_mul(sv("t1"), sv("ov"), sv("hi")))
        swv(C.v.tensor_scalar(sv("t2"), sv("ov"), 1.0, -1.0, ALU.mult, ALU.add))
        swv(C.v.tensor_add(sv("theta"), sv("t1"), sv("t2")))
        swv(C.v.tensor_copy(out=sel4m[:, 0:1], in_=sv("theta")))
        probes(wm[:], ALU.is_gt, 1)
        swv(C.v.tensor_copy(out=sv("cntgt"), in_=tot4[:, 0:1]))
        swv(C.v.tensor_scalar(sv("neq"), sv("cntgt"), -1.0, float(CAP), ALU.mult, ALU.add))
        swv(C.v.tensor_scalar(keptb[:], wm[:], sv("theta"), None, ALU.is_gt))
        swv(C.v.tensor_scalar(eqm[:], wm[:], sv("theta"), None, ALU.is_equal))
        swv(C.v.tensor_scalar(selb[:], eqm[:], -1e9, 1e9, ALU.mult, ALU.add))
        swv(C.v.tensor_mul(mskio[:], eqm[:], io_ref[:]))
        swv(C.v.tensor_add(mskio[:], mskio[:], selb[:]))
        swv(C.v.memset(sv("lo"), -1.0))
        swv(C.v.memset(sv("hi"), 32768.0))
        for rnd in range(9):
            swv(C.v.tensor_sub(sv("t1"), sv("hi"), sv("lo")))
            swv(C.v.tensor_scalar_mul(sv("t1"), sv("t1"), 0.25))
            swv(C.v.tensor_add(sel4m[:, 0:1], sv("lo"), sv("t1")))
            swv(C.v.tensor_add(sel4m[:, 1:2], sel4m[:, 0:1], sv("t1")))
            swv(C.v.tensor_add(sel4m[:, 2:3], sel4m[:, 1:2], sv("t1")))
            probes(mskio[:], ALU.is_le, 3)
            swv(C.v.tensor_tensor(ge4[:, 0:3], tot4[:, 0:3],
                                  sv("neq").to_broadcast([128, 3]), ALU.is_ge))
            upd_lohi(3, False)
            swv(C.v.tensor_scalar(ge4[:, 0:3], ge4[:, 0:3], -1.0, 1.0, ALU.mult, ALU.add))
            upd_lohi(3, True)
        swv(C.v.tensor_scalar(selb[:], io_ref[:], sv("hi"), None, ALU.is_le))
        swv(C.v.tensor_mul(selb[:], selb[:], eqm[:]))
        swv(C.v.tensor_scalar(eqm[:], wm[:], -0.5, None, ALU.is_gt))
        swv(C.v.tensor_mul(selb[:], selb[:], eqm[:]))
        swv(C.v.tensor_add(keptb[:], keptb[:], selb[:]))
        swv(C.v.memset(dshs[:], 1.0))
        swv(C.v.tensor_tensor_scan(pos[:], dshs[:], keptb[:], 0.0, ALU.mult, ALU.add))
        swv(C.v.tensor_copy(out=sv("m_p"), in_=pos[:, 255:256]))
        swv(C.v.tensor_sub(rexc[:], pos[:], keptb[:]))
        wait("gpsimd", sV, sV.n)
        mark(C.g.dma_start(out=cnt_dram.rearrange("(p a) -> p a", a=1), in_=sv("m_p")), sG, 16)
        wait("gpsimd", sG, sG.n)
        mark(C.g.dma_start(out=mrow, in_=cnt_dram.rearrange("(a p) -> a p", a=1)), sG, 16)
        g_n = sG.n
        wait("vector", sG, g_n)
        swv(C.v.tensor_tensor_scan(brow, dshs[0:1, 0:128], mrow, 0.0, ALU.mult, ALU.add))
        swv(C.v.memset(sv("t3"), 0.0))
        v_n = sV.n
        wait("gpsimd", sV, v_n)
        mark(C.g.dma_start(out=base_dram[1:129].rearrange("(a p) -> a p", a=1), in_=brow), sG, 16)
        mark(C.g.dma_start(out=base_dram[0:1].rearrange("(a p) -> a p", a=1), in_=sv("t3")[0:1, :]), sG, 16)
        wait("gpsimd", sG, sG.n)
        mark(C.g.dma_start(out=sv("bcol"), in_=base_dram[0:128].rearrange("(p a) -> p a", a=1)), sG, 16)
        mark(C.g.dma_start(out=base_bc, in_=base_dram[0:128].rearrange("(a p) -> a p", a=1).to_broadcast([128, 128])), sG, 16)
        base_ld = sG.n
        wait("vector", sG, base_ld)
        swv(C.v.tensor_scalar(slotm[:], rexc[:], sv("bcol"), None, ALU.add))
        swv(C.v.tensor_scalar(selb[:], slotm[:], 1024.0, None, ALU.is_ge))
        swv(C.v.tensor_scalar(eqm[:], slotm[:], 2048.0, None, ALU.is_ge))
        swv(C.v.tensor_add(selb[:], selb[:], eqm[:]))
        swv(C.v.tensor_scalar(eqm[:], slotm[:], 3072.0, None, ALU.is_ge))
        swv(C.v.tensor_add(selb[:], selb[:], eqm[:]))
        swv(C.v.tensor_scalar(eqm[:], slotm[:], 4096.0, None, ALU.is_ge))
        swv(C.v.tensor_add(selb[:], selb[:], eqm[:]))
        swv(C.v.scalar_tensor_tensor(rowm[:], selb[:], 7168.0, slotm[:], ALU.mult, ALU.add))
        swv(C.v.tensor_scalar(sv("t1"), cstc("mye"), 1024.0, None, ALU.mult))
        swv(C.v.tensor_scalar(rowm[:], rowm[:], sv("t1"), -SENT_ROW, ALU.add, ALU.add))
        swv(C.v.tensor_mul(rowm[:], rowm[:], keptb[:]))
        swv(C.v.tensor_scalar(rowm[:], rowm[:], SENT_ROW, None, ALU.add))
        v_n = sV.n
        wait("gpsimd", sV, v_n)
        mark(C.g.dma_start(out=cin_slot[0:1, :].rearrange("a (p j) -> (a p) j", p=128),
                           in_=rowm[:]), sG, 16)
        wait("gpsimd", sG, sG.n)
        mark(C.g.collective_compute("AllGather", ALU.bypass, replica_groups=RG,
                                    ins=[cin_slot.opt()], outs=[cout_slot.opt()]), sems["cSlot"], 1)
        emit_fence(3, sems["cSlot"])
        swv(C.v.tensor_scalar_mul(tokv[:, 0:128], io_ref[:, 0:128], 0.5))
        swv(C.v.tensor_scalar(tokv[:, 128:256], io_ref[:, 128:256], 0.5, -0.5, ALU.mult, ALU.add))
        swv(C.v.tensor_scalar(arr_[:], tokv[:], -16384.0, None, ALU.add))
        swv(C.v.tensor_mul(arr_[:], arr_[:], keptb[:]))
        swv(C.v.tensor_scalar(arr_[:], arr_[:], 16384.0, None, ALU.add))
        swv(C.v.tensor_sub(delt[:], io_j[:], rexc[:]))
        for bbit in range(8):
            sft = 1 << bbit
            w_ = 256 - sft
            swv(C.v.tensor_scalar(mskio[:], delt[:], 0.5 / sft, None, ALU.mult))
            swv(C.v.tensor_copy(out=bitu[:], in_=mskio[:]))
            swv(C.v.tensor_copy(out=dshs[:], in_=bitu[:]))
            swv(C.v.tensor_sub(dshs[:], mskio[:], dshs[:]))
            swv(C.v.tensor_mul(dshs[:], dshs[:], dshs[:]))
            swv(C.v.tensor_scalar(dshs[:], dshs[:], 0.0625, None, ALU.is_ge))
            if debug_stage == "fulldbg" and bbit == 0:
                wait("gpsimd", sV, sV.n)
                mark(C.g.dma_start(out=dbg_sel2[:, 1300:1556], in_=mskio[:]), sG, 16)
                mark(C.g.dma_start(out=dbg_sel2[:, 1556:1812], in_=dshs[:]), sG, 16)
                mark(C.g.dma_start(out=dbg_sel2[:, 1812:2068], in_=delt[:]), sG, 16)
                wait("vector", sG, sG.n)
            swv(C.v.tensor_copy(out=bitu[:], in_=dshs[:]))
            swv(C.v.scalar_tensor_tensor(mskio[:], dshs[:], -float(sft), delt[:],
                                         ALU.mult, ALU.add))
            swv(C.v.select(arr_[:, 0:w_], bitu[:, sft:256], arr_[:, sft:256], arr_[:, 0:w_]))
            swv(C.v.select(delt[:, 0:w_], bitu[:, sft:256], mskio[:, sft:256], mskio[:, 0:w_]))
            if sft > 0:
                swv(C.v.tensor_copy(out=delt[:, w_:256], in_=mskio[:, w_:256]))
        swv(C.v.tensor_scalar(dshs[:], io_j[:], sv("m_p"), None, ALU.is_ge))
        swv(C.v.tensor_scalar(selb[:], arr_[:], -1.0, 16384.0, ALU.mult, ALU.add))
        swv(C.v.tensor_mul(selb[:], selb[:], dshs[:]))
        swv(C.v.tensor_add(arr_[:], arr_[:], selb[:]))
        v_n = sV.n
        wait("gpsimd", sV, v_n)
        mark(C.g.dma_start(out=padded_dram.rearrange("(p j) -> p j", p=128), in_=arr_[:]), sG, 16)
        padded_w = sG.n
        if debug_stage == "fulldbg":
            wait("gpsimd", sG, padded_w)
            mark(C.g.dma_start(out=dbg_sel2[:, 0:16], in_=xf2v[:, 456:472]), sG, 16)
            mark(C.g.dma_start(out=dbg_sel2[:, 16:272], in_=keptb[:]), sG, 16)
            mark(C.g.dma_start(out=dbg_sel2[:, 272:528], in_=arr_[:]), sG, 16)
            mark(C.g.dma_start(out=dbg_sel2[:, 528:784], in_=io_j[:]), sG, 16)
            mark(C.g.dma_start(out=dbg_sel2[:, 784:1040], in_=delt[:]), sG, 16)
            mark(C.g.dma_start(out=dbg_sel2[:, 1040:1296], in_=rexc[:]), sG, 16)
        wait("vector", sG, padded_w)
        big20 = bview(0, 2560).rearrange("p (c q) -> p c q", c=20)
        for hh in range(2):
            cols = slice(hh * 20, (hh + 1) * 20)
            base_b20 = base_bc.rearrange("p (a q) -> p a q", a=1).to_broadcast([128, 20, 128])
            ios_b = io_slot[:, cols].rearrange("p (c a) -> p c a", a=1).to_broadcast([128, 20, 128])
            swv(C.v.tensor_tensor(big20[:], base_b20, ios_b, ALU.is_le))
            w_ = 128
            while w_ > 1:
                w_ //= 2
                swv(C.v.tensor_tensor(big20[:, :, 0:w_], big20[:, :, 0:w_],
                                      big20[:, :, w_:2 * w_], ALU.add))
            swv(C.v.tensor_scalar(srcx[:, cols], big20[:, :, 0:1].rearrange("p c a -> p (c a)"),
                                  256.0, -256.0, ALU.mult, ALU.add))
            swv(C.v.tensor_tensor(big20[:], base_b20, ios_b, ALU.is_le))
            swv(C.v.tensor_tensor(big20[:], big20[:], base_b20, ALU.mult))
            w_ = 128
            while w_ > 1:
                w_ //= 2
                swv(C.v.tensor_tensor(big20[:, :, 0:w_], big20[:, :, 0:w_],
                                      big20[:, :, w_:2 * w_], ALU.max))
            swv(C.v.tensor_tensor(srcx[:, cols], srcx[:, cols], io_slot[:, cols], ALU.add))
            swv(C.v.tensor_tensor(srcx[:, cols], srcx[:, cols],
                                  big20[:, :, 0:1].rearrange("p c a -> p (c a)"), ALU.subtract))
        swv(C.v.tensor_scalar_min(srcx[:], srcx[:], 32767.0))
        swv(C.v.tensor_scalar_max(srcx[:], srcx[:], 0.0))
        swv(C.v.tensor_copy(out=srcu[:], in_=srcx[:]))
        v_n = sV.n
        wait("gpsimd", sV, v_n)
        wait("gpsimd", sG, padded_w)
        for c40 in range(40):
            mark(C.g.indirect_dma_start(out=tokf[:, c40:c40 + 1], out_offset=None,
                                        in_=padded_dram.rearrange("(n a) -> n a", a=1),
                                        in_offset=IOA(srcu[:, c40:c40 + 1], 0)), sG, 16)
        tok_g = sG.n
        wait("vector", sG, tok_g)
        swv(C.v.tensor_scalar_min(tokf[:], tokf[:], 16383.0))
        swv(C.v.tensor_copy(out=toku[:], in_=tokf[:]))
        tok_ready = sV.n

        if debug_stage == "sel":
            wait("sync", sV, tok_ready)
            mark(C.s.dma_start(out=out[0:128, 0:16], in_=xf2v[:, 456:472]), sS, 16)
            mark(C.s.dma_start(out=out[128:256, 0:256], in_=rowm[:]), sS, 16)
            mark(C.s.dma_start(out=out[384:512, 0:40], in_=tokf[:]), sS, 16)
            mark(C.s.dma_start(out=out[512:640, 0:256], in_=keptb[:]), sS, 16)
            mark(C.s.dma_start(out=out[640:768, 0:40], in_=srcx[:]), sS, 16)
            wait("gpsimd", sS, sS.n)
            wait("gpsimd", sG, sG.n)
            return

        phase_barrier()
        _scope("expert")
        wait("gpsimd", sPE, sh1_pe_last)
        for k in range(G):
            mark(C.g.dma_start(out=fc1T_sb[:, k, :], in_=dd["fc1T"][k * 128:(k + 1) * 128, :]), sG, 16)
        fc1_ld = sG.n
        for k in range(16):
            mark(C.g.dma_start(out=fc2T_sb[:, k, :], in_=dd["fc2T"][k * 128:(k + 1) * 128, :]), sG, 16)
        fc2_ld = sG.n
        wait("gpsimd", sV, tok_ready)
        wait("gpsimd", sG, padded_w)
        mark(C.g.dma_start(out=fc2b_rp[:], in_=dd["fc2b_rep"]), sG, 16)
        mark(C.g.dma_start(out=sh2b_rp[:], in_=dd["sh2b_rep"]), sG, 16)
        rep_ld = sG.n

        wait("vector", sG, sel_ld)
        swv(C.v.tensor_scalar(sv("t1"), cstc("mye"), 4096.0, None, ALU.mult))
        swv(C.v.tensor_tensor(wABe[:, 0:16], io16x[:],
                              cstc("iota128").to_broadcast([128, 16]), ALU.add))
        swv(C.v.tensor_scalar(wABe[:, 0:16], wABe[:, 0:16], sv("t1"), None, ALU.add))
        swv(C.v.scalar_tensor_tensor(kAB[:, 0:16], e1a[:], 32768.0, wABe[:, 0:16],
                                     ALU.mult, ALU.add))
        swv(C.v.scalar_tensor_tensor(kAB[:, 16:32], e2a[:], 32768.0, wABe[:, 0:16],
                                     ALU.mult, ALU.add))
        swv(C.v.tensor_scalar(kAB[:, 16:32], kAB[:, 16:32], 128.0, None, ALU.add))
        swv(C.v.tensor_copy(out=ixu_all[:], in_=kAB[:]))
        p4pre_v = sV.n
        wait("gpsimd", sV, p4pre_v)
        wait("gpsimd", sems["cSlot"], 2)
        csf = cout_slot_flat.rearrange("(n a) -> n a", a=1)
        for cq in range(16):
            mark(C.g.indirect_dma_start(out=sva_all[:, cq:cq + 1], out_offset=None,
                                        in_=csf, in_offset=IOA(ixu_all[:, cq:cq + 1], 0)), sG, 16)
            mark(C.g.indirect_dma_start(out=svb_all[:, cq:cq + 1], out_offset=None,
                                        in_=csf, in_offset=IOA(ixu_all[:, 16 + cq:17 + cq], 0)), sG, 16)
        slotg = sG.n
        wait("vector", sG, slotg)
        swv(C.v.tensor_scalar(kAB[:, 0:16], sva_all[:], SENT_ROW, None, ALU.is_lt))
        swv(C.v.tensor_scalar(kAB[:, 16:32], svb_all[:], SENT_ROW, None, ALU.is_lt))
        swv(C.v.tensor_scalar_min(sva_all[:], sva_all[:], SENT_ROW - 1.0))
        swv(C.v.tensor_scalar_min(svb_all[:], svb_all[:], SENT_ROW - 1.0))
        swv(C.v.tensor_copy(out=rAu_all[:], in_=sva_all[:]))
        swv(C.v.tensor_copy(out=rBu_all[:], in_=svb_all[:]))
        swv(C.v.tensor_mul(wABe[:, 0:16], w1a[:], kAB[:, 0:16]))
        swv(C.v.tensor_mul(wABe[:, 16:32], w2a[:], kAB[:, 16:32]))
        p4pre_done = sV.n

        oe_w = {}
        fc2_pe = {-1: 0}
        wait("tensor", sV, evac.get(("sel",), 0))
        for ce in range(CAP // 512):
            wait("gpsimd", cC2a, 2)
            wait("gpsimd", sV, tok_ready)
            for q4 in range(4):
                cc = ce * 4 + q4
                xg = xeg[:, cc % 2, :]
                if ("xeg", cc % 2) in evac:
                    wait("gpsimd", sPE, evac[("xeg", cc % 2)])
                mark(C.g.indirect_dma_start(out=xg, out_offset=None,
                                            in_=cout_xm,
                                            in_offset=IOA(toku[:, cc:cc + 1], 0)), sG, 16)
                g_n = sG.n
                wait("tensor", sG, g_n)
                wait("tensor", sG, rconsts_ld)
                wait("tensor", sV, evac.get(("pb16",), 0))
                mm = None
                for g in range(G):
                    i = C.t.transpose(pb16[:, g * 128:(g + 1) * 128],
                                      xg.rearrange("p (g n) -> p g n", g=G)[:, g, :],
                                      idnb_sb[:])
                    mm = i if i is not None else mm
                pe_n = mark(mm, sPE, 1)
                evac[("xeg", cc % 2)] = pe_n
                wait("vector", sPE, pe_n)
                evac[("pb16",)] = swv(C.v.tensor_copy(
                    out=xe_fm[:, :, q4 * 128:(q4 + 1) * 128],
                    in_=pb16.rearrange("p (g n) -> p g n", g=G)))
            xe_ready = sV.n
            wait("tensor", sG, fc1_ld)
            wait("tensor", sV, xe_ready)
            wait("scalar", sPE, fc2_pe[ce - 1])
            for m in range(16):
                bank = m % 3
                wait("tensor", sA, evac.get(bank, 0))
                mm = None
                for k in range(G):
                    i = C.t.matmul(ps[bank][:], fc1T_sb[:, k, m * 128:(m + 1) * 128],
                                   xe_fm[:, k, :], start=(k == 0), stop=(k == G - 1))
                    mm = i if i is not None else mm
                pe_n = mark(mm, sPE, 1)
                wait("scalar", sPE, pe_n)
                wait("scalar", sG, smallb_ld)
                evac[bank] = mark(C.a.activation(out=he[:, m, :], in_=ps[bank][:],
                                                 func=AF.Gelu, bias=fc1b_sb[:, m:m + 1]), sA, 1)
            he_done = sA.n
            wait("tensor", sG, fc2_ld)
            wait("tensor", sA, he_done)
            for m4 in range(4):
                pwb = pw[m4 % 2]
                if ("pw2", m4 % 2) in evac:
                    wait("tensor", sV, evac[("pw2", m4 % 2)])
                mm = None
                for k in range(16):
                    for nn_ in range(2):
                        i = C.t.matmul(pwb[:, nn_ * 512:(nn_ + 1) * 512],
                                       he[:, k, m4 * 128:(m4 + 1) * 128],
                                       fc2T_sb[:, k, nn_ * 512:(nn_ + 1) * 512],
                                       start=(k == 0), stop=(k == 15))
                        mm = i if i is not None else mm
                pe_n = mark(mm, sPE, 1)
                wait("vector", sPE, pe_n)
                wait("vector", sG, rep_ld)
                if ("oeb", m4 % 2) in oe_w:
                    wait("vector", sS, oe_w[("oeb", m4 % 2)])
                ob = oeb[m4 % 2]
                ev = swv(C.v.tensor_add(ob[:], pwb[:], fc2b_rp[:]))
                evac[("pw2", m4 % 2)] = ev
                wait("sync", sV, ev)
                r0 = (ce * 4 + m4) * 128 - (ce // 2) * 1024
                mark(C.s.dma_start(out=cin_oes[ce // 2][r0:r0 + 128, :],
                                   in_=ob[:]), sS, 16)
                oe_w[("oeb", m4 % 2)] = sS.n
            fc2_pe[ce] = sPE.n
        oe_written = sS.n
        fc_pe_last = sPE.n
        wait("gpsimd", sS, oe_written)
        for s5 in range(5):
            mark(C.g.collective_compute(
                "AllGather", ALU.bypass, replica_groups=RG,
                ins=[cin_oes[s5].opt()],
                outs=[cout_oe[s5 * 8192:(s5 + 1) * 8192, :].opt()]), sems["cOe"], 1)
        emit_fence(4, sems["cOe"])

        wait("gpsimd", sPE, fc_pe_last)
        for k in range(32):
            mark(C.g.dma_start(out=sh2T_sb[:, k, :], in_=dd["sh2T"][k * 128:(k + 1) * 128, :]), sG, 16)
        sh2_ld = sG.n

        phase_barrier()
        _scope("sh2p4")
        comb_done = {-1: 0}
        arow2 = AR_X[:, 8192:16384].bitcast(F32)
        Arows = [h_t.rearrange("p a n -> p (a n)"), arow2[:, 0:D]]
        Brows = [p_t.rearrange("p a n -> p (a n)"), arow2[:, D:2 * D]]
        gAB = {}

        def issue_rowgather(c):
            bi = c % 2
            mark(C.g.indirect_dma_start(out=Arows[bi], out_offset=None, in_=cout_oe,
                                        in_offset=IOA(rAu_all[:, c:c + 1], 0)), sG, 16)
            mark(C.g.indirect_dma_start(out=Brows[bi], out_offset=None, in_=cout_oe,
                                        in_offset=IOA(rBu_all[:, c:c + 1], 0)), sG, 16)
            gAB[c] = sG.n

        sht_w = {}
        for chunk in range(TC // 128):
            hb2 = hsh2[chunk % 2]
            wait("sync", sPE, evac.get(("hsh2", chunk % 2), 0))
            if chunk == 0:
                wait("sync", sS, hsh_written)
                wait("sync", sPE, fc_pe_last)
            mark(C.s.dma_start(out=hb2[:],
                               in_=hsh_sp.rearrange("(k p) t -> p k t", p=128)[:, :, chunk * 128:(chunk + 1) * 128]), sS, 16)
            h2_ld = sS.n
            wait("tensor", sS, h2_ld)
            wait("tensor", sG, sh2_ld)
            pwb = pw[chunk % 2]
            wait("tensor", sV, evac.get(("pw2", chunk % 2), 0))
            if ("pw3", chunk % 2) in evac:
                wait("tensor", sV, evac[("pw3", chunk % 2)])
            mm = None
            for k in range(32):
                for nn_ in range(2):
                    i = C.t.matmul(pwb[:, nn_ * 512:(nn_ + 1) * 512], hb2[:, k, :],
                                   sh2T_sb[:, k, nn_ * 512:(nn_ + 1) * 512],
                                   start=(k == 0), stop=(k == 31))
                    mm = i if i is not None else mm
            pe_n = mark(mm, sPE, 1)
            evac[("hsh2", chunk % 2)] = pe_n
            wait("vector", sPE, pe_n)
            if chunk == 0:
                wait("vector", sG, rep_ld)
            shb = sht[chunk % 2]
            wait("vector", sS, sht_w.get(chunk % 2, 0))
            evac[("pw3", chunk % 2)] = swv(C.v.tensor_add(shb[:], pwb[:], sh2b_rp[:]))
            wait("sync", sV, evac[("pw3", chunk % 2)])
            mark(C.s.dma_start(out=sh2_sp[chunk * 128:(chunk + 1) * 128, :], in_=shb[:]), sS, 16)
            sht_w[chunk % 2] = sS.n

        wait("gpsimd", sems["cOe"], 6)
        wait("gpsimd", sV, p4pre_done)
        issue_rowgather(0)
        issue_rowgather(1)
        rel_n = {}

        def issue_reload(c):
            wait("sync", sV, comb_done.get(c - 2, 0))
            mark(C.s.dma_start(out=sht[c % 2][:], in_=sh2_sp[c * 128:(c + 1) * 128, :]), sS, 16)
            rel_n[c] = sS.n

        issue_reload(0)
        issue_reload(1)
        for chunk in range(TC // 128):
            shb = sht[chunk % 2]
            wait("vector", sS, rel_n[chunk])
            wait("vector", sG, gAB[chunk])
            C.v.scalar_tensor_tensor(r1_tok[:], Arows[chunk % 2], wABe[:, chunk:chunk + 1],
                                     xm_tok[:, chunk, :], ALU.mult, ALU.add)
            C.v.scalar_tensor_tensor(r1_tok[:], Brows[chunk % 2], wABe[:, 16 + chunk:17 + chunk],
                                     r1_tok[:], ALU.mult, ALU.add)
            mark(C.v.tensor_add(r1_tok[:], r1_tok[:], shb[:]), sV, 1)
            comb_done[chunk] = sV.n
            r2_n = sV.n

            if chunk + 2 < TC // 128:
                wait("gpsimd", sV, comb_done[chunk])
                issue_rowgather(chunk + 2)
                issue_reload(chunk + 2)

            wait("scalar", sV, r2_n)
            mark(C.a.activation(out=lnsq[:], in_=r1_tok[:], func=AF.Copy,
                                accum_out=stat[:, 0:1]), sA, 1)
            a_n = mark(C.a.activation(out=lnsq[:], in_=r1_tok[:], func=AF.Square,
                                      accum_out=stat[:, 1:2]), sA, 1)
            wait("vector", sA, a_n)
            swv(C.v.tensor_scalar_mul(stat[:, 2:3], stat[:, 0:1], 1.0 / D))
            swv(C.v.tensor_scalar_mul(stat[:, 3:4], stat[:, 1:2], 1.0 / D))
            swv(C.v.tensor_mul(stat[:, 4:5], stat[:, 2:3], stat[:, 2:3]))
            swv(C.v.tensor_sub(stat[:, 3:4], stat[:, 3:4], stat[:, 4:5]))
            mark(C.v.tensor_scalar(stat[:, 3:4], stat[:, 3:4], EPS, None, ALU.add), sV, 1)
            var_n = sV.n
            wait("scalar", sV, var_n)
            sd_n = mark(C.a.activation(out=stat[:, 5:6], in_=stat[:, 3:4], func=AF.Sqrt), sA, 1)
            wait("vector", sA, sd_n)
            swv(C.v.reciprocal(stat[:, 6:7], stat[:, 5:6]))
            if chunk > 0:
                wait("vector", sS, outw)
            mark(C.v.tensor_scalar(otok[:], r1_tok[:], stat[:, 2:3], stat[:, 6:7],
                                   ALU.subtract, ALU.mult), sV, 1)
            o_n = sV.n
            wait("sync", sV, o_n)
            mark(C.s.dma_start(out=out[chunk * 128:(chunk + 1) * 128, :], in_=otok[:]), sS, 16)
            outw = sS.n

        if debug_stage == "fulldbg":
            wait("sync", sV, sV.n)
            wait("sync", sPE, sPE.n)
            mark(C.s.dma_start(out=dbg_xe, in_=AR_X[:, 0:6144]), sS, 16)
            mark(C.s.dma_start(out=dbg_he, in_=AR_X[:, 6144:14336]), sS, 16)
            mark(C.s.dma_start(out=dbg_xall, in_=cout_xm), sS, 16)
            mark(C.s.dma_start(out=dbg_tok, in_=tokf[:]), sS, 16)
            mark(C.s.dma_start(out=dbg_pair, in_=cout_pair), sS, 16)
            mark(C.s.dma_start(out=dbg_slot, in_=cout_slot), sS, 16)
            for s5 in range(5):
                mark(C.s.dma_start(out=dbg_oe[s5 * 1024:(s5 + 1) * 1024, :], in_=cin_oes[s5]), sS, 16)
            mark(C.s.dma_start(out=dbg_xm, in_=cin_xm), sS, 16)
            mark(C.s.dma_start(out=dbg_hsh, in_=hsh_sp[:, 0:128]), sS, 16)

        wait("gpsimd", sS, sS.n)
        wait("gpsimd", sG, sG.n)
        if _cur_scope[0] is not None:
            pn, pid = _cur_scope[0]
            C.nc.leave_named_scope(pn, pid, False)
        return

    def walk(name):
        C.cur = name
        C.reset()
        program()

    with nc.Block() as block:
        block.sync(lambda e: walk("sync"))
        block.gpsimd(lambda e: walk("gpsimd"))
        block.vector(lambda e: walk("vector"))
        block.scalar(lambda e: walk("scalar"))
        block.tensor(lambda e: walk("tensor"))
    ctx.close()
    return nc


def kernel(**inputs):
    maps = _prep_inputs(inputs)
    nc = _build()
    res = run_bass_kernel_spmd(nc, maps, core_ids=list(range(NCORE)))
    outs = [res.results[c]["out"] for c in range(NCORE)]
    return np.stack(outs).reshape(B, T, D).astype(np.float32)


# version 10
# speedup vs baseline: 1.2900x; 1.0324x over previous
import contextlib

import numpy as np
import ml_dtypes
import concourse.bass as bass
import concourse.mybir as mybir
from concourse.bass_utils import run_bass_kernel_spmd
from bass_rust import IndirectOffsetOnAxis as IOA

F32 = mybir.dt.float32
BF16 = mybir.dt.bfloat16
U32 = mybir.dt.uint32
AF = mybir.ActivationFunctionType
ALU = mybir.AluOpType
AX = mybir.AxisListType

NCORE = 8
B, T, D = 2, 8192, 1024
TC = 2048
ST = 512
NTILE = TC // ST
G = 8
E, CAP, EH, SHD = 8, 5120, 2048, 4096
NTOK = B * T
EPS = 1e-5
RG = [list(range(NCORE))]
SENT_ROW = float(E * CAP)

ENGINES = ("sync", "gpsimd", "vector", "scalar", "tensor")


class Sem:
    __slots__ = ("h", "n")

    def __init__(self, h):
        self.h = h
        self.n = 0


class EngineProxy:
    def __init__(self, ctx, name):
        self._ctx = ctx
        self._name = name

    def __getattr__(self, op):
        ctx, name = self._ctx, self._name

        def f(*a, **kw):
            if ctx.cur == name:
                return getattr(ctx.eng[name], op)(*a, **kw)
            return None

        return f


class Ctx:
    def __init__(self, nc):
        self.nc = nc
        self.eng = {
            "sync": nc.sync, "gpsimd": nc.gpsimd, "vector": nc.vector,
            "scalar": nc.scalar, "tensor": nc.tensor,
        }
        self.cur = None
        self.s = EngineProxy(self, "sync")
        self.g = EngineProxy(self, "gpsimd")
        self.v = EngineProxy(self, "vector")
        self.a = EngineProxy(self, "scalar")
        self.t = EngineProxy(self, "tensor")
        self.sems = []

    def sem(self, handle):
        s = Sem(handle)
        self.sems.append(s)
        return s

    def reset(self):
        for s in self.sems:
            s.n = 0

    @staticmethod
    def mark(instr, sem, k=1):
        sem.n += k
        if instr is not None:
            instr.then_inc(sem.h, k)
        return sem.n

    def wait(self, eng_name, sem, val=None):
        if self.cur == eng_name:
            self.eng[eng_name].wait_ge(sem.h, sem.n if val is None else val)


def _chan2d(v):
    return np.ascontiguousarray(np.asarray(v).reshape(-1, 128).T)


def _prep_inputs(inp):
    x = np.asarray(inp["L0_attn_input_bxtxd"])
    maps = []
    shared = {
        "w_x": np.ascontiguousarray(inp["L0_attn_w_x"]),
        "gi_w": np.ascontiguousarray(inp["L0_attn_gi_w"]),
        "ga_w": np.ascontiguousarray(inp["L0_attn_ga_w"]),
        "w_y": np.ascontiguousarray(inp["L0_attn_w_y"]),
        "bx2d": _chan2d(inp["L0_attn_b_x"]),
        "by2d": _chan2d(inp["L0_attn_b_y"]),
        "cb2d": _chan2d(inp["L0_attn_conv_b"]),
        "gib2d": _chan2d(inp["L0_attn_gi_b"]),
        "gab2d": _chan2d(inp["L0_attn_ga_b"]),
        "ap2d": _chan2d(inp["L0_attn_a_param"]),
        "cw2d": np.ascontiguousarray(
            np.asarray(inp["L0_attn_conv_w"]).reshape(4, G, 128)
            .transpose(2, 1, 0).reshape(128, G * 4)),
        "projT": np.ascontiguousarray(np.asarray(inp["L0_ffn_proj_w"]).T),
        "projb_rep": np.ascontiguousarray(
            np.tile(np.asarray(inp["L0_ffn_proj_b"])[None, :], (128, 1))),
        "ebias_rep": np.ascontiguousarray(
            np.tile(np.asarray(inp["L0_ffn_expert_bias"])[None, :], (128, 1))),
        "sh1T": np.ascontiguousarray(np.asarray(inp["L0_ffn_sh1_w"]).T),
        "sh1b2d": _chan2d(inp["L0_ffn_sh1_b"]),
        "sh2T": np.ascontiguousarray(np.asarray(inp["L0_ffn_sh2_w"]).T),
        "sh2b_rep": np.ascontiguousarray(
            np.tile(np.asarray(inp["L0_ffn_sh2_b"])[None, :], (128, 1))),
        "idn": np.eye(128, dtype=np.float32),
        "idnb": np.eye(128, dtype=ml_dtypes.bfloat16),
        "iota8": np.tile(np.arange(8, dtype=np.float32)[None, :], (128, 1)),
        "iota8p": np.tile(np.arange(8, dtype=np.float32)[None, :] + 1000.0, (128, 1)),
        "iota_ref": (np.arange(128)[:, None] * 256
                     + np.concatenate([2 * np.arange(128), 2 * np.arange(128) + 1])[None, :]
                     ).astype(np.float32),
        "iota_j": np.tile(np.arange(256, dtype=np.float32)[None, :], (128, 1)),
        "iota_slot": (np.arange(40)[None, :] * 128
                      + np.arange(128)[:, None]).astype(np.float32),
        "iota16x256": np.tile((np.arange(16, dtype=np.float32) * 256)[None, :], (128, 1)),
    }
    fc1 = np.asarray(inp["L0_ffn_fc1_w"])
    fc1b = np.asarray(inp["L0_ffn_fc1_b"])
    fc2 = np.asarray(inp["L0_ffn_fc2_w"])
    fc2b = np.asarray(inp["L0_ffn_fc2_b"])

    for c in range(NCORE):
        b, k = c // 4, c % 4
        t0 = k * TC
        xs = np.zeros((TC + 3, D), np.float32)
        lo = max(t0 - 3, 0)
        xs[3 - (t0 - lo):] = x[b, lo:t0 + TC]
        hv = 0.0 if k == 0 else 1.0
        msk = np.zeros((128, 3), np.float32)
        msk[:, :k] = 1.0
        consts = np.concatenate([
            np.full((128, 1), hv, np.float32),
            np.full((128, 1), float(b), np.float32),
            msk,
            1.0 - msk,
            np.full((128, 1), float(c), np.float32),
            np.full((128, 1), float(c * CAP), np.float32),
            np.arange(128, dtype=np.float32).reshape(128, 1),
            np.full((128, 1), 1.0 - hv, np.float32),
        ], axis=1).astype(np.float32)
        m = dict(shared)
        m.update({
            "xT": np.ascontiguousarray(xs.T),
            "x_tok": np.ascontiguousarray(x[b, t0:t0 + TC]),
            "fc1T": np.ascontiguousarray(fc1[c].T),
            "fc1b2d": _chan2d(fc1b[c]),
            "fc2T": np.ascontiguousarray(fc2[c].T),
            "fc2b_rep": np.ascontiguousarray(np.tile(fc2b[c][None, :], (128, 1))),
            "consts": consts,
        })
        maps.append(m)
    return maps


def _build(debug_stage=None):
    nc = bass.Bass("TRN2", target_bir_lowering=False, debug=False)
    C = Ctx(nc)

    dd = {}

    def din(name, shape, dt=F32):
        dd[name] = nc.dram_tensor(name, shape, dt, kind="ExternalInput").ap()

    din("xT", [D, TC + 3]); din("x_tok", [TC, D])
    for n in ("w_x", "gi_w", "ga_w", "w_y"):
        din(n, [D, D])
    for n in ("bx2d", "by2d", "cb2d", "gib2d", "gab2d", "ap2d"):
        din(n, [128, G])
    din("cw2d", [128, G * 4])
    din("projT", [D, E]); din("projb_rep", [128, E]); din("ebias_rep", [128, E])
    din("sh1T", [D, SHD]); din("sh1b2d", [128, SHD // 128])
    din("sh2T", [SHD, D]); din("sh2b_rep", [128, D])
    din("fc1T", [D, EH]); din("fc1b2d", [128, EH // 128])
    din("fc2T", [EH, D]); din("fc2b_rep", [128, D])
    din("idn", [128, 128])
    din("idnb", [128, 128], BF16)
    din("iota8", [128, E]); din("iota8p", [128, E])
    din("iota_ref", [128, 256]); din("iota_j", [128, 256]); din("iota_slot", [128, 40])
    din("iota16x256", [128, 16])
    din("consts", [128, 12])

    out = nc.dram_tensor("out", [TC, D], F32, kind="ExternalOutput").ap()
    if debug_stage == "fulldbg":
        dbg_xe = nc.dram_tensor("dbg_xe", [128, 6144], BF16, kind="ExternalOutput").ap()
        dbg_he = nc.dram_tensor("dbg_he", [128, 8192], BF16, kind="ExternalOutput").ap()
        dbg_xall = nc.dram_tensor("dbg_xall", [NTOK, D], BF16, kind="ExternalOutput").ap()
        dbg_tok = nc.dram_tensor("dbg_tok", [128, 40], F32, kind="ExternalOutput").ap()
        dbg_pair = nc.dram_tensor("dbg_pair", [4 * NCORE, TC], F32, kind="ExternalOutput").ap()
        dbg_sel2 = nc.dram_tensor("dbg_sel2", [128, 2200], F32, kind="ExternalOutput").ap()
        dbg_slot = nc.dram_tensor("dbg_slot", [NCORE, 128 * 256], F32, kind="ExternalOutput").ap()
        dbg_oe = nc.dram_tensor("dbg_oe", [CAP, D], F32, kind="ExternalOutput").ap()
        dbg_xm = nc.dram_tensor("dbg_xm", [TC, D], BF16, kind="ExternalOutput").ap()
        dbg_hsh = nc.dram_tensor("dbg_hsh", [SHD, 128], BF16, kind="ExternalOutput").ap()

    h_sp = nc.dram_tensor("h_sp", [G, 128, TC], F32).ap()
    p_sp = nc.dram_tensor("p_sp", [G, 128, TC], F32).ap()
    cin_carry = nc.dram_tensor("cin_carry", [2, D], F32).ap()
    cout_carry = nc.dram_tensor("cout_carry", [2 * NCORE, D], F32).ap()
    cin_xm = nc.dram_tensor("cin_xm", [TC, D], BF16).ap()
    cout_xm = nc.dram_tensor("cout_xm", [NTOK, D], BF16).ap()
    cin_pair = nc.dram_tensor("cin_pair", [4, TC], F32).ap()
    cout_pair = nc.dram_tensor("cout_pair", [4 * NCORE, TC], F32).ap()
    cnt_dram = nc.dram_tensor("cnt_dram", [128], F32).ap()
    base_dram = nc.dram_tensor("base_dram", [130], F32).ap()
    padded_dram = nc.dram_tensor("padded_dram", [128 * 256], F32).ap()
    cin_slot = nc.dram_tensor("cin_slot", [1, 128 * 256], F32).ap()
    cout_slot = nc.dram_tensor("cout_slot", [NCORE, 128 * 256], F32).ap()
    cin_oes = [nc.dram_tensor(f"cin_oe{s5}", [1024, D], F32).ap() for s5 in range(5)]
    sh2_sp = nc.dram_tensor("sh2_sp", [TC, D], F32).ap()
    fence_in = nc.dram_tensor("fence_in", [1, 16], F32).ap()
    fence_out = [nc.dram_tensor(f"fence_out{i}", [NCORE, 16], F32).ap() for i in range(5)]
    cin_oe = None
    cout_oe = nc.dram_tensor("cout_oe", [NCORE * CAP, D], F32).ap()
    hsh_sp = nc.dram_tensor("hsh_sp", [SHD, TC], BF16).ap()

    ctx = contextlib.ExitStack()

    def sbuf(name, shape, dt=F32):
        return ctx.enter_context(nc.sbuf_tensor(name, shape, dt))

    def psum(name, shape, dt=F32):
        return ctx.enter_context(nc.psum_tensor(name, shape, dt))

    AR_W = sbuf("AR_W", [128, 32768], BF16)
    AR_A = sbuf("AR_A", [128, 16384], F32)
    AR_X = sbuf("AR_X", [128, 16384], BF16)
    AR_B = sbuf("AR_B", [128, 4120], F32)

    def vw(ar, lo, n, g=None):
        a = ar[:, lo:lo + n]
        if g is not None:
            a = a.rearrange("p (g n) -> p g n", g=g)
        return a

    wx_sb = vw(AR_W, 0, 8192, G)
    wgi_sb = vw(AR_W, 8192, 8192, G)
    wga_sb = vw(AR_W, 16384, 8192, G)
    gx = vw(AR_A, 0, 4096, G)
    aamu = vw(AR_A, 4096, 4096, G)
    av = vw(AR_A, 8192, 4096, G)
    x2f = vw(AR_A, 12288, 4096, G)
    xm_tok = AR_A[:, :].rearrange("p (c n) -> p c n", c=TC // 128)
    x_fmb = vw(AR_X, 0, 16384, G)
    x1e = AR_B[:, 0:4120].rearrange("p (g n) -> p g n", g=G)
    y_fm = vw(AR_B, 0, 4096, G)

    halob = sbuf("halob", [128, G, 3], BF16)
    x1h3 = sbuf("x1h3", [128, G, 3], F32)
    x2b = sbuf("x2b", [128, G, ST], BF16)
    h_t = sbuf("h_t", [128, 2, ST], F32)
    p_t = sbuf("p_t", [128, 2, ST], F32)
    hc = sbuf("hc", [128, G], F32)
    pc = sbuf("pc", [128, G], F32)
    zeros512 = sbuf("zeros512", [128, ST], F32)
    car = sbuf("car", [128, 2 * NCORE, G], F32)
    h_in = sbuf("h_in", [128, G], F32)
    Hb = [sbuf(f"Hb_{i}", [128, G], F32) for i in range(2)]
    tmp8a = sbuf("tmp8a", [128, G], F32)
    tmp8b = sbuf("tmp8b", [128, G], F32)
    cexp = sbuf("cexp", [128, G], F32)
    idn_sb = sbuf("idn_sb", [128, 128], F32)
    bx_sb = sbuf("bx_sb", [128, G], F32)
    by_sb = sbuf("by_sb", [128, G], F32)
    cb_sb = sbuf("cb_sb", [128, G], F32)
    gib_sb = sbuf("gib_sb", [128, G], F32)
    gab_sb = sbuf("gab_sb", [128, G], F32)
    ap_sb = sbuf("ap_sb", [128, G], F32)
    cw_sb = sbuf("cw_sb", [128, G * 4], F32)
    cst = sbuf("cst", [128, 12], F32)
    xf2 = sbuf("xf2", [128, 2, ST], F32)
    xm_bf = sbuf("xm_bf", [128, D], BF16)
    stat = sbuf("stat", [128, 8], F32)
    x2b_f32 = x2b.rearrange("p g n -> p (g n)").bitcast(F32)
    lnsq = x2b_f32[:, 0:D]
    r1_tok = x2b_f32[:, D:2 * D]

    projT_sb = sbuf("projT_sb", [128, G, 8], BF16)
    pjb_sb = sbuf("pjb_sb", [128, 8], F32)
    eb_sb = sbuf("eb_sb", [128, 8], F32)
    io8_sb = sbuf("io8_sb", [128, 8], F32)
    io8p_sb = sbuf("io8p_sb", [128, 8], F32)
    idnb_sb = sbuf("idnb_sb", [128, 128], BF16)
    e1a = sbuf("e1a", [128, 16], F32)
    e2a = sbuf("e2a", [128, 16], F32)
    w1a = sbuf("w1a", [128, 16], F32)
    w2a = sbuf("w2a", [128, 16], F32)
    zall = AR_B[:, 0:128].rearrange("p (a b) -> p a b", b=8)
    sgall = AR_B[:, 128:256].rearrange("p (a b) -> p a b", b=8)
    sgal2 = AR_B[:, 256:384].rearrange("p (a b) -> p a b", b=8)
    zt = AR_B[:, 384:448].rearrange("p (a b) -> p a b", b=4)
    m1a = sbuf("m1a", [128, 16], F32)
    m2a = sbuf("m2a", [128, 16], F32)
    sel4m = sbuf("sel4m", [128, 4], F32)
    cnt4 = sbuf("cnt4", [128, 4], F32)
    tot4 = sbuf("tot4", [128, 4], F32)
    ge4 = sbuf("ge4", [128, 4], F32)
    sva_all = sbuf("sva_all", [128, 16], F32)
    svb_all = sbuf("svb_all", [128, 16], F32)
    rAu_all = sbuf("rAu_all", [128, 16], U32)
    rBu_all = sbuf("rBu_all", [128, 16], U32)
    ixu_all = sbuf("ixu_all", [128, 32], U32)
    kAB = sbuf("kAB", [128, 32], F32)
    wABe = sbuf("wABe", [128, 32], F32)
    eA = eB = wAt = wBt = None
    fc1b_sb = sbuf("fc1b_sb", [128, 16], F32)
    sh1b_sb = sbuf("sh1b_sb", [128, 32], F32)
    hshb = sbuf("hshb", [128, 2, 512], BF16)

    arB = AR_B
    def bview(lo, n, g=None):
        a = arB[:, lo:lo + n]
        if g is not None:
            a = a.rearrange("p (g n) -> p g n", g=g)
        return a
    wm = bview(0, 256)
    eA = bview(768, 128)
    eB = bview(896, 128)
    wAt = bview(1024, 128)
    wBt = bview(1152, 128)
    io_ref = bview(256, 256)
    io_j = bview(512, 256)
    pos = bview(768, 256)
    rexc = bview(1024, 256)
    arr_ = bview(1280, 256)
    delt = bview(1536, 256)
    dshs = bview(1792, 256)
    selb = bview(2048, 256)
    bitu = sbuf("bitu_t", [128, 256], U32)
    mskio = bview(2304, 256)
    tokv = bview(2560, 256)
    eqm = bview(2816, 256)
    keptb = bview(3072, 256)
    rowm = bview(3328, 256)
    slotm = bview(3584, 256)
    fc2b_rp = bview(0, 1024)
    sh2b_rp = bview(1024, 1024)
    oeb = [bview(2048, 1024), bview(3072, 1024)]
    sht = [bview(0, 1024), bview(2048, 1024)]
    otok = bview(3072, 1024)

    xf2v = xf2.rearrange("p a n -> p (a n)")
    cntb = xf2v[:, 0:128]
    pr16 = xf2v[:, 0:128]
    base_bc = xf2v[:, 128:256]
    srcx = xf2v[:, 256:296]
    tokf = xf2v[:, 296:336]
    srcu = sbuf("srcu", [128, 40], U32)
    toku = sbuf("toku", [128, 40], U32)
    io_slot = xf2v[:, 416:456]
    io16x = xf2v[:, 748:764]
    _sc = {}
    for _i, _n in enumerate(("lo", "hi", "mid", "ge", "neq", "asg", "ov", "theta",
                             "cntgt", "m_p", "t1", "t2", "t3", "cnt1", "ctot", "bcol")):
        _sc[_n] = xf2v[:, 456 + _i:457 + _i]
    def sv(n):
        return _sc[n]
    mrow = xf2v[0:1, 472:600]
    brow = xf2v[0:1, 600:728]
    _p4 = {}
    for _i, _n in enumerate(("fo", "ixA", "ixB", "svA", "svB", "kA", "kB",
                             "wAe", "wBe")):
        _p4[_n] = xf2v[:, 728 + _i:729 + _i]
    def p4(n):
        return _p4[n]
    idxu4 = sbuf("idxu4", [128, 4], U32)
    ixAu = idxu4[:, 0:1]
    ixBu = idxu4[:, 1:2]
    rAu = idxu4[:, 2:3]
    rBu = idxu4[:, 3:4]
    sh1T_sb = AR_W[:, 0:32768].rearrange("p (g n) -> p g n", g=G)
    fc1T_sb = AR_W[:, 0:16384].rearrange("p (g n) -> p g n", g=G)
    fc2T_sb = AR_W[:, 16384:32768].rearrange("p (g n) -> p g n", g=16)
    sh2T_sb = AR_W[:, 0:32768].rearrange("p (g n) -> p g n", g=32)
    xeg = AR_X[:, 0:2048].rearrange("p (a n) -> p a n", a=2)
    xe_fm = AR_X[:, 2048:6144].rearrange("p (g n) -> p g n", g=G)
    he = AR_X[:, 6144:14336].rearrange("p (g n) -> p g n", g=16)
    hsh2 = [AR_X[:, 0:4096].rearrange("p (g n) -> p g n", g=32),
            AR_X[:, 4096:8192].rearrange("p (g n) -> p g n", g=32)]
    cout_slot_flat = cout_slot.rearrange("r n -> (r n)")

    ps = [psum(f"ps{i}", [128, ST], F32) for i in range(3)]
    pw = [psum(f"pw{i}", [128, D], F32) for i in range(2)]
    pb16 = psum("pb16", [128, D], BF16)

    sems = {}
    for name in ("sS", "sG", "sV", "sA", "sPE", "cC1", "cC2a", "cC2b",
                 "cSlot", "cOe"):
        sems[name] = C.sem(ctx.enter_context(nc.semaphore(name)))
    sS, sG, sV, sA, sPE = (sems[k] for k in ("sS", "sG", "sV", "sA", "sPE"))
    cC1 = sems["cC1"]
    cC2a = sems["cC2a"]
    cC2b = sems["cC2b"]

    CID = {"hv": 0, "mb": 1, "msk": 2, "msk1m": 5, "mye": 8, "ecap": 9,
           "iota128": 10, "hv1m": 11}

    def cstc(name, off=0):
        i = CID[name] + off
        return cst[:, i:i + 1]

    xT_r = dd["xT"].rearrange("(g p) t -> p g t", p=128)

    def program():
        mark, wait = C.mark, C.wait
        evac = {}

        def swv(i):
            n = mark(i, sV, 1)
            wait("vector", sV, n)
            return n

        _cur_scope = [None]

        def _scope(name):
            if _cur_scope[0] is not None:
                pn, pid = _cur_scope[0]
                C.nc.leave_named_scope(pn, pid, False)
            sid, _ = C.nc.enter_named_scope(name, False)
            _cur_scope[0] = (name, sid)

        def phase_barrier():
            for e in ENGINES:
                for sem in (sS, sG, sV, sA, sPE):
                    wait(e, sem, sem.n)

        XMT = False
        _scope("setup")
        for t_, d_ in [(idn_sb, dd["idn"]), (bx_sb, dd["bx2d"]), (by_sb, dd["by2d"]),
                       (cb_sb, dd["cb2d"]), (gib_sb, dd["gib2d"]), (gab_sb, dd["gab2d"]),
                       (ap_sb, dd["ap2d"]), (cw_sb, dd["cw2d"]), (cst, dd["consts"]),
                       (idnb_sb, dd["idnb"])]:
            mark(C.s.dma_start(out=t_[:], in_=d_), sS, 16)
        consts_ld = sS.n

        for wsb, wd in [(wx_sb, dd["w_x"]), (wgi_sb, dd["gi_w"]), (wga_sb, dd["ga_w"])]:
            for k in range(G):
                mark(C.g.dma_start(out=wsb[:, k, :], in_=wd[k * 128:(k + 1) * 128, :]), sG, 16)
        attn_w_ld = sG.n

        mark(C.g.dma_start(out=x_fmb[:, :, 0:ST], in_=xT_r[:, :, 3:3 + ST]), sG, 16)
        x_ld = {0: sG.n}
        mark(C.g.dma_start(out=halob[:], in_=xT_r[:, :, 0:3]), sG, 16)
        halo_ld = sG.n

        wait("scalar", sS, consts_ld)
        wait("vector", sS, consts_ld)
        a_n = mark(C.a.activation(out=tmp8a[:], in_=ap_sb[:], func=AF.Abs), sA, 1)
        wait("scalar", sA, a_n)
        a_n = mark(C.a.activation(out=tmp8b[:], in_=tmp8a[:], func=AF.Exp, scale=-1.0), sA, 1)
        wait("scalar", sA, a_n)
        mark(C.a.activation(out=tmp8a[:], in_=tmp8b[:], func=AF.Ln, bias=1.0), sA, 1)
        ln_done = sA.n
        swv(C.v.tensor_scalar_max(cexp[:], ap_sb[:], 0.0))
        wait("vector", sA, ln_done)
        swv(C.v.tensor_add(cexp[:], cexp[:], tmp8a[:]))
        swv(C.v.tensor_scalar_mul(cexp[:], cexp[:], -8.0))
        C.v.memset(zeros512[:], 0.0)
        C.v.memset(hc[:], 0.0)
        mark(C.v.memset(pc[:], 1.0), sV, 1)
        derived = sV.n

        _scope("attn1")
        scans_done = {-1: 0}
        gates_pe = {-1: 0}
        spill = {}
        last_x1_pe = 0
        for it in range(NTILE):
            if it + 1 < NTILE:
                mark(C.g.dma_start(out=x_fmb[:, :, (it + 1) * ST:(it + 2) * ST],
                                   in_=xT_r[:, :, 3 + (it + 1) * ST:3 + (it + 2) * ST]), sG, 16)
                x_ld[it + 1] = sG.n

            wait("scalar", sV, scans_done[it - 1])
            if it == 0:
                wait("scalar", sV, derived)

            if it == 0:
                wait("tensor", sG, attn_w_ld)
                wait("tensor", sG, halo_ld)
                for m in range(G):
                    mm = None
                    for k in range(G):
                        i = C.t.matmul(ps[0][:, 0:3], wx_sb[:, k, m * 128:(m + 1) * 128],
                                       halob[:, k, :], start=(k == 0), stop=(k == G - 1))
                        mm = i if i is not None else mm
                    pe_n = mark(mm, sPE, 1)
                    wait("scalar", sPE, pe_n)
                    a_n = mark(C.a.activation(out=x1e[:, m, 0:3], in_=ps[0][:, 0:3],
                                              func=AF.Identity, bias=bx_sb[:, m:m + 1]), sA, 1)
                    wait("tensor", sA, a_n)
                wait("vector", sA, sA.n)
                swv(C.v.tensor_scalar_mul(x1e[:, :, 0:3], x1e[:, :, 0:3], cstc("hv")))
                evac[0] = sA.n
            else:
                swv(C.v.tensor_copy(out=x1e[:, :, 0:3], in_=x1h3[:]))

            wait("tensor", sG, x_ld[it])
            wait("tensor", sG, attn_w_ld)
            wait("scalar", sV, scans_done[it - 1])
            for m in range(G):
                bank = m % 3
                if bank in evac:
                    wait("tensor", sA, evac[bank])
                mm = None
                for k in range(G):
                    i = C.t.matmul(ps[bank][:], wx_sb[:, k, m * 128:(m + 1) * 128],
                                   x_fmb[:, k, it * ST:(it + 1) * ST],
                                   start=(k == 0), stop=(k == G - 1))
                    mm = i if i is not None else mm
                pe_n = mark(mm, sPE, 1)
                wait("scalar", sPE, pe_n)
                evac[bank] = mark(C.a.activation(out=x1e[:, m, 3:3 + ST], in_=ps[bank][:],
                                                 func=AF.Identity, bias=bx_sb[:, m:m + 1]), sA, 1)
            x1_done = sA.n
            last_x1_pe = sPE.n

            wait("vector", sA, x1_done)
            for g in range(G):
                C.v.tensor_scalar(x2f[:, g], x1e[:, g, 3:3 + ST],
                                  cw_sb[:, g * 4 + 3:g * 4 + 4], cb_sb[:, g:g + 1],
                                  ALU.mult, ALU.add)
                for i3 in range(3):
                    C.v.scalar_tensor_tensor(x2f[:, g], x1e[:, g, i3:i3 + ST],
                                             cw_sb[:, g * 4 + i3:g * 4 + i3 + 1],
                                             x2f[:, g], ALU.mult, ALU.add)
            C.v.tensor_copy(out=x1h3[:], in_=x1e[:, :, ST:ST + 3])
            wait("vector", sPE, gates_pe[it - 1])
            mark(C.v.tensor_copy(out=x2b[:], in_=x2f[:]), sV, 1)
            x2_ready = sV.n

            wait("tensor", sV, x2_ready)
            for wsb, dstbuf, bias in ((wgi_sb, gx, gib_sb), (wga_sb, aamu, gab_sb)):
                for m in range(G):
                    bank = m % 3
                    wait("tensor", sA, evac[bank])
                    mm = None
                    for k in range(G):
                        i = C.t.matmul(ps[bank][:], wsb[:, k, m * 128:(m + 1) * 128],
                                       x2b[:, k, :], start=(k == 0), stop=(k == G - 1))
                        mm = i if i is not None else mm
                    pe_n = mark(mm, sPE, 1)
                    wait("scalar", sPE, pe_n)
                    evac[bank] = mark(C.a.activation(out=dstbuf[:, m], in_=ps[bank][:],
                                                     func=AF.Sigmoid, bias=bias[:, m:m + 1]), sA, 1)
            gates_pe[it] = sPE.n

            wait("scalar", sA, sA.n)
            laste = None
            for g in range(G):
                i = C.a.activation(out=av[:, g], in_=aamu[:, g], func=AF.Exp,
                                   scale=cexp[:, g:g + 1])
                laste = i if i is not None else laste
            a_n = mark(laste, sA, 1)
            wait("scalar", sA, a_n)
            mark(C.a.activation(out=aamu[:], in_=av[:], func=AF.Square), sA, 1)
            sq_done = sA.n
            wait("vector", sA, sq_done)
            C.v.tensor_scalar(aamu[:], aamu[:], -1.0, 1.0, ALU.mult, ALU.add)
            mark(C.v.tensor_scalar_max(aamu[:], aamu[:], 0.0), sV, 1)
            mu_pre = sV.n
            wait("scalar", sV, mu_pre)
            mark(C.a.activation(out=aamu[:], in_=aamu[:], func=AF.Sqrt), sA, 1)
            mu_done = sA.n

            wait("vector", sA, mu_done)
            if it == 0:
                swv(C.v.tensor_scalar_mul(av[:, :, 0:1], av[:, :, 0:1], cstc("hv")))
                swv(C.v.tensor_scalar(aamu[:, :, 0:1], aamu[:, :, 0:1], cstc("hv"),
                                      cstc("hv1m"), ALU.mult, ALU.add))

            if debug_stage == "p1t0" and it == 0:
                mark(C.v.nop(), sV, 1)
                wait("sync", sV, sV.n)
                wait("sync", sA, sA.n)
                for g in range(G):
                    mark(C.s.dma_start(out=out[g * 128:(g + 1) * 128, 0:512], in_=x1e[:, g, 3:3 + ST]), sS, 16)
                    mark(C.s.dma_start(out=out[g * 128:(g + 1) * 128, 512:1024], in_=gx[:, g]), sS, 16)
                    mark(C.s.dma_start(out=out[1024 + g * 128:1024 + (g + 1) * 128, 0:512], in_=av[:, g]), sS, 16)
                    mark(C.s.dma_start(out=out[1024 + g * 128:1024 + (g + 1) * 128, 512:1024], in_=aamu[:, g]), sS, 16)
                wait("gpsimd", sS, sS.n)
                wait("gpsimd", sG, sG.n)
                return
            C.v.tensor_mul(x2f[:], gx[:], x2f[:])
            C.v.tensor_mul(x2f[:], x2f[:], aamu[:])

            for g in range(G):
                gi = (it * G + g) % 2
                if (("sp", gi) in spill):
                    wait("vector", sS, spill[("sp", gi)])
                C.v.tensor_tensor_scan(h_t[:, gi], av[:, g], x2f[:, g],
                                       hc[:, g:g + 1], ALU.mult, ALU.add)
                swv(C.v.tensor_copy(out=hc[:, g:g + 1], in_=h_t[:, gi, ST - 1:ST]))
                C.v.tensor_tensor_scan(p_t[:, gi], av[:, g], zeros512[:],
                                       pc[:, g:g + 1], ALU.mult, ALU.add)
                v_n = swv(C.v.tensor_copy(out=pc[:, g:g + 1], in_=p_t[:, gi, ST - 1:ST]))
                wait("sync", sV, v_n)
                mark(C.s.dma_start(out=h_sp[g, :, it * ST:(it + 1) * ST], in_=h_t[:, gi]), sS, 16)
                mark(C.s.dma_start(out=p_sp[g, :, it * ST:(it + 1) * ST], in_=p_t[:, gi]), sS, 16)
                spill[("sp", gi)] = sS.n
            scans_done[it] = sV.n

        wait("gpsimd", sPE, last_x1_pe)
        for k in range(G):
            mark(C.g.dma_start(out=wx_sb[:, k, :], in_=dd["w_y"][k * 128:(k + 1) * 128, :]), sG, 16)
        wy_ld = sG.n

        wait("gpsimd", sV, derived)
        mark(C.g.dma_start(out=fence_in, in_=zeros512[0:1, 0:16]), sG, 16)
        phase_barrier()
        _scope("carry")
        wait("gpsimd", sV, scans_done[NTILE - 1])
        mark(C.g.dma_start(out=cin_carry[0:1, :].rearrange("a (p g) -> (a p) g", p=128),
                           in_=pc[:]), sG, 16)
        mark(C.g.dma_start(out=cin_carry[1:2, :].rearrange("a (p g) -> (a p) g", p=128),
                           in_=hc[:]), sG, 16)
        wait("gpsimd", sG, sG.n)
        mark(C.g.collective_compute("AllGather", ALU.bypass, replica_groups=RG,
                                    ins=[cin_carry.opt()], outs=[cout_carry.opt()]), cC1, 1)

        def emit_fence(i, sem):
            mark(C.g.collective_compute("AllGather", ALU.bypass, replica_groups=RG,
                                        ins=[fence_in.opt()], outs=[fence_out[i].opt()]),
                 sem, 1)

        emit_fence(0, cC1)
        wait("gpsimd", cC1, 2)
        for q in range(4):
            mark(C.g.dma_start(
                out=car[:, q * 4:(q + 1) * 4, :],
                in_=cout_carry.rearrange("r (p g) -> p r g", p=128)[:, q * 4:(q + 1) * 4, :]),
                sG, 16)
        carry_ld = sG.n

        wait("vector", sG, carry_ld)
        for bb in range(2):
            swv(C.v.memset(Hb[bb][:], 0.0))
            for s_ in range(3):
                r = 4 * bb + s_
                swv(C.v.tensor_scalar(tmp8a[:], car[:, 2 * r, :], cstc("msk", s_),
                                      cstc("msk1m", s_), ALU.mult, ALU.add))
                swv(C.v.tensor_scalar_mul(tmp8b[:], car[:, 2 * r + 1, :], cstc("msk", s_)))
                swv(C.v.tensor_mul(Hb[bb][:], Hb[bb][:], tmp8a[:]))
                swv(C.v.tensor_add(Hb[bb][:], Hb[bb][:], tmp8b[:]))
        swv(C.v.tensor_scalar_mul(h_in[:], Hb[1][:], cstc("mb")))
        swv(C.v.scalar_tensor_tensor(tmp8a[:], Hb[0][:], cstc("mb"), Hb[0][:],
                                     ALU.mult, ALU.subtract))
        mark(C.v.tensor_sub(h_in[:], h_in[:], tmp8a[:]), sV, 1)
        hin_done = sV.n

        if debug_stage == "carry":
            wait("sync", sV, hin_done)
            mark(C.s.dma_start(out=out[0:128, 0:8], in_=h_in[:]), sS, 16)
            mark(C.s.dma_start(out=out[128:256, 0:8], in_=Hb[0][:]), sS, 16)
            mark(C.s.dma_start(out=out[128:256, 8:16], in_=Hb[1][:]), sS, 16)
            mark(C.s.dma_start(out=out[128:256, 16:28], in_=cst[:]), sS, 16)
            mark(C.s.dma_start(out=out[128:256, 28:36], in_=tmp8a[:]), sS, 16)
            mark(C.s.dma_start(out=out[0:128, 8:136], in_=car[:].rearrange("p r g -> p (r g)")), sS, 16)
            mark(C.s.dma_start(out=out[0:128, 136:144], in_=hc[:]), sS, 16)
            mark(C.s.dma_start(out=out[0:128, 144:152], in_=pc[:]), sS, 16)
            for g in range(G):
                mark(C.s.dma_start(out=out[256 + g * 128:256 + (g + 1) * 128, 0:512], in_=h_sp[g, :, 0:512]), sS, 16)
            wait("gpsimd", sS, sS.n)
            wait("gpsimd", sG, sG.n)
            return

        phase_barrier()
        _scope("attn2")
        wait("sync", sS, sS.n)
        r1_done = {-1: 0}
        tr_pe = {-1: 0}
        hp_ld = {}
        xm_chunk = {}
        pb16_ev = {-1: 0}
        xmdma = {}
        for it in range(NTILE):
            wait("tensor", sG, wy_ld)
            wait("scalar", sV, r1_done[it - 1])
            wait("scalar", sPE, tr_pe[it - 1])
            wait("scalar", sV, scans_done[NTILE - 1] if it == 0 else 0)
            for m in range(G):
                bank = m % 3
                wait("tensor", sA, evac[bank])
                mm = None
                for k in range(G):
                    i = C.t.matmul(ps[bank][:], wx_sb[:, k, m * 128:(m + 1) * 128],
                                   x_fmb[:, k, it * ST:(it + 1) * ST],
                                   start=(k == 0), stop=(k == G - 1))
                    mm = i if i is not None else mm
                pe_n = mark(mm, sPE, 1)
                wait("scalar", sPE, pe_n)
                evac[bank] = mark(C.a.activation(out=y_fm[:, m], in_=ps[bank][:],
                                                 func=AF.Gelu, bias=by_sb[:, m:m + 1]), sA, 1)
            y_done = sA.n

            wait("vector", sA, y_done)
            wait("vector", sV, hin_done)
            lastr = None
            for g in range(G):
                j = it * G + g
                gi = j % 2
                if j - 2 in hp_ld:
                    wait("sync", sV, hp_ld[j - 2])
                mark(C.s.dma_start(out=h_t[:, gi], in_=h_sp[g, :, it * ST:(it + 1) * ST]), sS, 16)
                mark(C.s.dma_start(out=p_t[:, gi], in_=p_sp[g, :, it * ST:(it + 1) * ST]), sS, 16)
                mark(C.s.dma_start(out=xf2[:, gi], in_=xT_r[:, g, 3 + it * ST:3 + (it + 1) * ST]), sS, 16)
                s_n = sS.n
                wait("vector", sS, s_n)
                C.v.scalar_tensor_tensor(h_t[:, gi], p_t[:, gi], h_in[:, g:g + 1],
                                         h_t[:, gi], ALU.mult, ALU.add)
                C.v.tensor_mul(y_fm[:, g], y_fm[:, g], h_t[:, gi])
                i = C.v.tensor_add(y_fm[:, g], y_fm[:, g], xf2[:, gi])
                lastr = i if i is not None else lastr
                hp_ld[j] = mark(lastr, sV, 1)
            r1_done[it] = sV.n

            if debug_stage == "p2" and it == 0:
                wait("sync", sV, r1_done[0])
                for g in range(G):
                    mark(C.s.dma_start(out=out[g * 128:(g + 1) * 128, 0:512], in_=y_fm[:, g]), sS, 16)
                mark(C.s.dma_start(out=out[1024:1152, 0:512], in_=h_t[:, 0]), sS, 16)
                mark(C.s.dma_start(out=out[1152:1280, 0:512], in_=h_t[:, 1]), sS, 16)
                mark(C.s.dma_start(out=out[1280:1408, 0:8], in_=h_in[:]), sS, 16)
                wait("gpsimd", sS, sS.n)
                wait("gpsimd", sG, sG.n)
                return
            for cch in range(ST // 128):
                chunk = it * (ST // 128) + cch
                pb = pw[chunk % 2]
                wait("tensor", sV, r1_done[it])
                if ("pw", chunk % 2) in evac:
                    wait("tensor", sV, evac[("pw", chunk % 2)])
                mm = None
                for g in range(G):
                    i = C.t.transpose(pb[:, g * 128:(g + 1) * 128],
                                      y_fm[:, g, cch * 128:(cch + 1) * 128], idn_sb[:])
                    mm = i if i is not None else mm
                pe_n = mark(mm, sPE, 1)
                wait("vector", sPE, pe_n)
                if chunk > 0:
                    wait("vector", sA, xm_chunk[chunk - 1])
                evac[("pw", chunk % 2)] = mark(C.v.tensor_copy(out=r1_tok[:], in_=pb[:]), sV, 1)
                tr_done = sV.n

                wait("scalar", sV, tr_done)
                mark(C.a.activation(out=lnsq[:], in_=r1_tok[:], func=AF.Copy,
                                    accum_out=stat[:, 0:1]), sA, 1)
                a_n = mark(C.a.activation(out=lnsq[:], in_=r1_tok[:], func=AF.Square,
                                          accum_out=stat[:, 1:2]), sA, 1)
                xm_chunk[chunk] = a_n
                wait("vector", sA, a_n)
                swv(C.v.tensor_scalar_mul(stat[:, 2:3], stat[:, 0:1], 1.0 / D))
                swv(C.v.tensor_scalar_mul(stat[:, 3:4], stat[:, 1:2], 1.0 / D))
                swv(C.v.tensor_mul(stat[:, 4:5], stat[:, 2:3], stat[:, 2:3]))
                swv(C.v.tensor_sub(stat[:, 3:4], stat[:, 3:4], stat[:, 4:5]))
                mark(C.v.tensor_scalar(stat[:, 3:4], stat[:, 3:4], EPS, None, ALU.add), sV, 1)
                var_n = sV.n
                wait("scalar", sV, var_n)
                sd_n = mark(C.a.activation(out=stat[:, 5:6], in_=stat[:, 3:4], func=AF.Sqrt), sA, 1)
                wait("vector", sA, sd_n)
                swv(C.v.reciprocal(stat[:, 6:7], stat[:, 5:6]))
                C.v.tensor_scalar(xm_tok[:, chunk, :], r1_tok[:],
                                  stat[:, 2:3], stat[:, 6:7], ALU.subtract, ALU.mult)
                wait("vector", sS, xmdma.get(chunk - 1, 0))
                mark(C.v.tensor_copy(out=xm_bf[:], in_=xm_tok[:, chunk, :]), sV, 1)
                xmb_n = sV.n
                wait("sync", sV, xmb_n)
                mark(C.s.dma_start(out=cin_xm[chunk * 128:(chunk + 1) * 128, :],
                                   in_=xm_bf[:]), sS, 16)
                xmdma[chunk] = sS.n
                if XMT:
                    wait("tensor", sV, xmb_n)
                    if chunk == 0:
                        wait("tensor", sS, consts_ld)
                    wait("tensor", sV, pb16_ev[chunk - 1])
                    mm = None
                    for g in range(G):
                        i = C.t.transpose(pb16[:, g * 128:(g + 1) * 128],
                                          xm_bf[:, g * 128:(g + 1) * 128], idnb_sb[:])
                        mm = i if i is not None else mm
                    xmt_pe = mark(mm, sPE, 1)
                    wait("vector", sPE, xmt_pe)
                    pb16_ev[chunk] = swv(C.v.tensor_copy(
                        out=x_fmb[:, :, chunk * 128:(chunk + 1) * 128],
                        in_=pb16.rearrange("p (g n) -> p g n", g=G)))
            tr_pe[it] = sPE.n

        if debug_stage == "xm":
            wait("sync", sV, sV.n)
            for chunk in range(TC // 128):
                mark(C.s.dma_start(out=out[chunk * 128:(chunk + 1) * 128, :],
                                   in_=xm_tok[:, chunk, :]), sS, 16)
            wait("gpsimd", sS, sS.n)
            wait("gpsimd", sG, sG.n)
            return

        phase_barrier()
        _scope("routing")
        y_pe_last = tr_pe[NTILE - 1]
        xm_fm = x_fmb

        for k in range(G):
            mark(C.g.dma_start(out=projT_sb[:, k, :], in_=dd["projT"][k * 128:(k + 1) * 128, :]), sG, 16)
        for t_, d_ in [(pjb_sb, dd["projb_rep"]), (eb_sb, dd["ebias_rep"]),
                       (io8_sb, dd["iota8"]), (io8p_sb, dd["iota8p"])]:
            mark(C.g.dma_start(out=t_[:], in_=d_), sG, 16)
        rconsts_ld = sG.n

        xm_all_written = sS.n
        wait("gpsimd", sS, xm_all_written)
        mark(C.g.collective_compute("AllGather", ALU.bypass, replica_groups=RG,
                                    ins=[cin_xm.opt()], outs=[cout_xm.opt()]), cC2a, 1)
        emit_fence(1, cC2a)

        if XMT:
            xmfm_done = ("v", pb16_ev[TC // 128 - 1])
            evac[("pb16",)] = pb16_ev[TC // 128 - 1]
        else:
            wait("sync", sPE, y_pe_last)
            wait("sync", sS, xm_all_written)
            for g in range(G):
                for cq in range(4):
                    mark(C.s.dma_start_transpose(
                        xm_fm[:, g, cq * 512:(cq + 1) * 512],
                        cin_xm[cq * 512:(cq + 1) * 512, g * 128:(g + 1) * 128]), sS, 16)
            xmfm_done = ("s", sS.n)

        def wait_xmfm(eng):
            wait(eng, sV if xmfm_done[0] == "v" else sS, xmfm_done[1])

        wait_xmfm("tensor")
        wait("tensor", sG, rconsts_ld)
        wait("tensor", sA, evac.get(1, 0))
        mm = None
        for chunk in range(TC // 128):
            for k in range(G):
                i = C.t.matmul(ps[1][:, chunk * 8:(chunk + 1) * 8],
                               xm_fm[:, k, chunk * 128:(chunk + 1) * 128],
                               projT_sb[:, k, :], start=(k == 0), stop=(k == G - 1))
                mm = i if i is not None else mm
        pe_n = mark(mm, sPE, 1)
        wait("vector", sPE, pe_n)
        evac[("z",)] = swv(C.v.tensor_copy(out=zall[:].rearrange("p a b -> p (a b)"),
                                           in_=ps[1][:, 0:128]))

        def b8(t2d):
            return t2d.rearrange("p (a b) -> p a b", a=1).to_broadcast([128, 16, 8])

        def bc(t2d):
            return t2d.rearrange("p (a b) -> p a b", b=1).to_broadcast([128, 16, 8])

        def max8(srcT, dstT, op):
            swv(C.v.tensor_tensor(zt[:], srcT[:, :, 0:4], srcT[:, :, 4:8], op))
            swv(C.v.tensor_tensor(zt[:, :, 0:2], zt[:, :, 0:2], zt[:, :, 2:4], op))
            swv(C.v.tensor_tensor(dstT.rearrange("p (a b) -> p a b", b=1),
                                  zt[:, :, 0:1], zt[:, :, 1:2], op))

        swv(C.v.tensor_tensor(zall[:], zall[:], b8(pjb_sb), ALU.add))
        zn = sV.n
        wait("scalar", sV, zn)
        a_n = mark(C.a.activation(out=sgall[:], in_=zall[:], func=AF.Sigmoid), sA, 1)
        wait("vector", sA, a_n)
        swv(C.v.tensor_tensor(sgall[:], sgall[:], b8(eb_sb), ALU.add))
        swv(C.v.tensor_scalar_max(sgall[:], sgall[:], 1e-12))
        max8(sgall, m1a, ALU.max)
        swv(C.v.tensor_tensor(sgal2[:], sgall[:], bc(m1a), ALU.is_equal))
        swv(C.v.scalar_tensor_tensor(sgal2[:], sgal2[:], -1000.0, b8(io8p_sb), ALU.mult, ALU.add))
        max8(sgal2, e1a.rearrange("p n -> p n"), ALU.min)
        swv(C.v.tensor_tensor(sgal2[:], b8(io8_sb), bc(e1a), ALU.is_equal))
        swv(C.v.scalar_tensor_tensor(sgal2[:], sgal2[:], -4.0, sgall[:], ALU.mult, ALU.add))
        max8(sgal2, m2a, ALU.max)
        swv(C.v.tensor_tensor(sgal2[:], sgal2[:], bc(m2a), ALU.is_equal))
        swv(C.v.scalar_tensor_tensor(sgal2[:], sgal2[:], -1000.0, b8(io8p_sb), ALU.mult, ALU.add))
        max8(sgal2, e2a.rearrange("p n -> p n"), ALU.min)
        swv(C.v.tensor_add(w1a[:], m1a[:], m2a[:]))
        swv(C.v.reciprocal(w2a[:], w1a[:]))
        swv(C.v.tensor_mul(w1a[:], m1a[:], w2a[:]))
        swv(C.v.tensor_mul(w2a[:], m2a[:], w2a[:]))
        route_done = sV.n

        pair_dma = 0
        for ri, arrq in enumerate((e1a, e2a, w1a, w2a)):
            wait("tensor", sV, route_done)
            wait("tensor", sV, evac.get(("pair",), 0))
            wait("tensor", sV, evac.get(("z",), 0))
            mm = C.t.transpose(ps[1][0:16, 0:128], arrq[:], idn_sb[:])
            pe_n = mark(mm, sPE, 1)
            wait("vector", sPE, pe_n)
            wait("vector", sS, pair_dma)
            evac[("pair",)] = swv(C.v.tensor_copy(out=pr16[0:16, :], in_=ps[1][0:16, 0:128]))
            wait("sync", sV, sV.n)
            mark(C.s.dma_start(out=cin_pair[ri:ri + 1, :].rearrange("a (c p) -> (a c) p", c=16),
                               in_=pr16[0:16, :]), sS, 16)
            pair_dma = sS.n
        pair_written = sS.n
        wait("gpsimd", sS, pair_written)
        mark(C.g.collective_compute("AllGather", ALU.bypass, replica_groups=RG,
                                    ins=[cin_pair.opt()], outs=[cout_pair.opt()]), cC2b, 1)
        emit_fence(2, cC2b)

        if debug_stage == "routing":
            wait("gpsimd", cC2b, 2)
            mark(C.g.dma_start(out=out[0:128, 0:16], in_=w1a[:]), sG, 16)
            mark(C.g.dma_start(out=out[0:128, 16:32], in_=w2a[:]), sG, 16)
            mark(C.g.dma_start(out=out[0:128, 32:48], in_=e1a[:]), sG, 16)
            mark(C.g.dma_start(out=out[0:128, 48:64], in_=e2a[:]), sG, 16)
            for r in range(32):
                mark(C.g.dma_start(out=out[256 + r * 2:256 + r * 2 + 1, 0:1024], in_=cout_pair[r:r + 1, 0:1024]), sG, 16)
                mark(C.g.dma_start(out=out[256 + r * 2 + 1:256 + r * 2 + 2, 0:1024], in_=cout_pair[r:r + 1, 1024:2048]), sG, 16)
            wait("gpsimd", sS, sS.n)
            wait("gpsimd", sG, sG.n)
            return

        wait("gpsimd", sPE, y_pe_last)
        for k in range(G):
            mark(C.g.dma_start(out=sh1T_sb[:, k, :], in_=dd["sh1T"][k * 128:(k + 1) * 128, :]), sG, 16)
        sh1_ld = sG.n
        mark(C.g.dma_start(out=sh1b_sb[:], in_=dd["sh1b2d"]), sG, 16)
        mark(C.g.dma_start(out=fc1b_sb[:], in_=dd["fc1b2d"]), sG, 16)
        smallb_ld = sG.n

        phase_barrier()
        _scope("sh1")
        wait("tensor", sV, evac.get(("z",), 0))
        wait("tensor", sV, evac.get(("pair",), 0))
        hsh_ev = {}
        for itl in range(NTILE):
            for m in range(SHD // 128):
                bank = m % 3
                wait("tensor", sG, sh1_ld)
                wait_xmfm("tensor")
                wait("tensor", sA, evac.get(bank, 0))
                mm = None
                for k in range(G):
                    i = C.t.matmul(ps[bank][:], sh1T_sb[:, k, m * 128:(m + 1) * 128],
                                   xm_fm[:, k, itl * ST:(itl + 1) * ST],
                                   start=(k == 0), stop=(k == G - 1))
                    mm = i if i is not None else mm
                pe_n = mark(mm, sPE, 1)
                wait("scalar", sPE, pe_n)
                wait("scalar", sG, smallb_ld)
                hb = m % 2
                if ("hshb", hb) in hsh_ev:
                    wait("scalar", sS, hsh_ev[("hshb", hb)])
                evac[bank] = mark(C.a.activation(out=hshb[:, hb], in_=ps[bank][:],
                                                 func=AF.Gelu, bias=sh1b_sb[:, m:m + 1]), sA, 1)
                wait("sync", sA, evac[bank])
                mark(C.s.dma_start(out=hsh_sp[m * 128:(m + 1) * 128, itl * ST:(itl + 1) * ST],
                                   in_=hshb[:, hb]), sS, 16)
                hsh_ev[("hshb", hb)] = sS.n
        sh1_pe_last = sPE.n
        hsh_written = sS.n

        _scope("selection")
        wait("gpsimd", cC2b, 2)
        for qi, dst in ((0, eA), (1, eB), (2, wAt), (3, wBt)):
            for r in range(NCORE):
                mark(C.g.dma_start(
                    out=dst[r * 16:(r + 1) * 16, :],
                    in_=cout_pair[4 * r + qi:4 * r + qi + 1, :]
                        .rearrange("a (i j) -> (a i) j", i=16)), sG, 16)
        ptiles_ld = sG.n
        wait("gpsimd", sV, route_done)
        mark(C.g.dma_start(out=io_ref[:], in_=dd["iota_ref"]), sG, 16)
        mark(C.g.dma_start(out=io_j[:], in_=dd["iota_j"]), sG, 16)
        mark(C.g.dma_start(out=io_slot[:], in_=dd["iota_slot"]), sG, 16)
        mark(C.g.dma_start(out=io16x[:], in_=dd["iota16x256"]), sG, 16)
        sel_ld = sG.n

        wait("vector", sG, ptiles_ld)
        wait("vector", sG, sel_ld)
        for half, (eT, wT) in enumerate(((eA, wAt), (eB, wBt))):
            sl = slice(half * 128, (half + 1) * 128)
            swv(C.v.tensor_scalar(selb[:, sl], eT[:], cstc("mye"), None, ALU.is_equal))
            swv(C.v.tensor_mul(wm[:, sl], wT[:], selb[:, sl]))
            swv(C.v.tensor_add(wm[:, sl], wm[:, sl], selb[:, sl]))
            swv(C.v.tensor_scalar(wm[:, sl], wm[:, sl], -1.0, None, ALU.add))

        swv(C.v.memset(dshs[:], 1.0))
        swv(C.v.memset(cnt4[:], 0.0))

        def probes(src, op, ncols):
            lastp = None
            for q in range(ncols):
                i = C.v.tensor_scalar(selb[:], src, sel4m[:, q:q + 1], None, op,
                                      ALU.add, accum_out=cnt4[:, q:q + 1])
                lastp = i if i is not None else lastp
            n = mark(lastp, sV, 1)
            wait("vector", sV, n)
            wait("tensor", sV, n)
            wait("tensor", sV, evac.get(("sel",), 0))
            wait("tensor", sA, evac.get(2, 0))
            mm = C.t.matmul(ps[2][:, 0:4], dshs[:, 0:128], cnt4[:], start=True, stop=True)
            pe_n = mark(mm, sPE, 1)
            wait("vector", sPE, pe_n)
            evac[("sel",)] = swv(C.v.tensor_copy(out=tot4[:], in_=ps[2][:, 0:4]))

        def upd_both(pred_ge_keeps_lo):
            swv(C.v.tensor_scalar(tot4[:, 0:3], ge4[:, 0:3], -1.0, 1.0, ALU.mult, ALU.add))
            glo = ge4 if pred_ge_keeps_lo else tot4
            ghi = tot4 if pred_ge_keeps_lo else ge4
            swv(C.v.tensor_scalar(cnt4[:, 0:3], sel4m[:, 0:3], sv("lo"), None, ALU.subtract))
            swv(C.v.tensor_mul(cnt4[:, 0:3], cnt4[:, 0:3], glo[:, 0:3]))
            swv(C.v.tensor_tensor(cnt4[:, 0:1], cnt4[:, 0:1], cnt4[:, 1:2], ALU.max))
            swv(C.v.tensor_tensor(cnt4[:, 0:1], cnt4[:, 0:1], cnt4[:, 2:3], ALU.max))
            swv(C.v.tensor_add(sv("lo"), sv("lo"), cnt4[:, 0:1]))
            swv(C.v.tensor_scalar(cnt4[:, 0:3], sel4m[:, 0:3], sv("hi"), -1.0, ALU.subtract, ALU.mult))
            swv(C.v.tensor_mul(cnt4[:, 0:3], cnt4[:, 0:3], ghi[:, 0:3]))
            swv(C.v.tensor_tensor(cnt4[:, 0:1], cnt4[:, 0:1], cnt4[:, 1:2], ALU.max))
            swv(C.v.tensor_tensor(cnt4[:, 0:1], cnt4[:, 0:1], cnt4[:, 2:3], ALU.max))
            swv(C.v.tensor_sub(sv("hi"), sv("hi"), cnt4[:, 0:1]))

        swv(C.v.memset(sel4m[:, 0:1], -0.5))
        probes(wm[:], ALU.is_gt, 1)
        swv(C.v.tensor_copy(out=sv("asg"), in_=tot4[:, 0:1]))
        swv(C.v.memset(sv("lo"), 1e-14))
        swv(C.v.memset(sv("hi"), 2.0))
        for rnd in range(16):
            if rnd < 6:
                swv(C.v.tensor_mul(sv("t1"), sv("lo"), sv("hi")))
                n = sV.n
                wait("scalar", sV, n)
                a_n = mark(C.a.activation(out=sel4m[:, 1:2], in_=sv("t1"), func=AF.Sqrt), sA, 1)
                wait("vector", sA, a_n)
                swv(C.v.tensor_mul(sv("t1"), sv("lo"), sel4m[:, 1:2]))
                swv(C.v.tensor_mul(sv("t2"), sel4m[:, 1:2], sv("hi")))
                n = sV.n
                wait("scalar", sV, n)
                C.a.activation(out=sel4m[:, 0:1], in_=sv("t1"), func=AF.Sqrt)
                a_n = mark(C.a.activation(out=sel4m[:, 2:3], in_=sv("t2"), func=AF.Sqrt), sA, 1)
                wait("vector", sA, a_n)
            else:
                swv(C.v.tensor_sub(sv("t1"), sv("hi"), sv("lo")))
                swv(C.v.tensor_scalar_mul(sv("t1"), sv("t1"), 0.25))
                swv(C.v.tensor_add(sel4m[:, 0:1], sv("lo"), sv("t1")))
                swv(C.v.tensor_add(sel4m[:, 1:2], sel4m[:, 0:1], sv("t1")))
                swv(C.v.tensor_add(sel4m[:, 2:3], sel4m[:, 1:2], sv("t1")))
            probes(wm[:], ALU.is_gt, 3)
            swv(C.v.tensor_scalar(ge4[:, 0:3], tot4[:, 0:3], float(CAP), None, ALU.is_ge))
            upd_both(True)
        swv(C.v.tensor_scalar(sv("ov"), sv("asg"), float(CAP), None, ALU.is_gt))
        swv(C.v.tensor_mul(sv("t1"), sv("ov"), sv("hi")))
        swv(C.v.tensor_scalar(sv("t2"), sv("ov"), 1.0, -1.0, ALU.mult, ALU.add))
        swv(C.v.tensor_add(sv("theta"), sv("t1"), sv("t2")))
        swv(C.v.tensor_copy(out=sel4m[:, 0:1], in_=sv("theta")))
        probes(wm[:], ALU.is_gt, 1)
        swv(C.v.tensor_copy(out=sv("cntgt"), in_=tot4[:, 0:1]))
        swv(C.v.tensor_scalar(sv("neq"), sv("cntgt"), -1.0, float(CAP), ALU.mult, ALU.add))
        swv(C.v.tensor_scalar(keptb[:], wm[:], sv("theta"), None, ALU.is_gt))
        swv(C.v.tensor_scalar(eqm[:], wm[:], sv("theta"), None, ALU.is_equal))
        swv(C.v.tensor_scalar(selb[:], eqm[:], -1e9, 1e9, ALU.mult, ALU.add))
        swv(C.v.tensor_mul(mskio[:], eqm[:], io_ref[:]))
        swv(C.v.tensor_add(mskio[:], mskio[:], selb[:]))
        swv(C.v.memset(sv("lo"), -1.0))
        swv(C.v.memset(sv("hi"), 32768.0))
        for rnd in range(9):
            swv(C.v.tensor_sub(sv("t1"), sv("hi"), sv("lo")))
            swv(C.v.tensor_scalar_mul(sv("t1"), sv("t1"), 0.25))
            swv(C.v.tensor_add(sel4m[:, 0:1], sv("lo"), sv("t1")))
            swv(C.v.tensor_add(sel4m[:, 1:2], sel4m[:, 0:1], sv("t1")))
            swv(C.v.tensor_add(sel4m[:, 2:3], sel4m[:, 1:2], sv("t1")))
            probes(mskio[:], ALU.is_le, 3)
            swv(C.v.tensor_tensor(ge4[:, 0:3], tot4[:, 0:3],
                                  sv("neq").to_broadcast([128, 3]), ALU.is_ge))
            upd_both(False)
        swv(C.v.tensor_scalar(selb[:], io_ref[:], sv("hi"), None, ALU.is_le))
        swv(C.v.tensor_mul(selb[:], selb[:], eqm[:]))
        swv(C.v.tensor_scalar(eqm[:], wm[:], -0.5, None, ALU.is_gt))
        swv(C.v.tensor_mul(selb[:], selb[:], eqm[:]))
        swv(C.v.tensor_add(keptb[:], keptb[:], selb[:]))
        swv(C.v.memset(dshs[:], 1.0))
        swv(C.v.tensor_tensor_scan(pos[:], dshs[:], keptb[:], 0.0, ALU.mult, ALU.add))
        swv(C.v.tensor_copy(out=sv("m_p"), in_=pos[:, 255:256]))
        swv(C.v.tensor_sub(rexc[:], pos[:], keptb[:]))
        wait("gpsimd", sV, sV.n)
        mark(C.g.dma_start(out=cnt_dram.rearrange("(p a) -> p a", a=1), in_=sv("m_p")), sG, 16)
        wait("gpsimd", sG, sG.n)
        mark(C.g.dma_start(out=mrow, in_=cnt_dram.rearrange("(a p) -> a p", a=1)), sG, 16)
        g_n = sG.n
        wait("vector", sG, g_n)
        swv(C.v.tensor_tensor_scan(brow, dshs[0:1, 0:128], mrow, 0.0, ALU.mult, ALU.add))
        swv(C.v.memset(sv("t3"), 0.0))
        v_n = sV.n
        wait("gpsimd", sV, v_n)
        mark(C.g.dma_start(out=base_dram[1:129].rearrange("(a p) -> a p", a=1), in_=brow), sG, 16)
        mark(C.g.dma_start(out=base_dram[0:1].rearrange("(a p) -> a p", a=1), in_=sv("t3")[0:1, :]), sG, 16)
        wait("gpsimd", sG, sG.n)
        mark(C.g.dma_start(out=sv("bcol"), in_=base_dram[0:128].rearrange("(p a) -> p a", a=1)), sG, 16)
        mark(C.g.dma_start(out=base_bc, in_=base_dram[0:128].rearrange("(a p) -> a p", a=1).to_broadcast([128, 128])), sG, 16)
        base_ld = sG.n
        wait("vector", sG, base_ld)
        swv(C.v.tensor_scalar(slotm[:], rexc[:], sv("bcol"), None, ALU.add))
        swv(C.v.tensor_scalar(selb[:], slotm[:], 1024.0, None, ALU.is_ge))
        swv(C.v.tensor_scalar(eqm[:], slotm[:], 2048.0, None, ALU.is_ge))
        swv(C.v.tensor_add(selb[:], selb[:], eqm[:]))
        swv(C.v.tensor_scalar(eqm[:], slotm[:], 3072.0, None, ALU.is_ge))
        swv(C.v.tensor_add(selb[:], selb[:], eqm[:]))
        swv(C.v.tensor_scalar(eqm[:], slotm[:], 4096.0, None, ALU.is_ge))
        swv(C.v.tensor_add(selb[:], selb[:], eqm[:]))
        swv(C.v.scalar_tensor_tensor(rowm[:], selb[:], 7168.0, slotm[:], ALU.mult, ALU.add))
        swv(C.v.tensor_scalar(sv("t1"), cstc("mye"), 1024.0, None, ALU.mult))
        swv(C.v.tensor_scalar(rowm[:], rowm[:], sv("t1"), -SENT_ROW, ALU.add, ALU.add))
        swv(C.v.tensor_mul(rowm[:], rowm[:], keptb[:]))
        swv(C.v.tensor_scalar(rowm[:], rowm[:], SENT_ROW, None, ALU.add))
        v_n = sV.n
        wait("gpsimd", sV, v_n)
        mark(C.g.dma_start(out=cin_slot[0:1, :].rearrange("a (p j) -> (a p) j", p=128),
                           in_=rowm[:]), sG, 16)
        wait("gpsimd", sG, sG.n)
        mark(C.g.collective_compute("AllGather", ALU.bypass, replica_groups=RG,
                                    ins=[cin_slot.opt()], outs=[cout_slot.opt()]), sems["cSlot"], 1)
        emit_fence(3, sems["cSlot"])
        swv(C.v.tensor_scalar_mul(tokv[:, 0:128], io_ref[:, 0:128], 0.5))
        swv(C.v.tensor_scalar(tokv[:, 128:256], io_ref[:, 128:256], 0.5, -0.5, ALU.mult, ALU.add))
        swv(C.v.tensor_scalar(arr_[:], tokv[:], -16384.0, None, ALU.add))
        swv(C.v.tensor_mul(arr_[:], arr_[:], keptb[:]))
        swv(C.v.tensor_scalar(arr_[:], arr_[:], 16384.0, None, ALU.add))
        swv(C.v.tensor_sub(delt[:], io_j[:], rexc[:]))
        for bbit in range(8):
            sft = 1 << bbit
            w_ = 256 - sft
            swv(C.v.tensor_scalar(mskio[:], delt[:], 0.5 / sft, None, ALU.mult))
            swv(C.v.tensor_copy(out=bitu[:], in_=mskio[:]))
            swv(C.v.tensor_copy(out=dshs[:], in_=bitu[:]))
            swv(C.v.tensor_sub(dshs[:], mskio[:], dshs[:]))
            swv(C.v.tensor_mul(dshs[:], dshs[:], dshs[:]))
            swv(C.v.tensor_scalar(dshs[:], dshs[:], 0.0625, None, ALU.is_ge))
            if debug_stage == "fulldbg" and bbit == 0:
                wait("gpsimd", sV, sV.n)
                mark(C.g.dma_start(out=dbg_sel2[:, 1300:1556], in_=mskio[:]), sG, 16)
                mark(C.g.dma_start(out=dbg_sel2[:, 1556:1812], in_=dshs[:]), sG, 16)
                mark(C.g.dma_start(out=dbg_sel2[:, 1812:2068], in_=delt[:]), sG, 16)
                wait("vector", sG, sG.n)
            swv(C.v.tensor_copy(out=bitu[:], in_=dshs[:]))
            swv(C.v.scalar_tensor_tensor(mskio[:], dshs[:], -float(sft), delt[:],
                                         ALU.mult, ALU.add))
            swv(C.v.select(arr_[:, 0:w_], bitu[:, sft:256], arr_[:, sft:256], arr_[:, 0:w_]))
            swv(C.v.select(delt[:, 0:w_], bitu[:, sft:256], mskio[:, sft:256], mskio[:, 0:w_]))
            if sft > 0:
                swv(C.v.tensor_copy(out=delt[:, w_:256], in_=mskio[:, w_:256]))
        swv(C.v.tensor_scalar(dshs[:], io_j[:], sv("m_p"), None, ALU.is_ge))
        swv(C.v.tensor_scalar(selb[:], arr_[:], -1.0, 16384.0, ALU.mult, ALU.add))
        swv(C.v.tensor_mul(selb[:], selb[:], dshs[:]))
        swv(C.v.tensor_add(arr_[:], arr_[:], selb[:]))
        v_n = sV.n
        wait("gpsimd", sV, v_n)
        mark(C.g.dma_start(out=padded_dram.rearrange("(p j) -> p j", p=128), in_=arr_[:]), sG, 16)
        padded_w = sG.n
        if debug_stage == "fulldbg":
            wait("gpsimd", sG, padded_w)
            mark(C.g.dma_start(out=dbg_sel2[:, 0:16], in_=xf2v[:, 456:472]), sG, 16)
            mark(C.g.dma_start(out=dbg_sel2[:, 16:272], in_=keptb[:]), sG, 16)
            mark(C.g.dma_start(out=dbg_sel2[:, 272:528], in_=arr_[:]), sG, 16)
            mark(C.g.dma_start(out=dbg_sel2[:, 528:784], in_=io_j[:]), sG, 16)
            mark(C.g.dma_start(out=dbg_sel2[:, 784:1040], in_=delt[:]), sG, 16)
            mark(C.g.dma_start(out=dbg_sel2[:, 1040:1296], in_=rexc[:]), sG, 16)
        wait("vector", sG, padded_w)
        big20 = bview(0, 2560).rearrange("p (c q) -> p c q", c=20)
        for hh in range(2):
            cols = slice(hh * 20, (hh + 1) * 20)
            base_b20 = base_bc.rearrange("p (a q) -> p a q", a=1).to_broadcast([128, 20, 128])
            ios_b = io_slot[:, cols].rearrange("p (c a) -> p c a", a=1).to_broadcast([128, 20, 128])
            swv(C.v.tensor_tensor(big20[:], base_b20, ios_b, ALU.is_le))
            w_ = 128
            while w_ > 1:
                w_ //= 2
                swv(C.v.tensor_tensor(big20[:, :, 0:w_], big20[:, :, 0:w_],
                                      big20[:, :, w_:2 * w_], ALU.add))
            swv(C.v.tensor_scalar(srcx[:, cols], big20[:, :, 0:1].rearrange("p c a -> p (c a)"),
                                  256.0, -256.0, ALU.mult, ALU.add))
            swv(C.v.tensor_tensor(big20[:], base_b20, ios_b, ALU.is_le))
            swv(C.v.tensor_tensor(big20[:], big20[:], base_b20, ALU.mult))
            w_ = 128
            while w_ > 1:
                w_ //= 2
                swv(C.v.tensor_tensor(big20[:, :, 0:w_], big20[:, :, 0:w_],
                                      big20[:, :, w_:2 * w_], ALU.max))
            swv(C.v.tensor_tensor(srcx[:, cols], srcx[:, cols], io_slot[:, cols], ALU.add))
            swv(C.v.tensor_tensor(srcx[:, cols], srcx[:, cols],
                                  big20[:, :, 0:1].rearrange("p c a -> p (c a)"), ALU.subtract))
        swv(C.v.tensor_scalar_min(srcx[:], srcx[:], 32767.0))
        swv(C.v.tensor_scalar_max(srcx[:], srcx[:], 0.0))
        swv(C.v.tensor_copy(out=srcu[:], in_=srcx[:]))
        v_n = sV.n
        wait("gpsimd", sV, v_n)
        wait("gpsimd", sG, padded_w)
        for c40 in range(40):
            mark(C.g.indirect_dma_start(out=tokf[:, c40:c40 + 1], out_offset=None,
                                        in_=padded_dram.rearrange("(n a) -> n a", a=1),
                                        in_offset=IOA(srcu[:, c40:c40 + 1], 0)), sG, 16)
        tok_g = sG.n
        wait("vector", sG, tok_g)
        swv(C.v.tensor_scalar_min(tokf[:], tokf[:], 16383.0))
        swv(C.v.tensor_copy(out=toku[:], in_=tokf[:]))
        tok_ready = sV.n

        if debug_stage == "sel":
            wait("sync", sV, tok_ready)
            mark(C.s.dma_start(out=out[0:128, 0:16], in_=xf2v[:, 456:472]), sS, 16)
            mark(C.s.dma_start(out=out[128:256, 0:256], in_=rowm[:]), sS, 16)
            mark(C.s.dma_start(out=out[384:512, 0:40], in_=tokf[:]), sS, 16)
            mark(C.s.dma_start(out=out[512:640, 0:256], in_=keptb[:]), sS, 16)
            mark(C.s.dma_start(out=out[640:768, 0:40], in_=srcx[:]), sS, 16)
            wait("gpsimd", sS, sS.n)
            wait("gpsimd", sG, sG.n)
            return

        phase_barrier()
        _scope("expert")
        wait("gpsimd", sPE, sh1_pe_last)
        for k in range(G):
            mark(C.g.dma_start(out=fc1T_sb[:, k, :], in_=dd["fc1T"][k * 128:(k + 1) * 128, :]), sG, 16)
        fc1_ld = sG.n
        for k in range(16):
            mark(C.g.dma_start(out=fc2T_sb[:, k, :], in_=dd["fc2T"][k * 128:(k + 1) * 128, :]), sG, 16)
        fc2_ld = sG.n
        wait("gpsimd", sV, tok_ready)
        wait("gpsimd", sG, padded_w)
        mark(C.g.dma_start(out=fc2b_rp[:], in_=dd["fc2b_rep"]), sG, 16)
        mark(C.g.dma_start(out=sh2b_rp[:], in_=dd["sh2b_rep"]), sG, 16)
        rep_ld = sG.n

        wait("vector", sG, sel_ld)
        swv(C.v.tensor_scalar(sv("t1"), cstc("mye"), 4096.0, None, ALU.mult))
        swv(C.v.tensor_tensor(wABe[:, 0:16], io16x[:],
                              cstc("iota128").to_broadcast([128, 16]), ALU.add))
        swv(C.v.tensor_scalar(wABe[:, 0:16], wABe[:, 0:16], sv("t1"), None, ALU.add))
        swv(C.v.scalar_tensor_tensor(kAB[:, 0:16], e1a[:], 32768.0, wABe[:, 0:16],
                                     ALU.mult, ALU.add))
        swv(C.v.scalar_tensor_tensor(kAB[:, 16:32], e2a[:], 32768.0, wABe[:, 0:16],
                                     ALU.mult, ALU.add))
        swv(C.v.tensor_scalar(kAB[:, 16:32], kAB[:, 16:32], 128.0, None, ALU.add))
        swv(C.v.tensor_copy(out=ixu_all[:], in_=kAB[:]))
        p4pre_v = sV.n
        wait("gpsimd", sV, p4pre_v)
        wait("gpsimd", sems["cSlot"], 2)
        csf = cout_slot_flat.rearrange("(n a) -> n a", a=1)
        for cq in range(16):
            mark(C.g.indirect_dma_start(out=sva_all[:, cq:cq + 1], out_offset=None,
                                        in_=csf, in_offset=IOA(ixu_all[:, cq:cq + 1], 0)), sG, 16)
            mark(C.g.indirect_dma_start(out=svb_all[:, cq:cq + 1], out_offset=None,
                                        in_=csf, in_offset=IOA(ixu_all[:, 16 + cq:17 + cq], 0)), sG, 16)
        slotg = sG.n
        wait("vector", sG, slotg)
        swv(C.v.tensor_scalar(kAB[:, 0:16], sva_all[:], SENT_ROW, None, ALU.is_lt))
        swv(C.v.tensor_scalar(kAB[:, 16:32], svb_all[:], SENT_ROW, None, ALU.is_lt))
        swv(C.v.tensor_scalar_min(sva_all[:], sva_all[:], SENT_ROW - 1.0))
        swv(C.v.tensor_scalar_min(svb_all[:], svb_all[:], SENT_ROW - 1.0))
        swv(C.v.tensor_copy(out=rAu_all[:], in_=sva_all[:]))
        swv(C.v.tensor_copy(out=rBu_all[:], in_=svb_all[:]))
        swv(C.v.tensor_mul(wABe[:, 0:16], w1a[:], kAB[:, 0:16]))
        swv(C.v.tensor_mul(wABe[:, 16:32], w2a[:], kAB[:, 16:32]))
        p4pre_done = sV.n

        oe_w = {}
        fc2_pe = {-1: 0}
        wait("tensor", sV, evac.get(("sel",), 0))
        for ce in range(CAP // 512):
            wait("gpsimd", cC2a, 2)
            wait("gpsimd", sV, tok_ready)
            for q4 in range(4):
                cc = ce * 4 + q4
                xg = xeg[:, cc % 2, :]
                if ("xeg", cc % 2) in evac:
                    wait("gpsimd", sPE, evac[("xeg", cc % 2)])
                mark(C.g.indirect_dma_start(out=xg, out_offset=None,
                                            in_=cout_xm,
                                            in_offset=IOA(toku[:, cc:cc + 1], 0)), sG, 16)
                g_n = sG.n
                wait("tensor", sG, g_n)
                wait("tensor", sG, rconsts_ld)
                wait("tensor", sV, evac.get(("pb16",), 0))
                mm = None
                for g in range(G):
                    i = C.t.transpose(pb16[:, g * 128:(g + 1) * 128],
                                      xg.rearrange("p (g n) -> p g n", g=G)[:, g, :],
                                      idnb_sb[:])
                    mm = i if i is not None else mm
                pe_n = mark(mm, sPE, 1)
                evac[("xeg", cc % 2)] = pe_n
                wait("vector", sPE, pe_n)
                evac[("pb16",)] = swv(C.v.tensor_copy(
                    out=xe_fm[:, :, q4 * 128:(q4 + 1) * 128],
                    in_=pb16.rearrange("p (g n) -> p g n", g=G)))
            xe_ready = sV.n
            wait("tensor", sG, fc1_ld)
            wait("tensor", sV, xe_ready)
            wait("scalar", sPE, fc2_pe[ce - 1])
            for m in range(16):
                bank = m % 3
                wait("tensor", sA, evac.get(bank, 0))
                mm = None
                for k in range(G):
                    i = C.t.matmul(ps[bank][:], fc1T_sb[:, k, m * 128:(m + 1) * 128],
                                   xe_fm[:, k, :], start=(k == 0), stop=(k == G - 1))
                    mm = i if i is not None else mm
                pe_n = mark(mm, sPE, 1)
                wait("scalar", sPE, pe_n)
                wait("scalar", sG, smallb_ld)
                evac[bank] = mark(C.a.activation(out=he[:, m, :], in_=ps[bank][:],
                                                 func=AF.Gelu, bias=fc1b_sb[:, m:m + 1]), sA, 1)
            he_done = sA.n
            wait("tensor", sG, fc2_ld)
            wait("tensor", sA, he_done)
            for m4 in range(4):
                pwb = pw[m4 % 2]
                if ("pw2", m4 % 2) in evac:
                    wait("tensor", sV, evac[("pw2", m4 % 2)])
                mm = None
                for k in range(16):
                    for nn_ in range(2):
                        i = C.t.matmul(pwb[:, nn_ * 512:(nn_ + 1) * 512],
                                       he[:, k, m4 * 128:(m4 + 1) * 128],
                                       fc2T_sb[:, k, nn_ * 512:(nn_ + 1) * 512],
                                       start=(k == 0), stop=(k == 15))
                        mm = i if i is not None else mm
                pe_n = mark(mm, sPE, 1)
                wait("vector", sPE, pe_n)
                wait("vector", sG, rep_ld)
                if ("oeb", m4 % 2) in oe_w:
                    wait("vector", sS, oe_w[("oeb", m4 % 2)])
                ob = oeb[m4 % 2]
                ev = swv(C.v.tensor_add(ob[:], pwb[:], fc2b_rp[:]))
                evac[("pw2", m4 % 2)] = ev
                wait("sync", sV, ev)
                r0 = (ce * 4 + m4) * 128 - (ce // 2) * 1024
                mark(C.s.dma_start(out=cin_oes[ce // 2][r0:r0 + 128, :],
                                   in_=ob[:]), sS, 16)
                oe_w[("oeb", m4 % 2)] = sS.n
            fc2_pe[ce] = sPE.n
        oe_written = sS.n
        fc_pe_last = sPE.n
        wait("gpsimd", sS, oe_written)
        for s5 in range(5):
            mark(C.g.collective_compute(
                "AllGather", ALU.bypass, replica_groups=RG,
                ins=[cin_oes[s5].opt()],
                outs=[cout_oe[s5 * 8192:(s5 + 1) * 8192, :].opt()]), sems["cOe"], 1)
        emit_fence(4, sems["cOe"])

        wait("gpsimd", sPE, fc_pe_last)
        for k in range(32):
            mark(C.g.dma_start(out=sh2T_sb[:, k, :], in_=dd["sh2T"][k * 128:(k + 1) * 128, :]), sG, 16)
        sh2_ld = sG.n

        phase_barrier()
        _scope("sh2p4")
        comb_done = {-1: 0}
        arow2 = AR_X[:, 8192:16384].bitcast(F32)
        Arows = [h_t.rearrange("p a n -> p (a n)"), arow2[:, 0:D]]
        Brows = [p_t.rearrange("p a n -> p (a n)"), arow2[:, D:2 * D]]
        gAB = {}

        def issue_rowgather(c):
            bi = c % 2
            mark(C.g.indirect_dma_start(out=Arows[bi], out_offset=None, in_=cout_oe,
                                        in_offset=IOA(rAu_all[:, c:c + 1], 0)), sG, 16)
            mark(C.g.indirect_dma_start(out=Brows[bi], out_offset=None, in_=cout_oe,
                                        in_offset=IOA(rBu_all[:, c:c + 1], 0)), sG, 16)
            gAB[c] = sG.n

        sht_w = {}
        for chunk in range(TC // 128):
            hb2 = hsh2[chunk % 2]
            wait("sync", sPE, evac.get(("hsh2", chunk % 2), 0))
            if chunk == 0:
                wait("sync", sS, hsh_written)
                wait("sync", sPE, fc_pe_last)
            mark(C.s.dma_start(out=hb2[:],
                               in_=hsh_sp.rearrange("(k p) t -> p k t", p=128)[:, :, chunk * 128:(chunk + 1) * 128]), sS, 16)
            h2_ld = sS.n
            wait("tensor", sS, h2_ld)
            wait("tensor", sG, sh2_ld)
            pwb = pw[chunk % 2]
            wait("tensor", sV, evac.get(("pw2", chunk % 2), 0))
            if ("pw3", chunk % 2) in evac:
                wait("tensor", sV, evac[("pw3", chunk % 2)])
            mm = None
            for k in range(32):
                for nn_ in range(2):
                    i = C.t.matmul(pwb[:, nn_ * 512:(nn_ + 1) * 512], hb2[:, k, :],
                                   sh2T_sb[:, k, nn_ * 512:(nn_ + 1) * 512],
                                   start=(k == 0), stop=(k == 31))
                    mm = i if i is not None else mm
            pe_n = mark(mm, sPE, 1)
            evac[("hsh2", chunk % 2)] = pe_n
            wait("vector", sPE, pe_n)
            if chunk == 0:
                wait("vector", sG, rep_ld)
            shb = sht[chunk % 2]
            wait("vector", sS, sht_w.get(chunk % 2, 0))
            evac[("pw3", chunk % 2)] = swv(C.v.tensor_add(shb[:], pwb[:], sh2b_rp[:]))
            wait("sync", sV, evac[("pw3", chunk % 2)])
            mark(C.s.dma_start(out=sh2_sp[chunk * 128:(chunk + 1) * 128, :], in_=shb[:]), sS, 16)
            sht_w[chunk % 2] = sS.n

        wait("gpsimd", sems["cOe"], 6)
        wait("gpsimd", sV, p4pre_done)
        issue_rowgather(0)
        issue_rowgather(1)
        rel_n = {}

        def issue_reload(c):
            wait("sync", sV, comb_done.get(c - 2, 0))
            mark(C.s.dma_start(out=sht[c % 2][:], in_=sh2_sp[c * 128:(c + 1) * 128, :]), sS, 16)
            rel_n[c] = sS.n

        issue_reload(0)
        issue_reload(1)
        for chunk in range(TC // 128):
            shb = sht[chunk % 2]
            wait("vector", sS, rel_n[chunk])
            wait("vector", sG, gAB[chunk])
            C.v.scalar_tensor_tensor(r1_tok[:], Arows[chunk % 2], wABe[:, chunk:chunk + 1],
                                     xm_tok[:, chunk, :], ALU.mult, ALU.add)
            C.v.scalar_tensor_tensor(r1_tok[:], Brows[chunk % 2], wABe[:, 16 + chunk:17 + chunk],
                                     r1_tok[:], ALU.mult, ALU.add)
            mark(C.v.tensor_add(r1_tok[:], r1_tok[:], shb[:]), sV, 1)
            comb_done[chunk] = sV.n
            r2_n = sV.n

            if chunk + 2 < TC // 128:
                wait("gpsimd", sV, comb_done[chunk])
                issue_rowgather(chunk + 2)
                issue_reload(chunk + 2)

            wait("scalar", sV, r2_n)
            mark(C.a.activation(out=lnsq[:], in_=r1_tok[:], func=AF.Copy,
                                accum_out=stat[:, 0:1]), sA, 1)
            a_n = mark(C.a.activation(out=lnsq[:], in_=r1_tok[:], func=AF.Square,
                                      accum_out=stat[:, 1:2]), sA, 1)
            wait("vector", sA, a_n)
            swv(C.v.tensor_scalar_mul(stat[:, 2:3], stat[:, 0:1], 1.0 / D))
            swv(C.v.tensor_scalar_mul(stat[:, 3:4], stat[:, 1:2], 1.0 / D))
            swv(C.v.tensor_mul(stat[:, 4:5], stat[:, 2:3], stat[:, 2:3]))
            swv(C.v.tensor_sub(stat[:, 3:4], stat[:, 3:4], stat[:, 4:5]))
            mark(C.v.tensor_scalar(stat[:, 3:4], stat[:, 3:4], EPS, None, ALU.add), sV, 1)
            var_n = sV.n
            wait("scalar", sV, var_n)
            sd_n = mark(C.a.activation(out=stat[:, 5:6], in_=stat[:, 3:4], func=AF.Sqrt), sA, 1)
            wait("vector", sA, sd_n)
            swv(C.v.reciprocal(stat[:, 6:7], stat[:, 5:6]))
            if chunk > 0:
                wait("vector", sS, outw)
            mark(C.v.tensor_scalar(otok[:], r1_tok[:], stat[:, 2:3], stat[:, 6:7],
                                   ALU.subtract, ALU.mult), sV, 1)
            o_n = sV.n
            wait("sync", sV, o_n)
            mark(C.s.dma_start(out=out[chunk * 128:(chunk + 1) * 128, :], in_=otok[:]), sS, 16)
            outw = sS.n

        if debug_stage == "fulldbg":
            wait("sync", sV, sV.n)
            wait("sync", sPE, sPE.n)
            mark(C.s.dma_start(out=dbg_xe, in_=AR_X[:, 0:6144]), sS, 16)
            mark(C.s.dma_start(out=dbg_he, in_=AR_X[:, 6144:14336]), sS, 16)
            mark(C.s.dma_start(out=dbg_xall, in_=cout_xm), sS, 16)
            mark(C.s.dma_start(out=dbg_tok, in_=tokf[:]), sS, 16)
            mark(C.s.dma_start(out=dbg_pair, in_=cout_pair), sS, 16)
            mark(C.s.dma_start(out=dbg_slot, in_=cout_slot), sS, 16)
            for s5 in range(5):
                mark(C.s.dma_start(out=dbg_oe[s5 * 1024:(s5 + 1) * 1024, :], in_=cin_oes[s5]), sS, 16)
            mark(C.s.dma_start(out=dbg_xm, in_=cin_xm), sS, 16)
            mark(C.s.dma_start(out=dbg_hsh, in_=hsh_sp[:, 0:128]), sS, 16)

        wait("gpsimd", sS, sS.n)
        wait("gpsimd", sG, sG.n)
        if _cur_scope[0] is not None:
            pn, pid = _cur_scope[0]
            C.nc.leave_named_scope(pn, pid, False)
        return

    def walk(name):
        C.cur = name
        C.reset()
        program()

    with nc.Block() as block:
        block.sync(lambda e: walk("sync"))
        block.gpsimd(lambda e: walk("gpsimd"))
        block.vector(lambda e: walk("vector"))
        block.scalar(lambda e: walk("scalar"))
        block.tensor(lambda e: walk("tensor"))
    ctx.close()
    return nc


def kernel(**inputs):
    maps = _prep_inputs(inputs)
    nc = _build()
    res = run_bass_kernel_spmd(nc, maps, core_ids=list(range(NCORE)))
    outs = [res.results[c]["out"] for c in range(NCORE)]
    return np.stack(outs).reshape(B, T, D).astype(np.float32)
